# version 01
# baseline (speedup 1.0000x reference)
import numpy as np
import ml_dtypes
from contextlib import ExitStack

import concourse.bass as bass
import concourse.mybir as mybir
from concourse.bass_utils import run_bass_kernel_spmd

F32 = mybir.dt.float32
F16 = mybir.dt.bfloat16
AF = mybir.ActivationFunctionType
ALU = mybir.AluOpType
AX = mybir.AxisListType

B, S, D = 2, 4096, 2048
NH, DH, C = 16, 128, 512
TOK = 1024
NCORE = 8
HPC = 2
SCALE = float(128) ** -0.5
LN_EPS = 1e-5
Q_EPS = 1e-8
H0 = 5632
H1 = 8192
QSLF = (H0 * D // NCORE) // 128
FW_SC = float(H1) ** -0.5


def _owner(g):
    return (min(g, 7 - g), 0 if g < 4 else 1)


def _masks_np():
    p = np.arange(128)[:, None]
    q = np.arange(512)[None, :]
    cols = []
    for o in (0, 128, 256, 384):
        cols.append((p + o >= q).astype(ml_dtypes.bfloat16))
    for o in (0, 128, 256, 384):
        cols.append((p + o <= q).astype(ml_dtypes.bfloat16))
    return np.concatenate(cols, axis=1)


def _hadamard(n):
    H = np.array([[1.0]], dtype=np.float32)
    while H.shape[0] < n:
        H = np.block([[H, H], [H, -H]])
    return H


class Ctx:
    def __init__(self, nc, cfg):
        self.nc = nc
        self.cfg = cfg
        self.cnt = {}
        self.stack = ExitStack()
        self.sems = {}

    def sem(self, name):
        if name not in self.sems:
            self.sems[name] = self.stack.enter_context(self.nc.semaphore(name))
            self.cnt[name] = 0
        return self.sems[name]

    def dinc(self, inst, name):
        inst.then_inc(self.sems[name], 16)
        self.cnt[name] += 16
        return self.cnt[name]

    def slot_dma(self, inst, slot):
        return self.dinc(inst, "q%d" % slot)

    def slot_wait(self, eng, slot, val):
        eng.wait_ge(self.sems["q%d" % slot], val)

    def cinc(self, inst, name, val=1):
        inst.then_inc(self.sems[name], val)
        self.cnt[name] += val
        return self.cnt[name]


def _alloc(ctx):
    nc = ctx.nc
    a = {}
    a["x_fm"] = nc.alloc_sbuf_tensor("x_fm", [128, 16 * TOK], F32)
    a["xn"] = nc.alloc_sbuf_tensor("xn", [128, 16 * TOK], F16)
    a["arena"] = nc.alloc_sbuf_tensor("arena", [128, 32768], F16)
    a["wpool"] = nc.alloc_sbuf_tensor("wpool", [128, 8192], F16)
    a["xpool"] = nc.alloc_sbuf_tensor("xpool", [128, 2048], F16)
    a["praw"] = nc.alloc_sbuf_tensor("praw", [128, 2048], F16)
    a["pmask"] = nc.alloc_sbuf_tensor("pmask", [128, 2048], F16)
    a["sc_a"] = nc.alloc_sbuf_tensor("sc_a", [128, 512], F32)
    a["sc_b"] = nc.alloc_sbuf_tensor("sc_b", [128, 512], F32)
    a["scr"] = nc.alloc_sbuf_tensor("scr", [128, 1024], F16)
    a["bc_sb"] = nc.alloc_sbuf_tensor("bc_sb", [128, 512], F16)
    a["hidout"] = nc.alloc_sbuf_tensor("hidout", [128, 2048], F16)
    a["vtmp"] = nc.alloc_sbuf_tensor("vtmp", [128, 1024], F16)
    a["consts"] = nc.alloc_sbuf_tensor("consts", [128, 768], F16)
    a["identf"] = nc.alloc_sbuf_tensor("identf", [128, 128], F32)
    a["fconst"] = nc.alloc_sbuf_tensor("fconst", [128, 136], F32)
    print(f"[build] sbuf remaining: {nc.sbuf_bytes_remaining} B/part")
    a["ps_a"] = nc.alloc_psum_tensor("ps_a", [128, 2048], F32)
    a["ps_b"] = nc.alloc_psum_tensor("ps_b", [128, 1024], F32)
    a["ps_c"] = nc.alloc_psum_tensor("ps_c", [128, 1024], F32)
    return a


def p_consts(ctx):
    nc, a = ctx.nc, ctx.a
    ctx.sem("s_ld"); ctx.sem("s_sc"); ctx.sem("s_gp"); ctx.sem("s_st")
    ctx.sem("s_pe"); ctx.sem("s_act"); ctx.sem("s_dve"); ctx.sem("s_cc")
    ctx.sem("s_cc_q"); ctx.sem("s_cc_at"); ctx.sem("s_sp")
    for i in range(8):
        ctx.sem("q%d" % i)
    with nc.Block() as blk:
        @blk.sync
        def _(s):
            ctx.dinc(s.dma_start(out=a["identf"][:, :], in_=ctx.identf_dram[:, :]), "s_ld")
            ctx.dinc(s.dma_start(out=a["consts"][:, :], in_=ctx.consts_dram[:, :]), "s_ld")
            w = ctx.dinc(s.dma_start(out=a["fconst"][:, :], in_=ctx.fconst_dram[:, :]), "s_ld")
            s.wait_ge(ctx.sems["s_ld"], w)


def p_load_x(ctx, x_dram):
    nc, a = ctx.nc, ctx.a
    arena_f32 = a["arena"].ap().bitcast(F32)

    pe0 = ctx.cnt["s_pe"]
    dve0 = ctx.cnt["s_dve"]
    wv = {}

    with nc.Block() as blk:
        @blk.sync
        def _(s):
            for tt in range(8):
                buf = tt % 2
                if tt >= 2:
                    s.wait_ge(ctx.sems["s_pe"], pe0 + (tt - 1) * 16)
                wv[tt] = ctx.slot_dma(s.dma_start(
                    out=arena_f32[:, buf * 2048:(buf + 1) * 2048],
                    in_=x_dram[tt * 128:(tt + 1) * 128, :]), buf)

        @blk.tensor
        def _(t):
            for tt in range(8):
                buf = tt % 2
                ctx.slot_wait(t, buf, wv[tt])
                for d in range(16):
                    j = tt * 16 + d
                    dst = a["ps_b"][:, (d % 2) * 512:(d % 2) * 512 + 128]
                    if j >= 2:
                        t.wait_ge(ctx.sems["s_dve"], dve0 + j - 1)
                    ctx.cinc(t.transpose(
                        dst,
                        arena_f32[:, buf * 2048 + d * 128: buf * 2048 + (d + 1) * 128],
                        a["identf"][:, :]), "s_pe")

        @blk.vector
        def _(v):
            for tt in range(8):
                for d in range(16):
                    j = tt * 16 + d
                    src = a["ps_b"][:, (d % 2) * 512:(d % 2) * 512 + 128]
                    v.wait_ge(ctx.sems["s_pe"], pe0 + j + 1)
                    ctx.cinc(v.tensor_copy(
                        a["x_fm"][:, d * TOK + tt * 128: d * TOK + (tt + 1) * 128], src),
                        "s_dve")


def p_layernorm(ctx):
    nc, a = ctx.nc, ctx.a
    ones_col = a["consts"][:, 320:321]
    x_fm = a["x_fm"]
    act0 = ctx.cnt["s_act"]
    pe0 = ctx.cnt["s_pe"]

    bufs = [a["scr"], a["hidout"]]
    with nc.Block() as blk:
        @blk.scalar
        def _(sc):
            for j in range(32):
                d, tb = j // 2, j % 2
                bf = bufs[j % 2]
                src = x_fm[:, d * TOK + tb * 512: d * TOK + (tb + 1) * 512]
                if j >= 2:
                    sc.wait_ge(ctx.sems["s_pe"], pe0 + j - 1)
                sc.copy(bf[:, 0:512], src)
                ctx.cinc(sc.activation(bf[:, 512:1024], src, AF.Square), "s_act")

        @blk.tensor
        def _(t):
            for j in range(32):
                d, tb = j // 2, j % 2
                bf = bufs[j % 2]
                t.wait_ge(ctx.sems["s_act"], act0 + j + 1)
                first, last = (d == 0), (d == 15)
                t.matmul(a["ps_c"][0:1, tb * 512:(tb + 1) * 512], ones_col,
                         bf[:, 0:512], start=first, stop=last)
                ctx.cinc(t.matmul(a["ps_b"][0:1, tb * 512:(tb + 1) * 512], ones_col,
                                  bf[:, 512:1024], start=first, stop=last), "s_pe")
    pe_stats = ctx.cnt["s_pe"]

    xp32 = a["xpool"].ap().bitcast(F32)
    for tb in range(2):
        r_mean = a["sc_a"][0:1, 0:512]
        r_rstd = xp32[0:1, 0:512]
        r_negmr = xp32[0:1, 512:1024]
        with nc.Block() as blk:
            @blk.vector
            def _(v):
                v.wait_ge(ctx.sems["s_pe"], pe_stats)
                v.tensor_scalar_mul(r_mean, a["ps_c"][0:1, tb * 512:(tb + 1) * 512], 1.0 / D)
                ctx.cinc(v.tensor_scalar_mul(r_rstd, a["ps_b"][0:1, tb * 512:(tb + 1) * 512],
                                             1.0 / D), "s_dve")
            d1 = ctx.cnt["s_dve"]

            @blk.scalar
            def _(sc):
                sc.wait_ge(ctx.sems["s_dve"], d1)
                ctx.cinc(sc.activation(r_negmr, r_mean, AF.Square), "s_act")
            a1 = ctx.cnt["s_act"]

            @blk.vector
            def _(v):
                v.wait_ge(ctx.sems["s_act"], a1)
                ctx.cinc(v.tensor_sub(r_rstd, r_rstd, r_negmr), "s_dve")
            d2 = ctx.cnt["s_dve"]

            @blk.scalar
            def _(sc):
                sc.wait_ge(ctx.sems["s_dve"], d2)
                sc.activation(r_rstd, r_rstd, AF.Ln, bias=a["fconst"][0:1, 128:129])
                ctx.cinc(sc.activation(r_rstd, r_rstd, AF.Exp, scale=-0.5), "s_act")
            a2 = ctx.cnt["s_act"]

            @blk.vector
            def _(v):
                v.wait_ge(ctx.sems["s_act"], a2)
                v.tensor_mul(r_negmr, r_mean, r_rstd)
                v.tensor_scalar_mul(r_negmr, r_negmr, -1.0)

        with nc.Block() as blk:
            @blk.tensor
            def _(t):
                t.matmul(a["ps_a"][:, 0:512], a["fconst"][0:1, 0:128],
                         r_rstd, start=True, stop=True)
                ctx.cinc(t.matmul(a["ps_a"][:, 512:1024], a["fconst"][0:1, 0:128],
                                  r_negmr, start=True, stop=True), "s_pe")
            pe_bc = ctx.cnt["s_pe"]

            @blk.vector
            def _(v):
                v.wait_ge(ctx.sems["s_pe"], pe_bc)
                for d in range(16):
                    xt = x_fm[:, d * TOK + tb * 512: d * TOK + (tb + 1) * 512]
                    v.tensor_tensor(a["sc_b"][:, :], xt, a["ps_a"][:, 0:512], ALU.mult)
                    v.tensor_tensor(a["xn"][:, d * TOK + tb * 512: d * TOK + (tb + 1) * 512],
                                    a["sc_b"][:, :], a["ps_a"][:, 512:1024], ALU.add)


def p_xn_ag(ctx):
    nc, a = ctx.nc, ctx.a
    with nc.Block() as blk:
        @blk.scalar
        def _(sc):
            for d in range(16):
                ctx.dinc(sc.dma_start(out=ctx.xn_ag_in[d * 128:(d + 1) * 128, :],
                                      in_=a["xn"][:, d * TOK:(d + 1) * TOK]), "s_sc")

        sc_done = ctx.cnt["s_sc"]

        @blk.gpsimd
        def _(g):
            g.wait_ge(ctx.sems["s_sc"], sc_done)
            ctx.cinc(g.collective_compute(
                "AllGather", ALU.bypass,
                replica_groups=[list(range(8))],
                ins=[ctx.xn_ag_in.ap().opt()], outs=[ctx.xn_ag_out.ap().opt()]), "s_cc")
    ctx.xn_cc_done = ctx.cnt["s_cc"]



QO, KO, VO, MO, WLO = 0, 4096, 8192, 12288, 16384
RHS = 22528


def p_wcast(ctx):
    nc = ctx.nc
    ctx.wready = {}
    with nc.Block() as blk:
        @blk.gpsimd
        def _(g):
            for name, src_t, dst_t in ctx.cast_list:
                if ctx.wready:
                    g.wait_ge(ctx.sems["s_gp"], ctx.cnt["s_gp"])
                ctx.wready[name] = ctx.dinc(
                    g.dma_start(out=dst_t.ap(), in_=src_t.ap()), "s_gp")


def p_qkv(ctx, layer, b):
    nc, a = ctx.nc, ctx.a
    ar = a["arena"]
    wsl = ctx.wq_bf[layer]
    sp = ctx.kvspill

    with nc.Block() as blk:
        @blk.sync
        def _(s):
            s.wait_ge(ctx.sems["s_gp"], ctx.wready["wq%d" % layer])
            ctx.mask_ld_done = ctx.dinc(
                s.dma_start(out=ar[:, MO:MO + 4096], in_=ctx.masks_dram[:, :]), "s_ld")
            for d in range(16):
                ctx.dinc(s.dma_start(
                    out=ar[:, WLO + d * 768: WLO + (d + 1) * 768],
                    in_=wsl[d * 128:(d + 1) * 128, 0:768]), "s_ld")
    ld_batch = ctx.cnt["s_ld"]
    ctx.qkv_ld_batch = ld_batch

    xv = ctx.xn_ag_out.ap().rearrange("(r p) n -> p r n", p=128)
    pe0 = ctx.cnt["s_pe"]
    dve0 = ctx.cnt["s_dve"]
    sp0 = ctx.cnt["s_sp"]
    wv = {}
    with nc.Block() as blk:
        @blk.scalar
        def _(sc):
            sc.wait_ge(ctx.sems["s_cc"], ctx.xn_cc_done)
            for g in range(8):
                r, sl = _owner(g)
                rr = 16 * (4 * b + r)
                for d in range(16):
                    idx = g * 16 + d
                    slot = idx % 4
                    if idx >= 4:
                        gp, dp = (idx - 4) // 16, (idx - 4) % 16
                        sc.wait_ge(ctx.sems["s_pe"], pe0 + gp * 24 + dp + 1)
                    wv[idx] = ctx.slot_dma(sc.dma_start(
                        out=a["xpool"][:, slot * 512:(slot + 1) * 512],
                        in_=xv[:, rr + d, sl * 512:(sl + 1) * 512]), slot)

        @blk.tensor
        def _(t):
            t.wait_ge(ctx.sems["s_ld"], ld_batch)
            for g in range(8):
                for d in range(16):
                    idx = g * 16 + d
                    slot = idx % 4
                    ctx.slot_wait(t, slot, wv[idx])
                    if d == 0 and g > 0:
                        t.wait_ge(ctx.sems["s_dve"], dve0 + g * 8)
                    rhs = a["xpool"][:, slot * 512:(slot + 1) * 512]
                    first, last = (d == 0), (d == 15)
                    w = ar[:, WLO + d * 768:WLO + (d + 1) * 768]
                    t.matmul(a["ps_a"][:, 0:512], w[:, 0:128], rhs, start=first, stop=last)
                    t.matmul(a["ps_a"][:, 512:1024], w[:, 128:256], rhs, start=first, stop=last)
                    t.matmul(a["ps_a"][:, 1024:1536], w[:, 256:384], rhs, start=first, stop=last)
                    t.matmul(a["ps_a"][:, 1536:2048], w[:, 384:512], rhs, start=first, stop=last)
                    t.matmul(a["ps_b"][:, 0:512], w[:, 512:640], rhs, start=first, stop=last)
                    ctx.cinc(t.matmul(a["ps_b"][:, 512:1024], w[:, 640:768], rhs,
                                      start=first, stop=last), "s_pe")
                for j in range(4):
                    if j == 0:
                        t.wait_ge(ctx.sems["s_dve"], dve0 + g * 8 + 5)
                    ctx.cinc(t.transpose(a["ps_c"][:, j * 128:(j + 1) * 128],
                                         a["sc_a"][:, j * 128:(j + 1) * 128],
                                         a["identf"][:, :]), "s_pe")
                for j in range(4):
                    if j == 0:
                        t.wait_ge(ctx.sems["s_dve"], dve0 + g * 8 + 6)
                    ctx.cinc(t.transpose(a["ps_c"][:, 512 + j * 128: 512 + (j + 1) * 128],
                                         a["sc_b"][:, j * 128:(j + 1) * 128],
                                         a["identf"][:, :]), "s_pe")

        @blk.vector
        def _(v):
            for g in range(8):
                v.wait_ge(ctx.sems["s_pe"], pe0 + g * 24 + 16)
                if g > 0:
                    v.wait_ge(ctx.sems["s_sp"], sp0 + g * 48)
                ctx.cinc(v.tensor_copy(ar[:, QO + g * 512:QO + (g + 1) * 512],
                                       a["ps_a"][:, 0:512]), "s_dve")
                ctx.cinc(v.tensor_copy(ar[:, KO + g * 512:KO + (g + 1) * 512],
                                       a["ps_a"][:, 1024:1536]), "s_dve")
                ctx.cinc(v.tensor_copy(a["scr"][:, 0:512], a["ps_a"][:, 512:1024]), "s_dve")
                ctx.cinc(v.tensor_copy(a["scr"][:, 512:1024], a["ps_a"][:, 1536:2048]), "s_dve")
                ctx.cinc(v.tensor_copy(a["sc_a"][:, :], a["ps_b"][:, 0:512]), "s_dve")
                ctx.cinc(v.tensor_copy(a["sc_b"][:, :], a["ps_b"][:, 512:1024]), "s_dve")
                v.wait_ge(ctx.sems["s_pe"], pe0 + g * 24 + 20)
                ctx.cinc(v.tensor_copy(ar[:, VO + g * 512:VO + (g + 1) * 512],
                                       a["ps_c"][:, 0:512]), "s_dve")
                v.wait_ge(ctx.sems["s_pe"], pe0 + g * 24 + 24)
                ctx.cinc(v.tensor_copy(a["vtmp"][:, 0:512], a["ps_c"][:, 512:1024]), "s_dve")

        @blk.sync
        def _(s2):
            for g in range(8):
                s2.wait_ge(ctx.sems["s_dve"], dve0 + g * 8 + 8)
                ctx.dinc(s2.dma_start(out=sp[384 * b + 0:384 * b + 128, g * 512:(g + 1) * 512],
                                      in_=a["scr"][:, 0:512]), "s_sp")
                ctx.dinc(s2.dma_start(out=sp[384 * b + 128:384 * b + 256, g * 512:(g + 1) * 512],
                                      in_=a["scr"][:, 512:1024]), "s_sp")
                ctx.dinc(s2.dma_start(out=sp[384 * b + 256:384 * b + 384, g * 512:(g + 1) * 512],
                                      in_=a["vtmp"][:, 0:512]), "s_sp")


def p_quant1(ctx):
    nc, a = ctx.nc, ctx.a
    arF = a["arena"].ap().bitcast(F32)
    vF = a["vtmp"].ap().bitcast(F32)
    with nc.Block() as blk:
        dve0 = ctx.cnt["s_dve"]
        wv = {}

        @blk.gpsimd
        def _(g):
            for m in range(3):
                for ch in range(2):
                    i = m * 2 + ch
                    if i >= 2:
                        g.wait_ge(ctx.sems["s_dve"], dve0 + i - 1)
                    wv[i] = ctx.slot_dma(g.dma_start(
                        out=arF[:, (i % 2) * 5632:(i % 2) * 5632 + 5632],
                        in_=ctx.ffn0_sl[:, m * QSLF + ch * 5632: m * QSLF + (ch + 1) * 5632]),
                        i % 2)

        @blk.vector
        def _(v):
            for m in range(3):
                for ch in range(2):
                    i = m * 2 + ch
                    ctx.slot_wait(v, i % 2, wv[i])
                    ctx.cinc(v.tensor_reduce(vF[:, i:i + 1],
                                             arF[:, (i % 2) * 5632:(i % 2) * 5632 + 5632],
                                             AX.X, ALU.add, apply_absolute_value=True), "s_dve")
        dve_red = ctx.cnt["s_dve"]

        @blk.tensor
        def _(t):
            t.wait_ge(ctx.sems["s_dve"], dve_red)
            ctx.cinc(t.matmul(a["ps_c"][0:1, 0:6], a["fconst"][:, 0:1], vF[:, 0:6],
                              start=True, stop=True), "s_pe")
        pe_s = ctx.cnt["s_pe"]

        @blk.vector
        def _(v):
            v.wait_ge(ctx.sems["s_pe"], pe_s)
            ctx.cinc(v.tensor_reduce(
                a["xpool"].ap().bitcast(F32)[0:1, 0:3],
                a["ps_c"][0:1, 0:6].rearrange("p (m two) -> p m two", two=2),
                AX.X, ALU.add), "s_dve")
        dve_c = ctx.cnt["s_dve"]

        @blk.scalar
        def _(sc):
            sc.wait_ge(ctx.sems["s_dve"], dve_c)
            ctx.dinc(sc.dma_start(out=ctx.gm_in[:, :],
                                  in_=a["xpool"].ap().bitcast(F32)[0:1, 0:3]), "s_sc")
        sc_w = ctx.cnt["s_sc"]

        @blk.gpsimd
        def _(g):
            g.wait_ge(ctx.sems["s_sc"], sc_w)
            ctx.cinc(g.collective_compute(
                "AllGather", ALU.bypass, replica_groups=[list(range(8))],
                ins=[ctx.gm_in.ap().opt()], outs=[ctx.gm_out.ap().opt()]), "s_cc_q")
    ctx.gm_cc_done = ctx.cnt["s_cc_q"]


def p_quant2(ctx):
    nc, a = ctx.nc, ctx.a
    vF = a["vtmp"].ap().bitcast(F32)
    wF = a["wpool"].ap().bitcast(F32)
    N = float(H0 * D)
    with nc.Block() as blk:
        @blk.gpsimd
        def _(g):
            g.wait_ge(ctx.sems["s_cc_q"], ctx.gm_cc_done)
            ctx.dinc(g.dma_start(out=a["sc_a"][0:8, 0:3], in_=ctx.gm_out[:, :]), "s_gp")
        gp_l = ctx.cnt["s_gp"]

        @blk.tensor
        def _(t):
            t.wait_ge(ctx.sems["s_gp"], gp_l)
            ctx.cinc(t.matmul(a["ps_c"][0:1, 0:3], a["fconst"][0:8, 0:1],
                              a["sc_a"][0:8, 0:3], start=True, stop=True), "s_pe")
        pe_t = ctx.cnt["s_pe"]

        @blk.vector
        def _(v):
            v.wait_ge(ctx.sems["s_pe"], pe_t)
            ctx.cinc(v.tensor_scalar(a["xpool"].ap().bitcast(F32)[0:1, 4:7],
                                     a["ps_c"][0:1, 0:3], 1.0 / (2.0 * N), Q_EPS / 2.0,
                                     ALU.mult, ALU.add), "s_dve")
        act_t = ctx.cnt["s_dve"]

        @blk.tensor
        def _(t):
            t.wait_ge(ctx.sems["s_dve"], act_t)
            for m in range(3):
                ctx.cinc(t.matmul(a["ps_a"][:, m * 512:m * 512 + 1],
                                  a["fconst"][0:1, 0:128],
                                  a["xpool"].ap().bitcast(F32)[0:1, 4 + m:5 + m],
                                  start=True, stop=True), "s_pe")
        pe_b = ctx.cnt["s_pe"]

        @blk.vector
        def _(v):
            v.wait_ge(ctx.sems["s_pe"], pe_b)
            for m in range(3):
                v.tensor_copy(vF[:, 8 + m:9 + m], a["ps_a"][:, m * 512:m * 512 + 1])
                ctx.cinc(v.tensor_scalar_mul(vF[:, 12 + m:13 + m],
                                             a["ps_a"][:, m * 512:m * 512 + 1], -1.0), "s_dve")
    thr_done = ctx.cnt["s_dve"]

    dve0 = ctx.cnt["s_dve"]
    wv = {}
    sv = {}
    with nc.Block() as blk:
        @blk.sync
        def _(s):
            for m in range(3):
                for ch in range(11):
                    i = m * 11 + ch
                    if i >= 2:
                        s.wait_ge(ctx.sems["s_dve"], dve0 + (i - 2) * 3 + 3)
                    wv[i] = ctx.slot_dma(s.dma_start(
                        out=wF[:, (i % 2) * 1024:(i % 2) * 1024 + 1024],
                        in_=ctx.ffn0_sl[:, m * QSLF + ch * 1024: m * QSLF + (ch + 1) * 1024]),
                        i % 2)

        @blk.scalar
        def _(sc):
            for m in range(3):
                for ch in range(11):
                    i = m * 11 + ch
                    sc.wait_ge(ctx.sems["s_dve"], dve0 + i * 3 + 3)
                    sv[i] = ctx.slot_dma(sc.dma_start(
                        out=ctx.q_in[m][:, ch * 1024:(ch + 1) * 1024],
                        in_=a["hidout"][:, (i % 2) * 1024:(i % 2) * 1024 + 1024]), 2 + i % 2)
        sc_all = [None]

        @blk.vector
        def _(v):
            v.wait_ge(ctx.sems["s_dve"], thr_done)
            for m in range(3):
                for ch in range(11):
                    i = m * 11 + ch
                    ctx.slot_wait(v, i % 2, wv[i])
                    wbuf = wF[:, (i % 2) * 1024:(i % 2) * 1024 + 1024]
                    if i >= 2:
                        ctx.slot_wait(v, 2 + i % 2, sv[i - 2])
                    v.tensor_scalar(a["praw"][:, 0:1024], wbuf, vF[:, 8 + m:9 + m], None,
                                    ALU.is_gt)
                    v.tensor_scalar(a["pmask"][:, 0:1024], wbuf, vF[:, 12 + m:13 + m], None,
                                    ALU.is_lt)
                    ctx.cinc(v.tensor_sub(a["hidout"][:, (i % 2) * 1024:(i % 2) * 1024 + 1024],
                                          a["praw"][:, 0:1024], a["pmask"][:, 0:1024]),
                             "s_dve", 3)

        @blk.gpsimd
        def _(g):
            g.wait_ge(ctx.sems["q3"], sv[31])
            g.wait_ge(ctx.sems["q2"], sv[32])
            for m in range(3):
                ctx.cinc(g.collective_compute(
                    "AllGather", ALU.bypass, replica_groups=[list(range(8))],
                    ins=[ctx.q_in[m].ap().opt()], outs=[ctx.q_out[m].ap().opt()]), "s_cc_q")
    ctx.q_cc_done = ctx.cnt["s_cc_q"]


def p_ffn0(ctx, hf):
    nc, a = ctx.nc, ctx.a
    ar = a["arena"]
    q1 = ctx.q_out[0].ap().rearrange("p n -> (p n)").rearrange("(d h) -> d h", h=H0)
    q2 = ctx.q_out[1].ap().rearrange("p n -> (p n)").rearrange("(d h) -> d h", h=H0)
    q3 = ctx.q_out[2].ap().rearrange("p n -> (p n)").rearrange("(h d) -> h d", d=D)

    hts = [hf * 22 + i for i in range(22)]
    pe0 = ctx.cnt["s_pe"]
    act0 = ctx.cnt["s_act"]
    dve0 = ctx.cnt["s_dve"]
    st3_pre = ctx.cnt["q3"]
    st7_pre = ctx.cnt["q7"]
    wv = {}
    sgm = [a["scr"].ap()[:, 0:1024], a["hidout"].ap()[:, 0:1024]]
    with nc.Block() as blk:
        @blk.sync
        def _(s):
            s.wait_ge(ctx.sems["s_cc_q"], ctx.q_cc_done)
            for i, ht in enumerate(hts):
                for dh in range(2):
                    step = i * 2 + dh
                    slot = step % 4
                    if step >= 4:
                        s.wait_ge(ctx.sems["s_pe"], pe0 + step - 3)
                    for mi, qm in enumerate((q1, q2)):
                        src = qm.rearrange("(dt p) h -> p dt h", p=128)
                        dst = a["wpool"][:, slot * 2048 + mi * 1024:
                                         slot * 2048 + (mi + 1) * 1024]
                        wv[step] = ctx.slot_dma(s.dma_start(
                            out=dst.rearrange("p (a b) -> p a b", b=128),
                            in_=src[:, dh * 8:(dh + 1) * 8, ht * 128:(ht + 1) * 128]), slot)

        @blk.tensor
        def _(t):
            for i, ht in enumerate(hts):
                psx1 = a["ps_a"].ap()[:, 0:1024] if i % 2 == 0 else a["ps_b"].ap()[:, :]
                psx2 = a["ps_a"].ap()[:, 1024:2048] if i % 2 == 0 else a["ps_c"].ap()[:, :]
                if i >= 2:
                    t.wait_ge(ctx.sems["s_dve"], dve0 + i - 1)
                for dh in range(2):
                    step = i * 2 + dh
                    slot = step % 4
                    ctx.slot_wait(t, slot, wv[step])
                    for mi in range(2):
                        psd = psx1 if mi == 0 else psx2
                        for d8 in range(8):
                            d = dh * 8 + d8
                            w = a["wpool"][:, slot * 2048 + mi * 1024 + d8 * 128:
                                           slot * 2048 + mi * 1024 + (d8 + 1) * 128]
                            for th in range(2):
                                ii = t.matmul(psd[:, th * 512:(th + 1) * 512], w,
                                              a["xn"][:, d * TOK + th * 512: d * TOK + (th + 1) * 512],
                                              start=(d == 0), stop=(d == 15))
                    ctx.cinc(ii, "s_pe")

        @blk.scalar
        def _(sc):
            sc.wait_ge(ctx.sems["q3"], st3_pre)
            sc.wait_ge(ctx.sems["q7"], st7_pre)
            for i, ht in enumerate(hts):
                psx1 = a["ps_a"].ap()[:, 0:1024] if i % 2 == 0 else a["ps_b"].ap()[:, :]
                sc.wait_ge(ctx.sems["s_pe"], pe0 + i * 2 + 2)
                if i >= 2:
                    sc.wait_ge(ctx.sems["s_dve"], dve0 + i - 1)
                ctx.cinc(sc.activation(sgm[i % 2], psx1, AF.Sigmoid), "s_act")

        @blk.vector
        def _(v):
            for i, ht in enumerate(hts):
                psx2 = a["ps_a"].ap()[:, 1024:2048] if i % 2 == 0 else a["ps_c"].ap()[:, :]
                v.wait_ge(ctx.sems["s_act"], act0 + i + 1)
                ctx.cinc(v.tensor_tensor(ar[:, i * 1024:(i + 1) * 1024], sgm[i % 2],
                                         psx2, ALU.mult), "s_dve")

    pe1 = ctx.cnt["s_pe"]
    dve1 = ctx.cnt["s_dve"]
    wv3 = {}
    with nc.Block() as blk:
        @blk.sync
        def _(s):
            for dgrp in range(4):
                for i, ht in enumerate(hts):
                    step = dgrp * 22 + i
                    slot = step % 8
                    if step >= 8:
                        s.wait_ge(ctx.sems["s_pe"], pe1 + step - 7)
                    wv3[step] = ctx.slot_dma(s.dma_start(
                        out=a["wpool"][:, slot * 1024:slot * 1024 + 512],
                        in_=q3[ht * 128:(ht + 1) * 128, dgrp * 512:(dgrp + 1) * 512]), slot)

        @blk.tensor
        def _(t):
            for dgrp in range(4):
                if dgrp > 0:
                    t.wait_ge(ctx.sems["s_dve"], dve1 + dgrp * 8)
                for i, ht in enumerate(hts):
                    step = dgrp * 22 + i
                    slot = step % 8
                    ctx.slot_wait(t, slot, wv3[step])
                    for do in range(4):
                        w = a["wpool"][:, slot * 1024 + do * 128: slot * 1024 + (do + 1) * 128]
                        for th in range(2):
                            psd = (a["ps_a"].ap()[:, do * 512:(do + 1) * 512] if th == 0 else
                                   (a["ps_b"].ap()[:, do % 2 * 512:do % 2 * 512 + 512] if do < 2 else
                                    a["ps_c"].ap()[:, (do - 2) % 2 * 512:(do - 2) % 2 * 512 + 512]))
                            ii = t.matmul(psd, w,
                                          ar[:, i * 1024 + th * 512: i * 1024 + (th + 1) * 512],
                                          start=(i == 0), stop=(i == 21))
                    ctx.cinc(ii, "s_pe")

        @blk.vector
        def _(v):
            for dgrp in range(4):
                v.wait_ge(ctx.sems["s_pe"], pe1 + dgrp * 22 + 22)
                for do in range(4):
                    dt_ = dgrp * 4 + do
                    for th in range(2):
                        psd = (a["ps_a"].ap()[:, do * 512:(do + 1) * 512] if th == 0 else
                               (a["ps_b"].ap()[:, do % 2 * 512:do % 2 * 512 + 512] if do < 2 else
                                a["ps_c"].ap()[:, (do - 2) % 2 * 512:(do - 2) % 2 * 512 + 512]))
                        xt = a["x_fm"][:, dt_ * TOK + th * 512: dt_ * TOK + (th + 1) * 512]
                        ctx.cinc(v.tensor_tensor(xt, xt, psd, ALU.add), "s_dve")


def _unit_plan(layer, qb):
    if layer == 0:
        kts = []
        if qb > 0:
            kts += [(4 * (qb - 1) + j, MO + j * 512) for j in range(4)]
        kts += [(4 * qb + j, MO + 2048 + j * 512) for j in range(4)]
    else:
        kts = [(j, None) for j in range(4 * qb)]
        kts += [(4 * qb + j, MO + 2048 + j * 512) for j in range(4)]
    return [(kts[2 * i], kts[2 * i + 1]) for i in range(len(kts) // 2)]


def p_attn_unit(ctx, layer, b, hl, reload):
    nc, a = ctx.nc, ctx.a
    ar = a["arena"]
    if reload:
        spill_all = ctx.cnt["s_sp"]
        with nc.Block() as blk:
            @blk.sync
            def _(s_):
                s_.wait_ge(ctx.sems["s_sp"], spill_all)
                ws = []
                for i, off in enumerate((QO, KO, VO)):
                    ws.append(ctx.slot_dma(s_.dma_start(
                        out=ar[:, off:off + 4096],
                        in_=ctx.kvspill[384 * b + 128 * i:384 * b + 128 * (i + 1), :]), 4 + i))
                for i in range(3):
                    s_.wait_ge(ctx.sems["q%d" % (4 + i)], ws[i])

    pe0 = ctx.cnt["s_pe"]
    act0 = ctx.cnt["s_act"]
    dve0 = ctx.cnt["s_dve"]
    stv = {}
    st3_0 = ctx.cnt["q3"]
    st7_0 = ctx.cnt["q7"]
    sp_all = ctx.cnt["s_sp"]

    nqb = ctx.cfg.get("nqb", 8)
    pe_n, act_n, dve_n, sc_n = [0], [0], [0], [0]
    recs = []
    gi = 0
    for qb in range(nqb):
        prs = []
        for (p1, p2) in _unit_plan(layer, qb):
            masked = p1[1] is not None
            pe_n[0] += 2
            act_n[0] += 1
            if masked:
                dve_n[0] += 1
            prs.append(dict(p1=p1, p2=p2, gi=gi, masked=masked,
                            s_after=pe_n[0] - 1, pv_after=pe_n[0],
                            e_after=act_n[0], m_after=dve_n[0]))
            gi += 1
        rec = dict(pairs=prs, qb=qb,
                   recip_after=dve_n[0] + 1, bcev_after=dve_n[0] + 2,
                   mult_after=dve_n[0] + 3, bc_after=pe_n[0] + 1)
        dve_n[0] += 3
        pe_n[0] += 1
        recs.append(rec)

    hoff = hl * 128

    with nc.Block() as blk:
        @blk.tensor
        def _(t):
            for rec in recs:
                qb = rec["qb"]
                qcols = slice(QO + qb * 512, QO + (qb + 1) * 512)
                psO = a["ps_b"][:, (qb % 2) * 512:(qb % 2) * 512 + 512]
                psD = a["ps_c"][0:1, (qb % 2) * 512:(qb % 2) * 512 + 512]
                for pi, pr in enumerate(rec["pairs"]):
                    buf = pr["gi"] % 2
                    if pr["gi"] >= 2:
                        t.wait_ge(ctx.sems["s_act"], act0 + pr["e_after"] - 2)
                    for j, (kt, _m) in enumerate((pr["p1"], pr["p2"])):
                        i = t.matmul(a["ps_a"][:, buf * 1024 + j * 512: buf * 1024 + (j + 1) * 512],
                                     ar[:, KO + kt * 128:KO + (kt + 1) * 128],
                                     ar[:, qcols], start=True, stop=True)
                    ctx.cinc(i, "s_pe")
                    if pr["masked"]:
                        t.wait_ge(ctx.sems["s_dve"], dve0 + pr["m_after"])
                    else:
                        t.wait_ge(ctx.sems["s_act"], act0 + pr["e_after"])
                    if pi == 0 and qb >= 2:
                        t.wait_ge(ctx.sems["s_dve"], dve0 + recs[qb - 2]["mult_after"])
                    src = a["pmask"] if pr["masked"] else a["praw"]
                    for j, (kt, _m) in enumerate((pr["p1"], pr["p2"])):
                        pp = src[:, buf * 1024 + j * 512: buf * 1024 + (j + 1) * 512]
                        t.matmul(psO, ar[:, VO + kt * 128:VO + (kt + 1) * 128], pp,
                                 start=(pi == 0 and j == 0), stop=False)
                        i = t.matmul(psD, a["consts"][:, 320:321], pp,
                                     start=(pi == 0 and j == 0), stop=False)
                    ctx.cinc(i, "s_pe")
                t.wait_ge(ctx.sems["s_dve"], dve0 + rec["recip_after"])
                psDfull = a["ps_c"][:, (qb % 2) * 512:(qb % 2) * 512 + 512]
                ctx.cinc(t.matmul(psDfull, a["consts"][0:1, 320:448],
                                  a["vtmp"][0:1, (qb % 2) * 512:(qb % 2) * 512 + 512],
                                  start=True, stop=True), "s_pe")

        @blk.scalar
        def _(sc):
            for rec in recs:
                for pr in rec["pairs"]:
                    buf = pr["gi"] % 2
                    sc.wait_ge(ctx.sems["s_pe"], pe0 + pr["s_after"])
                    ctx.cinc(sc.activation(a["praw"][:, buf * 1024:(buf + 1) * 1024],
                                           a["ps_a"][:, buf * 1024:(buf + 1) * 1024],
                                           AF.Exp, scale=SCALE), "s_act")
                qb = rec["qb"]
                sc.wait_ge(ctx.sems["s_dve"], dve0 + rec["mult_after"])
                r, sl = _owner(qb)
                dst_c = 1024 * (4 * b + r) + 512 * sl
                stv[qb] = ctx.slot_dma(sc.dma_start(
                    out=ctx.attn_ag_in[hoff:hoff + 128, dst_c:dst_c + 512],
                    in_=a["scr"][:, (qb % 2) * 512:(qb % 2) * 512 + 512]), 7 if qb % 2 else 3)

        @blk.vector
        def _(v):
            v.wait_ge(ctx.sems["s_ld"], ctx.qkv_ld_batch)
            for rec in recs:
                qb = rec["qb"]
                for pr in rec["pairs"]:
                    if not pr["masked"]:
                        continue
                    buf = pr["gi"] % 2
                    v.wait_ge(ctx.sems["s_act"], act0 + pr["e_after"])
                    moff = pr["p1"][1]
                    ctx.cinc(v.tensor_tensor(a["pmask"][:, buf * 1024:(buf + 1) * 1024],
                                             a["praw"][:, buf * 1024:(buf + 1) * 1024],
                                             ar[:, moff:moff + 1024], ALU.mult), "s_dve")
                psD = a["ps_c"][0:1, (qb % 2) * 512:(qb % 2) * 512 + 512]
                psDfull = a["ps_c"][:, (qb % 2) * 512:(qb % 2) * 512 + 512]
                psO = a["ps_b"][:, (qb % 2) * 512:(qb % 2) * 512 + 512]
                rrow32 = a["xpool"].ap().bitcast(F32)[0:1, (qb % 2) * 512:(qb % 2) * 512 + 512]
                rrow16 = a["vtmp"][0:1, (qb % 2) * 512:(qb % 2) * 512 + 512]
                if qb == 0:
                    v.wait_ge(ctx.sems["s_sp"], sp_all)
                v.wait_ge(ctx.sems["s_pe"], pe0 + rec["pairs"][-1]["pv_after"])
                v.reciprocal(rrow32, psD)
                ctx.cinc(v.tensor_copy(rrow16, rrow32), "s_dve")
                v.wait_ge(ctx.sems["s_pe"], pe0 + rec["bc_after"])
                ctx.cinc(v.tensor_copy(a["bc_sb"][:, :], psDfull), "s_dve")
                if qb >= 2:
                    ctx.slot_wait(v, 7 if qb % 2 else 3, stv[qb - 2])
                elif qb == 0:
                    v.wait_ge(ctx.sems["q3"], st3_0)
                elif qb == 1:
                    v.wait_ge(ctx.sems["q7"], st7_0)
                ctx.cinc(v.tensor_tensor(a["scr"][:, (qb % 2) * 512:(qb % 2) * 512 + 512],
                                         psO, a["bc_sb"][:, :], ALU.mult), "s_dve")
                if ctx.cfg.get("dump_p") and qb == 0 and b == 0 and hl == 0:
                    ctx.cinc(v.tensor_copy(a["sc_a"][:, :], a["ps_c"][:, 0:512]), "s_dve")
                    ctx.cinc(v.tensor_copy(a["sc_b"][:, :], a["ps_b"][:, 0:512]), "s_dve")


def p_attn_ag(ctx):
    nc = ctx.nc
    w3, w7 = ctx.cnt["q3"], ctx.cnt["q7"]
    with nc.Block() as blk:
        @blk.gpsimd
        def _(g):
            g.wait_ge(ctx.sems["q3"], w3)
            g.wait_ge(ctx.sems["q7"], w7)
            ctx.cinc(g.collective_compute(
                "AllGather", ALU.bypass,
                replica_groups=[list(range(8))],
                ins=[ctx.attn_ag_in.ap().opt()], outs=[ctx.attn_ag_out.ap().opt()]), "s_cc_at")
    ctx.attn_cc_done = ctx.cnt["s_cc_at"]


def p_proj(ctx, layer):
    nc, a = ctx.nc, ctx.a
    ar = a["arena"]
    wo = ctx.wo_bf[layer]
    av = ctx.attn_ag_out.ap().rearrange("(t p) n -> p t n", p=128)

    for tb in range(2):
        sc0 = ctx.cnt["s_sc"]
        pe0 = ctx.cnt["s_pe"]
        dve0 = ctx.cnt["s_dve"]
        wv = {}
        with nc.Block() as blk:
            @blk.scalar
            def _(sc):
                sc.wait_ge(ctx.sems["s_cc_at"], ctx.attn_cc_done)
                pid = sc.partition_id()
                dst = ar[:, RHS:RHS + 8192].rearrange("p (t n) -> p t n", n=512)
                ctx.dinc(sc.dma_start(
                    out=dst, in_=av[:, :, bass.ds(pid * 1024 + tb * 512, 512)]), "s_sc")

            @blk.sync
            def _(s):
                s.wait_ge(ctx.sems["s_gp"], ctx.wready["wo%d" % layer])
                for dgrp in range(4):
                    for din in range(16):
                        i = dgrp * 16 + din
                        slot = i % 8
                        if i >= 8:
                            s.wait_ge(ctx.sems["s_pe"], pe0 + (i - 8) + 1)
                        wv[i] = ctx.slot_dma(s.dma_start(
                            out=a["wpool"][:, slot * 1024:slot * 1024 + 512],
                            in_=wo[din * 128:(din + 1) * 128, dgrp * 512:(dgrp + 1) * 512]), slot)

            @blk.tensor
            def _(t):
                t.wait_ge(ctx.sems["s_sc"], sc0 + 16)
                for dgrp in range(4):
                    if dgrp > 0:
                        t.wait_ge(ctx.sems["s_dve"], dve0 + dgrp * 4)
                    for din in range(16):
                        i = dgrp * 16 + din
                        slot = i % 8
                        ctx.slot_wait(t, slot, wv[i])
                        rhs = ar[:, RHS + din * 512: RHS + (din + 1) * 512]
                        for do in range(4):
                            ii = t.matmul(a["ps_a"][:, do * 512:(do + 1) * 512],
                                          a["wpool"][:, slot * 1024 + do * 128: slot * 1024 + (do + 1) * 128],
                                          rhs, start=(din == 0), stop=(din == 15))
                        ctx.cinc(ii, "s_pe")

            @blk.vector
            def _(v):
                for dgrp in range(4):
                    v.wait_ge(ctx.sems["s_pe"], pe0 + dgrp * 16 + 16)
                    for do in range(4):
                        dt_ = dgrp * 4 + do
                        xt = a["x_fm"][:, dt_ * TOK + tb * 512: dt_ * TOK + (tb + 1) * 512]
                        ctx.cinc(v.tensor_tensor(xt, xt, a["ps_a"][:, do * 512:(do + 1) * 512],
                                                 ALU.add), "s_dve")


def p_ffn1a(ctx):
    nc, a = ctx.nc, ctx.a
    w1 = ctx.w1t_bf.ap().rearrange("(dt p) h -> p dt h", p=128)
    w3 = ctx.w3t_bf.ap().rearrange("(dt p) h -> p dt h", p=128)
    hv = ctx.hid_dram.ap().rearrange("(t p) n -> t p n", p=128)

    pe0 = ctx.cnt["s_pe"]
    act0 = ctx.cnt["s_act"]
    dve0 = ctx.cnt["s_dve"]
    b4, b5 = ctx.cnt["q4"], ctx.cnt["q5"]
    sv = {i: (b4 if i % 2 == 0 else b5) + (i // 2 + 1) * 16 for i in range(64)}
    wv = {}
    with nc.Block() as blk:
        @blk.sync
        def _(s):
            s.wait_ge(ctx.sems["s_gp"], ctx.wready["w3t"])
            for i in range(64):
                for dh in range(2):
                    step = i * 2 + dh
                    slot = step % 4
                    if step >= 4:
                        s.wait_ge(ctx.sems["s_pe"], pe0 + step - 3)
                    for mi, qm in enumerate((w1, w3)):
                        dst = a["wpool"][:, slot * 2048 + mi * 1024:
                                         slot * 2048 + (mi + 1) * 1024]
                        wv[step] = ctx.slot_dma(s.dma_start(
                            out=dst.rearrange("p (x b) -> p x b", b=128),
                            in_=qm[:, dh * 8:(dh + 1) * 8, i * 128:(i + 1) * 128]), slot)

        @blk.scalar
        def _(sc):
            for i in range(64):
                psx1 = a["ps_a"].ap()[:, 0:1024] if i % 2 == 0 else a["ps_b"].ap()[:, :]
                sc.wait_ge(ctx.sems["s_pe"], pe0 + i * 2 + 2)
                if i >= 2:
                    sc.wait_ge(ctx.sems["s_dve"], dve0 + i - 1)
                ctx.cinc(sc.activation(a["scr"][:, (i % 2) * 512:(i % 2) * 512 + 512],
                                       psx1[:, 0:512], AF.Silu), "s_act")
                ctx.cinc(sc.activation(a["vtmp"][:, (i % 2) * 512:(i % 2) * 512 + 512],
                                       psx1[:, 512:1024], AF.Silu), "s_act")

        @blk.tensor
        def _(t):
            for i in range(64):
                psx1 = a["ps_a"].ap()[:, 0:1024] if i % 2 == 0 else a["ps_b"].ap()[:, :]
                psx2 = a["ps_a"].ap()[:, 1024:2048] if i % 2 == 0 else a["ps_c"].ap()[:, :]
                if i >= 2:
                    t.wait_ge(ctx.sems["s_dve"], dve0 + i - 1)
                for dh in range(2):
                    step = i * 2 + dh
                    slot = step % 4
                    ctx.slot_wait(t, slot, wv[step])
                    for mi in range(2):
                        psd = psx1 if mi == 0 else psx2
                        for d8 in range(8):
                            d = dh * 8 + d8
                            w = a["wpool"][:, slot * 2048 + mi * 1024 + d8 * 128:
                                           slot * 2048 + mi * 1024 + (d8 + 1) * 128]
                            for th in range(2):
                                ii = t.matmul(psd[:, th * 512:(th + 1) * 512], w,
                                              a["xn"][:, d * TOK + th * 512: d * TOK + (th + 1) * 512],
                                              start=(d == 0), stop=(d == 15))
                    ctx.cinc(ii, "s_pe")

        @blk.vector
        def _(v):
            for i in range(64):
                psx2 = a["ps_a"].ap()[:, 1024:2048] if i % 2 == 0 else a["ps_c"].ap()[:, :]
                v.wait_ge(ctx.sems["s_act"], act0 + i * 2 + 2)
                if i >= 2:
                    ctx.slot_wait(v, 4 + i % 2, sv[i - 2])
                v.tensor_tensor(a["hidout"][:, (i % 2) * 1024:(i % 2) * 1024 + 512],
                                a["scr"][:, (i % 2) * 512:(i % 2) * 512 + 512],
                                psx2[:, 0:512], ALU.mult)
                ctx.cinc(v.tensor_tensor(
                    a["hidout"][:, (i % 2) * 1024 + 512:(i % 2) * 1024 + 1024],
                    a["vtmp"][:, (i % 2) * 512:(i % 2) * 512 + 512],
                    psx2[:, 512:1024], ALU.mult), "s_dve")

        @blk.gpsimd
        def _(g):
            for i in range(64):
                g.wait_ge(ctx.sems["s_dve"], dve0 + i + 1)
                w = ctx.slot_dma(g.dma_start(
                    out=hv[i, :, :],
                    in_=a["hidout"][:, (i % 2) * 1024:(i % 2) * 1024 + 1024]), 4 + i % 2)
                assert w == sv[i], (w, sv[i])


def p_fwht_b(ctx):
    nc, a = ctx.nc, ctx.a
    ar = a["arena"]
    hv = ctx.hid_dram.ap().rearrange("(t p) n -> t p n", p=128)
    bv = ctx.hidB_dram.ap().rearrange("(t p) n -> t p n", p=128)
    h64 = a["consts"][0:64, 256:320]
    pe0 = ctx.cnt["s_pe"]
    dve0 = ctx.cnt["s_dve"]
    hw4, hw5 = ctx.cnt["q4"], ctx.cnt["q5"]
    b2, b3 = ctx.cnt["q2"], ctx.cnt["q3"]
    sv = {c: (b2 if c % 2 == 0 else b3) + (c // 2 + 1) * 16 for c in range(16)}
    wv = {}
    with nc.Block() as blk:
        @blk.sync
        def _(s):
            s.wait_ge(ctx.sems["q4"], hw4)
            s.wait_ge(ctx.sems["q5"], hw5)
            for c in range(16):
                slot = c % 2
                if c >= 2:
                    s.wait_ge(ctx.sems["s_pe"], pe0 + (c - 1) * 16)
                wv[c] = ctx.slot_dma(s.dma_start(
                    out=ar[0:64, slot * 8192:(slot + 1) * 8192].rearrange(
                        "t (p n) -> t p n", n=1024),
                    in_=hv[:, c * 8:(c + 1) * 8, :]), slot)

        @blk.tensor
        def _(t):
            for c in range(16):
                slot = c % 2
                ctx.slot_wait(t, slot, wv[c])
                for j in range(16):
                    psd = a["ps_b"].ap() if j % 2 == 0 else a["ps_c"].ap()
                    if c * 16 + j >= 2:
                        t.wait_ge(ctx.sems["s_dve"], dve0 + c * 16 + j - 1)
                    ctx.cinc(t.matmul(psd[0:64, 0:512], h64,
                                      ar[0:64, slot * 8192 + j * 512: slot * 8192 + (j + 1) * 512],
                                      start=True, stop=True), "s_pe")

        @blk.vector
        def _(v):
            for c in range(16):
                oslot = c % 2
                if c >= 2:
                    ctx.slot_wait(v, 2 + oslot, sv[c - 2])
                for j in range(16):
                    psd = a["ps_b"].ap() if j % 2 == 0 else a["ps_c"].ap()
                    v.wait_ge(ctx.sems["s_pe"], pe0 + c * 16 + j + 1)
                    ctx.cinc(v.tensor_copy(
                        ar[0:64, 16384 + oslot * 8192 + j * 512: 16384 + oslot * 8192 + (j + 1) * 512],
                        psd[0:64, 0:512]), "s_dve")

        @blk.scalar
        def _(sc):
            for c in range(16):
                oslot = c % 2
                sc.wait_ge(ctx.sems["s_dve"], dve0 + c * 16 + 16)
                w = ctx.slot_dma(sc.dma_start(
                    out=bv[:, c * 8:(c + 1) * 8, :],
                    in_=ar[0:64, 16384 + oslot * 8192: 16384 + (oslot + 1) * 8192].rearrange(
                        "t (p n) -> t p n", n=1024)), 2 + oslot)
                assert w == sv[c], (w, sv[c])


def p_w2a(ctx):
    nc, a = ctx.nc, ctx.a
    ar = a["arena"]
    bv = ctx.hidB_dram.ap().rearrange("(t p) n -> t p n", p=128)
    w2 = ctx.w2t_bf
    h128 = a["consts"][:, 128:256]
    for tb in range(2):
        pe0 = ctx.cnt["s_pe"]
        dve0 = ctx.cnt["s_dve"]
        wv, wv2 = {}, {}
        with nc.Block() as blk:
            @blk.scalar
            def _(sc):
                if tb == 0:
                    sc.wait_ge(ctx.sems["s_gp"], ctx.wready["w2t"])
                for T in range(64):
                    slot = T % 2
                    if T >= 2:
                        sc.wait_ge(ctx.sems["s_pe"], pe0 + T - 1)
                    wv[T] = ctx.slot_dma(sc.dma_start(
                        out=a["xpool"][:, slot * 1024: slot * 1024 + 512],
                        in_=bv[T, :, :][:, tb * 512:(tb + 1) * 512]), slot)

            @blk.tensor
            def _(t):
                for T in range(64):
                    slot = T % 2
                    psd = a["ps_b"].ap() if T % 2 == 0 else a["ps_c"].ap()
                    ctx.slot_wait(t, slot, wv[T])
                    if T >= 2:
                        t.wait_ge(ctx.sems["s_dve"], dve0 + T - 1)
                    ctx.cinc(t.matmul(psd[:, 0:512], h128,
                                      a["xpool"][:, slot * 1024: slot * 1024 + 512],
                                      start=True, stop=True), "s_pe")

            @blk.vector
            def _(v):
                v.wait_ge(ctx.sems["q2"], ctx.cnt["q2"])
                v.wait_ge(ctx.sems["q3"], ctx.cnt["q3"])
                for T in range(64):
                    psd = a["ps_b"].ap() if T % 2 == 0 else a["ps_c"].ap()
                    v.wait_ge(ctx.sems["s_pe"], pe0 + T + 1)
                    ctx.cinc(v.tensor_scalar_mul(ar[:, T * 512:(T + 1) * 512],
                                                 psd[:, 0:512], FW_SC), "s_dve")

        pe1 = ctx.cnt["s_pe"]
        dve1 = ctx.cnt["s_dve"]
        with nc.Block() as blk:
            @blk.sync
            def _(s):
                for dgrp in range(4):
                    for T in range(64):
                        step = dgrp * 64 + T
                        slot = step % 8
                        if step >= 8:
                            s.wait_ge(ctx.sems["s_pe"], pe1 + step - 7)
                        wv2[step] = ctx.slot_dma(s.dma_start(
                            out=a["wpool"][:, slot * 1024:slot * 1024 + 512],
                            in_=w2[T * 128:(T + 1) * 128, dgrp * 512:(dgrp + 1) * 512]), slot)

            @blk.tensor
            def _(t):
                for dgrp in range(4):
                    if dgrp > 0:
                        t.wait_ge(ctx.sems["s_dve"], dve1 + dgrp * 4)
                    for T in range(64):
                        step = dgrp * 64 + T
                        slot = step % 8
                        ctx.slot_wait(t, slot, wv2[step])
                        for do in range(4):
                            ii = t.matmul(a["ps_a"][:, do * 512:(do + 1) * 512],
                                          a["wpool"][:, slot * 1024 + do * 128:
                                                      slot * 1024 + (do + 1) * 128],
                                          ar[:, T * 512:(T + 1) * 512],
                                          start=(T == 0), stop=(T == 63))
                        ctx.cinc(ii, "s_pe")

            @blk.vector
            def _(v):
                for dgrp in range(4):
                    v.wait_ge(ctx.sems["s_pe"], pe1 + dgrp * 64 + 64)
                    for do in range(4):
                        dt_ = dgrp * 4 + do
                        xt = a["x_fm"][:, dt_ * TOK + tb * 512: dt_ * TOK + (tb + 1) * 512]
                        ctx.cinc(v.tensor_tensor(xt, xt, a["ps_a"][:, do * 512:(do + 1) * 512],
                                                 ALU.add), "s_dve")


def p_store(ctx):
    nc, a = ctx.nc, ctx.a
    arF = a["arena"].ap().bitcast(F32)
    pe0 = ctx.cnt["s_pe"]
    dve0 = ctx.cnt["s_dve"]
    b0, b1 = ctx.cnt["q0"], ctx.cnt["q1"]
    sv = {tt: (b0 if tt % 2 == 0 else b1) + (tt // 2 + 1) * 16 for tt in range(8)}
    with nc.Block() as blk:
        @blk.tensor
        def _(t):
            for tt in range(8):
                for d in range(16):
                    j = tt * 16 + d
                    dst = a["ps_b"][:, (d % 2) * 512:(d % 2) * 512 + 128]
                    if j >= 2:
                        t.wait_ge(ctx.sems["s_dve"], dve0 + j - 1)
                    ctx.cinc(t.transpose(
                        dst, a["x_fm"][:, d * TOK + tt * 128: d * TOK + (tt + 1) * 128],
                        a["identf"][:, :]), "s_pe")

        @blk.vector
        def _(v):
            for tt in range(8):
                oslot = tt % 2
                if tt >= 2:
                    ctx.slot_wait(v, oslot, sv[tt - 2])
                for d in range(16):
                    j = tt * 16 + d
                    src = a["ps_b"][:, (d % 2) * 512:(d % 2) * 512 + 128]
                    v.wait_ge(ctx.sems["s_pe"], pe0 + j + 1)
                    ctx.cinc(v.tensor_copy(
                        arF[:, oslot * 2048 + d * 128: oslot * 2048 + (d + 1) * 128], src),
                        "s_dve")

        @blk.sync
        def _(s):
            for tt in range(8):
                oslot = tt % 2
                s.wait_ge(ctx.sems["s_dve"], dve0 + tt * 16 + 16)
                w = ctx.slot_dma(s.dma_start(
                    out=ctx.out_own[tt * 128:(tt + 1) * 128, :],
                    in_=arF[:, oslot * 2048:(oslot + 1) * 2048]), oslot)
                assert w == sv[tt], (w, sv[tt])
            s.wait_ge(ctx.sems["q0"], sv[6])
            s.wait_ge(ctx.sems["q1"], sv[7])


def build(cfg):
    nc = bass.Bass("TRN2", target_bir_lowering=False, debug=False)
    ctx = Ctx(nc, cfg)

    ctx.x_in = nc.dram_tensor("x_own", [TOK, D], F32, kind="ExternalInput")

    ctx.identf_dram = nc.inline_tensor(np.eye(128, dtype=np.float32), "identf_c")
    consts = np.zeros((128, 768), dtype=ml_dtypes.bfloat16)
    consts[:, 0:128] = np.eye(128, dtype=ml_dtypes.bfloat16)
    consts[:, 128:256] = _hadamard(128).astype(ml_dtypes.bfloat16)
    consts[0:64, 256:320] = _hadamard(64).astype(ml_dtypes.bfloat16)
    consts[:, 320:448] = 1.0
    ctx.consts_dram = nc.inline_tensor(consts, "consts_c")
    ctx.masks_dram = nc.inline_tensor(_masks_np(), "masks_c")
    fconst = np.zeros((128, 136), dtype=np.float32)
    fconst[:, 0:128] = 1.0
    fconst[:, 128] = LN_EPS
    fconst[:, 129] = Q_EPS / 2.0
    ctx.fconst_dram = nc.inline_tensor(fconst, "fconst_c")

    ctx.xn_ag_in = nc.dram_tensor("xn_ag_in", [D, TOK], F16, kind="Internal")
    ctx.xn_ag_out = nc.dram_tensor("xn_ag_out", [8 * D, TOK], F16, kind="Internal")
    ctx.attn_ag_in = nc.dram_tensor("attn_ag_in", [256, 8192], F16, kind="Internal")
    ctx.attn_ag_out = nc.dram_tensor("attn_ag_out", [D, 8192], F16, kind="Internal")
    ctx.kvspill = nc.dram_tensor("kvspill", [768, 4096], F16, kind="Internal")

    ctx.wq_in = [nc.dram_tensor("l%d_qkv_sl" % l, [D, 768], F32, kind="ExternalInput")
                 for l in range(2)]
    ctx.wo_in = [nc.dram_tensor("l%d_wo" % l, [D, D], F32, kind="ExternalInput")
                 for l in range(2)]
    ctx.wq_bf = [nc.dram_tensor("l%d_qkv_bf" % l, [D, 768], F16, kind="Internal")
                 for l in range(2)]
    ctx.wo_bf = [nc.dram_tensor("l%d_wo_bf" % l, [D, D], F16, kind="Internal")
                 for l in range(2)]
    ctx.w1t_in = nc.dram_tensor("l1_w1t", [D, H1], F32, kind="ExternalInput")
    ctx.w3t_in = nc.dram_tensor("l1_w3t", [D, H1], F32, kind="ExternalInput")
    ctx.w2t_in = nc.dram_tensor("l1_w2t", [H1, D], F32, kind="ExternalInput")
    ctx.w1t_bf = nc.dram_tensor("w1t_bf", [D, H1], F16, kind="Internal")
    ctx.w3t_bf = nc.dram_tensor("w3t_bf", [D, H1], F16, kind="Internal")
    ctx.w2t_bf = nc.dram_tensor("w2t_bf", [H1, D], F16, kind="Internal")
    ctx.hid_dram = nc.dram_tensor("hid_dram", [H1, TOK], F16, kind="Internal")
    ctx.hidB_dram = nc.dram_tensor("hidB_dram", [H1, TOK], F16, kind="Internal")
    ctx.out_own = nc.dram_tensor("out_own", [TOK, D], F32, kind="ExternalOutput")
    ctx.cast_list = [
        ("wq0", ctx.wq_in[0], ctx.wq_bf[0]),
        ("wo0", ctx.wo_in[0], ctx.wo_bf[0]),
        ("w1t", ctx.w1t_in, ctx.w1t_bf),
        ("w3t", ctx.w3t_in, ctx.w3t_bf),
        ("wq1", ctx.wq_in[1], ctx.wq_bf[1]),
        ("wo1", ctx.wo_in[1], ctx.wo_bf[1]),
        ("w2t", ctx.w2t_in, ctx.w2t_bf),
    ]
    ctx.ffn0_sl = nc.dram_tensor("l0_ffn_sl", [128, 3 * QSLF], F32, kind="ExternalInput")
    ctx.q_in = [nc.dram_tensor("q_in%d" % m, [128, QSLF], F16, kind="Internal")
                for m in range(3)]
    ctx.q_out = [nc.dram_tensor("q_out%d" % m, [1024, QSLF], F16, kind="Internal")
                 for m in range(3)]
    ctx.gm_in = nc.dram_tensor("gm_in", [1, 3], F32, kind="Internal")
    ctx.gm_out = nc.dram_tensor("gm_out", [8, 3], F32, kind="Internal")

    if cfg.get("dump_xn"):
        ctx.out_xn = nc.dram_tensor("out_xn", [D, TOK], F32, kind="ExternalOutput")
        ctx.out_xng = nc.dram_tensor("out_xng", [8 * D, TOK], F32, kind="ExternalOutput")
    if cfg.get("dump_attn"):
        ctx.out_attng = nc.dram_tensor("out_attng", [D, 8192], F32, kind="ExternalOutput")
    if cfg.get("dump_x"):
        ctx.out_x = nc.dram_tensor("out_x", [D, TOK], F32, kind="ExternalOutput")
    if cfg.get("dump_qkv"):
        ctx.out_qkv = nc.dram_tensor("out_qkv", [384, 4096], F16, kind="ExternalOutput")
    if cfg.get("dump_p"):
        ctx.out_p = nc.dram_tensor("out_p", [512, 2048], F16, kind="ExternalOutput")
    if cfg.get("dump_qw"):
        ctx.out_qw = nc.dram_tensor("out_qw", [1024, QSLF], F16, kind="ExternalOutput")
        ctx.out_hid = nc.dram_tensor("out_hid", [128, 22528], F16, kind="ExternalOutput")

    ctx.a = _alloc(ctx)

    upto = cfg.get("upto", 99)
    p_consts(ctx)
    p_wcast(ctx)
    p_quant1(ctx)
    p_load_x(ctx, ctx.x_in)
    p_layernorm(ctx)
    p_xn_ag(ctx)
    if upto == 66:
        with nc.Block() as blk:
            @blk.gpsimd
            def _(g):
                ctx.cinc(g.collective_compute(
                    "AllGather", ALU.bypass, replica_groups=[list(range(8))],
                    ins=[ctx.attn_ag_in.ap().opt()], outs=[ctx.attn_ag_out.ap().opt()]),
                    "s_cc_at")
                g.wait_ge(ctx.sems["s_cc_at"], 1)
        ctx.stack.close()
        return nc, ctx
    if upto >= 2:
        p_quant2(ctx)
    if upto == 67:
        with nc.Block() as blk:
            @blk.gpsimd
            def _(g):
                g.wait_ge(ctx.sems["s_cc_q"], ctx.q_cc_done)
                ctx.cinc(g.collective_compute(
                    "AllGather", ALU.bypass, replica_groups=[list(range(8))],
                    ins=[ctx.attn_ag_in.ap().opt()], outs=[ctx.attn_ag_out.ap().opt()]),
                    "s_cc_at")
                g.wait_ge(ctx.sems["s_cc_at"], 1)
        ctx.stack.close()
        return nc, ctx
    if cfg.get("upto") == "ln":
        pass
    else:
        if upto < 3:
            pass
        elif cfg.get("dump_p"):
            p_qkv(ctx, 0, 0)
            p_attn_unit(ctx, 0, 0, 0, reload=False)
            with nc.Block() as blk:
                @blk.sync
                def _(s):
                    ctx.dinc(s.dma_start(out=ctx.out_p[0:128, 0:2048], in_=ctx.a["praw"][:, :]), "s_st")
                    ctx.dinc(s.dma_start(out=ctx.out_p[128:256, 0:2048], in_=ctx.a["pmask"][:, :]), "s_st")
                    ctx.dinc(s.dma_start(out=ctx.out_p[256:384, 0:1024].bitcast(F32),
                                         in_=ctx.a["sc_a"][:, :]), "s_st")
                    ctx.dinc(s.dma_start(out=ctx.out_p[384:512, 0:1024].bitcast(F32),
                                         in_=ctx.a["sc_b"][:, :]), "s_st")
                    s.wait_ge(ctx.sems["s_st"], ctx.cnt["s_st"])
            ctx.stack.close()
            return nc, ctx
        def _ccprobe():
            with nc.Block() as blk:
                @blk.gpsimd
                def _(g):
                    ctx.cinc(g.collective_compute(
                        "AllGather", ALU.bypass, replica_groups=[list(range(8))],
                        ins=[ctx.attn_ag_in.ap().opt()], outs=[ctx.attn_ag_out.ap().opt()]),
                        "s_cc_at")
                    g.wait_ge(ctx.sems["s_cc_at"], ctx.cnt["s_cc_at"])
        blist = [] if upto < 3 else ([0] if upto in (3, 68) else [0, 1])
        for b in blist:
            p_qkv(ctx, 0, b)
            if upto == 68:
                _ccprobe()
                ctx.stack.close()
                return nc, ctx
            if upto == 3:
                continue
            if upto == 69 and b == 0:
                p_attn_unit(ctx, 0, 0, 0, reload=False)
                p_attn_unit(ctx, 0, 0, 1, reload=True)
                _ccprobe()
                ctx.stack.close()
                return nc, ctx
            if b == 0 and cfg.get("dump_qkv"):
                with nc.Block() as blk:
                    @blk.sync
                    def _(s):
                        ctx.dinc(s.dma_start(out=ctx.out_qkv[0:128, :], in_=ctx.a["arena"][:, QO:QO + 4096]), "s_st")
                        ctx.dinc(s.dma_start(out=ctx.out_qkv[128:256, :], in_=ctx.a["arena"][:, KO:KO + 4096]), "s_st")
                        ctx.dinc(s.dma_start(out=ctx.out_qkv[256:384, :], in_=ctx.a["arena"][:, VO:VO + 4096]), "s_st")
                        s.wait_ge(ctx.sems["s_st"], ctx.cnt["s_st"])
            p_attn_unit(ctx, 0, b, 0, reload=False)
            p_attn_unit(ctx, 0, b, 1, reload=True)
        if upto == 70:
            _ccprobe()
            ctx.stack.close()
            return nc, ctx
        if upto < 5:
            ctx.stack.close()
            return nc, ctx
        p_attn_ag(ctx)
        if upto == 71:
            with nc.Block() as blk:
                @blk.gpsimd
                def _(g):
                    g.wait_ge(ctx.sems["s_cc_at"], ctx.attn_cc_done)
            ctx.stack.close()
            return nc, ctx
        if upto < 6:
            ctx.stack.close()
            return nc, ctx
        p_ffn0(ctx, 0)
        if cfg.get("dump_qw"):
            with nc.Block() as blk:
                @blk.sync
                def _(s):
                    ctx.dinc(s.dma_start(out=ctx.out_qw[:, :], in_=ctx.q_out[0][:, :]), "s_st")
                    ctx.dinc(s.dma_start(out=ctx.out_hid[:, :],
                                         in_=ctx.a["arena"][:, 0:22528]), "s_st")
                    s.wait_ge(ctx.sems["s_st"], ctx.cnt["s_st"])
        if upto == 65:
            with nc.Block() as blk:
                @blk.gpsimd
                def _(g):
                    g.wait_ge(ctx.sems["s_cc_at"], ctx.attn_cc_done)
        if upto >= 7:
            p_proj(ctx, 0)
        if upto >= 8:
            p_ffn0(ctx, 1)
        if upto >= 9:
            p_layernorm(ctx)
            p_xn_ag(ctx)
            p_ffn1a(ctx)
            for b in range(2):
                p_qkv(ctx, 1, b)
                p_attn_unit(ctx, 1, b, 0, reload=False)
                p_attn_unit(ctx, 1, b, 1, reload=True)
            p_attn_ag(ctx)
            p_fwht_b(ctx)
            p_proj(ctx, 1)
            p_w2a(ctx)
            p_store(ctx)

    if cfg.get("dump_xn"):
        with nc.Block() as blk:
            @blk.gpsimd
            def _(g):
                g.wait_ge(ctx.sems["s_cc"], ctx.xn_cc_done)
                ctx.dinc(g.dma_start(out=ctx.out_xng[:, :], in_=ctx.xn_ag_out[:, :]), "s_st")
                for d in range(16):
                    ctx.dinc(g.dma_start(out=ctx.out_xn[d * 128:(d + 1) * 128, :],
                                         in_=ctx.a["xn"][:, d * TOK:(d + 1) * TOK]), "s_st")
                g.wait_ge(ctx.sems["s_st"], ctx.cnt["s_st"])
    if cfg.get("dump_attn"):
        with nc.Block() as blk:
            @blk.gpsimd
            def _(g):
                g.wait_ge(ctx.sems["s_cc"], ctx.attn_cc_done)
                ctx.dinc(g.dma_start(out=ctx.out_attng[:, :], in_=ctx.attn_ag_out[:, :]), "s_st")
                g.wait_ge(ctx.sems["s_st"], ctx.cnt["s_st"])
    if cfg.get("dump_x"):
        with nc.Block() as blk:
            @blk.sync
            def _(s):
                for d in range(16):
                    ctx.dinc(s.dma_start(out=ctx.out_x[d * 128:(d + 1) * 128, :],
                                         in_=ctx.a["x_fm"][:, d * TOK:(d + 1) * TOK]), "s_st")
                s.wait_ge(ctx.sems["s_st"], ctx.cnt["s_st"])

    ctx.stack.close()
    return nc, ctx


def make_in_maps(inputs):
    x = np.asarray(inputs["L0_attn_x"])
    maps = []
    for c in range(NCORE):
        b, k = c // 4, c % 4
        blocks = [k, 7 - k]
        wqkv = np.asarray(inputs["L0_attn_w_qkv"])
        l0sl = np.concatenate([wqkv[256 * c:256 * (c + 1)],
                               wqkv[2048 + 256 * c:2048 + 256 * (c + 1)],
                               wqkv[4096 + 256 * c:4096 + 256 * (c + 1)]], axis=0)
        l1sl = np.concatenate([np.asarray(inputs["L1_attn_Wq"])[256 * c:256 * (c + 1)],
                               np.asarray(inputs["L1_attn_Wk"])[256 * c:256 * (c + 1)],
                               np.asarray(inputs["L1_attn_Wv"])[256 * c:256 * (c + 1)]], axis=0)
        QSL = H0 * D // NCORE
        fsl = [np.asarray(inputs["L0_ffn_w%d" % (m + 1)]).T.reshape(-1)[c * QSL:(c + 1) * QSL]
               .reshape(128, QSLF) for m in range(3)]
        m = {
            "x_own": np.ascontiguousarray(
                np.concatenate([x[b, 512 * g:512 * (g + 1)] for g in blocks], axis=0)),
            "l0_qkv_sl": np.ascontiguousarray(l0sl.T),
            "l1_qkv_sl": np.ascontiguousarray(l1sl.T),
            "l0_wo": np.ascontiguousarray(np.asarray(inputs["L0_attn_w_out"]).T),
            "l1_wo": np.ascontiguousarray(np.asarray(inputs["L1_attn_Wo"]).T),
            "l0_ffn_sl": np.ascontiguousarray(np.concatenate(fsl, axis=1)),
            "l1_w1t": np.ascontiguousarray(np.asarray(inputs["L1_ffn_W1"]).T),
            "l1_w3t": np.ascontiguousarray(np.asarray(inputs["L1_ffn_W3"]).T),
            "l1_w2t": np.ascontiguousarray(np.asarray(inputs["L1_ffn_W2"]).T),
        }
        maps.append(m)
    return maps


def kernel(**inputs):
    nc, ctx = build({})
    maps = make_in_maps(inputs)
    res = run_bass_kernel_spmd(nc, maps, core_ids=list(range(NCORE)))
    out = np.empty((B, S, D), dtype=np.float32)
    for c in range(NCORE):
        b, k = c // 4, c % 4
        oc = res.results[c]["out_own"]
        for s_i, g in enumerate([k, 7 - k]):
            out[b, 512 * g:512 * (g + 1)] = oc[512 * s_i:512 * (s_i + 1)]
    return out


# version 02
# speedup vs baseline: 1.0034x; 1.0034x over previous
import numpy as np
import ml_dtypes
from contextlib import ExitStack

import concourse.bass as bass
import concourse.mybir as mybir
from concourse.bass_utils import run_bass_kernel_spmd

F32 = mybir.dt.float32
F16 = mybir.dt.bfloat16
AF = mybir.ActivationFunctionType
ALU = mybir.AluOpType
AX = mybir.AxisListType

B, S, D = 2, 4096, 2048
NH, DH, C = 16, 128, 512
TOK = 1024
NCORE = 8
HPC = 2
SCALE = float(128) ** -0.5
LN_EPS = 1e-5
Q_EPS = 1e-8
H0 = 5632
H1 = 8192
QSLF = (H0 * D // NCORE) // 128
FW_SC = float(H1) ** -0.5


def _owner(g):
    return (min(g, 7 - g), 0 if g < 4 else 1)


def _masks_np():
    p = np.arange(128)[:, None]
    q = np.arange(512)[None, :]
    cols = []
    for o in (0, 128, 256, 384):
        cols.append((p + o >= q).astype(ml_dtypes.bfloat16))
    for o in (0, 128, 256, 384):
        cols.append((p + o <= q).astype(ml_dtypes.bfloat16))
    return np.concatenate(cols, axis=1)


def _hadamard(n):
    H = np.array([[1.0]], dtype=np.float32)
    while H.shape[0] < n:
        H = np.block([[H, H], [H, -H]])
    return H


class Ctx:
    def __init__(self, nc, cfg):
        self.nc = nc
        self.cfg = cfg
        self.cnt = {}
        self.stack = ExitStack()
        self.sems = {}

    def sem(self, name):
        if name not in self.sems:
            self.sems[name] = self.stack.enter_context(self.nc.semaphore(name))
            self.cnt[name] = 0
        return self.sems[name]

    def dinc(self, inst, name):
        inst.then_inc(self.sems[name], 16)
        self.cnt[name] += 16
        return self.cnt[name]

    def slot_dma(self, inst, slot):
        return self.dinc(inst, "q%d" % slot)

    def slot_wait(self, eng, slot, val):
        eng.wait_ge(self.sems["q%d" % slot], val)

    def cinc(self, inst, name, val=1):
        inst.then_inc(self.sems[name], val)
        self.cnt[name] += val
        return self.cnt[name]


def _alloc(ctx):
    nc = ctx.nc
    a = {}
    a["x_fm"] = nc.alloc_sbuf_tensor("x_fm", [128, 16 * TOK], F32)
    a["xn"] = nc.alloc_sbuf_tensor("xn", [128, 16 * TOK], F16)
    a["arena"] = nc.alloc_sbuf_tensor("arena", [128, 32768], F16)
    a["wpool"] = nc.alloc_sbuf_tensor("wpool", [128, 8192], F16)
    a["xpool"] = nc.alloc_sbuf_tensor("xpool", [128, 2048], F16)
    a["praw"] = nc.alloc_sbuf_tensor("praw", [128, 2048], F16)
    a["pmask"] = nc.alloc_sbuf_tensor("pmask", [128, 2048], F16)
    a["sc_a"] = nc.alloc_sbuf_tensor("sc_a", [128, 512], F32)
    a["sc_b"] = nc.alloc_sbuf_tensor("sc_b", [128, 512], F32)
    a["scr"] = nc.alloc_sbuf_tensor("scr", [128, 1024], F16)
    a["bc_sb"] = nc.alloc_sbuf_tensor("bc_sb", [128, 512], F16)
    a["hidout"] = nc.alloc_sbuf_tensor("hidout", [128, 2048], F16)
    a["vtmp"] = nc.alloc_sbuf_tensor("vtmp", [128, 1024], F16)
    a["consts"] = nc.alloc_sbuf_tensor("consts", [128, 768], F16)
    a["identf"] = nc.alloc_sbuf_tensor("identf", [128, 128], F32)
    a["fconst"] = nc.alloc_sbuf_tensor("fconst", [128, 136], F32)
    print(f"[build] sbuf remaining: {nc.sbuf_bytes_remaining} B/part")
    a["ps_a"] = nc.alloc_psum_tensor("ps_a", [128, 2048], F32)
    a["ps_b"] = nc.alloc_psum_tensor("ps_b", [128, 1024], F32)
    a["ps_c"] = nc.alloc_psum_tensor("ps_c", [128, 1024], F32)
    return a


def p_consts(ctx):
    nc, a = ctx.nc, ctx.a
    ctx.sem("s_ld"); ctx.sem("s_sc"); ctx.sem("s_gp"); ctx.sem("s_st")
    ctx.sem("s_pe"); ctx.sem("s_act"); ctx.sem("s_dve"); ctx.sem("s_cc")
    ctx.sem("s_cc_q"); ctx.sem("s_cc_at"); ctx.sem("s_sp")
    for i in range(8):
        ctx.sem("q%d" % i)
    with nc.Block() as blk:
        @blk.sync
        def _(s):
            ctx.dinc(s.dma_start(out=a["identf"][:, :], in_=ctx.identf_dram[:, :]), "s_ld")
            ctx.dinc(s.dma_start(out=a["consts"][:, :], in_=ctx.consts_dram[:, :]), "s_ld")
            w = ctx.dinc(s.dma_start(out=a["fconst"][:, :], in_=ctx.fconst_dram[:, :]), "s_ld")
            s.wait_ge(ctx.sems["s_ld"], w)


def p_load_x(ctx, x_dram):
    nc, a = ctx.nc, ctx.a
    arena_f32 = a["arena"].ap().bitcast(F32)

    pe0 = ctx.cnt["s_pe"]
    dve0 = ctx.cnt["s_dve"]
    wv = {}

    with nc.Block() as blk:
        @blk.sync
        def _(s):
            for tt in range(8):
                buf = tt % 2
                if tt >= 2:
                    s.wait_ge(ctx.sems["s_pe"], pe0 + (tt - 1) * 16)
                wv[tt] = ctx.slot_dma(s.dma_start(
                    out=arena_f32[:, buf * 2048:(buf + 1) * 2048],
                    in_=x_dram[tt * 128:(tt + 1) * 128, :]), buf)

        @blk.tensor
        def _(t):
            for tt in range(8):
                buf = tt % 2
                ctx.slot_wait(t, buf, wv[tt])
                for d in range(16):
                    j = tt * 16 + d
                    dst = a["ps_b"][:, (d % 2) * 512:(d % 2) * 512 + 128]
                    if j >= 2:
                        t.wait_ge(ctx.sems["s_dve"], dve0 + j - 1)
                    ctx.cinc(t.transpose(
                        dst,
                        arena_f32[:, buf * 2048 + d * 128: buf * 2048 + (d + 1) * 128],
                        a["identf"][:, :]), "s_pe")

        @blk.vector
        def _(v):
            for tt in range(8):
                for d in range(16):
                    j = tt * 16 + d
                    src = a["ps_b"][:, (d % 2) * 512:(d % 2) * 512 + 128]
                    v.wait_ge(ctx.sems["s_pe"], pe0 + j + 1)
                    ctx.cinc(v.tensor_copy(
                        a["x_fm"][:, d * TOK + tt * 128: d * TOK + (tt + 1) * 128], src),
                        "s_dve")


def p_layernorm(ctx):
    nc, a = ctx.nc, ctx.a
    ones_col = a["consts"][:, 320:321]
    x_fm = a["x_fm"]
    act0 = ctx.cnt["s_act"]
    pe0 = ctx.cnt["s_pe"]

    bufs = [a["scr"], a["hidout"]]
    with nc.Block() as blk:
        @blk.scalar
        def _(sc):
            for j in range(32):
                d, tb = j // 2, j % 2
                bf = bufs[j % 2]
                src = x_fm[:, d * TOK + tb * 512: d * TOK + (tb + 1) * 512]
                if j >= 2:
                    sc.wait_ge(ctx.sems["s_pe"], pe0 + j - 1)
                sc.copy(bf[:, 0:512], src)
                ctx.cinc(sc.activation(bf[:, 512:1024], src, AF.Square), "s_act")

        @blk.tensor
        def _(t):
            for j in range(32):
                d, tb = j // 2, j % 2
                bf = bufs[j % 2]
                t.wait_ge(ctx.sems["s_act"], act0 + j + 1)
                first, last = (d == 0), (d == 15)
                t.matmul(a["ps_c"][0:1, tb * 512:(tb + 1) * 512], ones_col,
                         bf[:, 0:512], start=first, stop=last)
                ctx.cinc(t.matmul(a["ps_b"][0:1, tb * 512:(tb + 1) * 512], ones_col,
                                  bf[:, 512:1024], start=first, stop=last), "s_pe")
    pe_stats = ctx.cnt["s_pe"]

    xp32 = a["xpool"].ap().bitcast(F32)
    for tb in range(2):
        r_mean = a["sc_a"][0:1, 0:512]
        r_rstd = xp32[0:1, 0:512]
        r_negmr = xp32[0:1, 512:1024]
        with nc.Block() as blk:
            @blk.vector
            def _(v):
                v.wait_ge(ctx.sems["s_pe"], pe_stats)
                v.tensor_scalar_mul(r_mean, a["ps_c"][0:1, tb * 512:(tb + 1) * 512], 1.0 / D)
                ctx.cinc(v.tensor_scalar_mul(r_rstd, a["ps_b"][0:1, tb * 512:(tb + 1) * 512],
                                             1.0 / D), "s_dve")
            d1 = ctx.cnt["s_dve"]

            @blk.scalar
            def _(sc):
                sc.wait_ge(ctx.sems["s_dve"], d1)
                ctx.cinc(sc.activation(r_negmr, r_mean, AF.Square), "s_act")
            a1 = ctx.cnt["s_act"]

            @blk.vector
            def _(v):
                v.wait_ge(ctx.sems["s_act"], a1)
                ctx.cinc(v.tensor_sub(r_rstd, r_rstd, r_negmr), "s_dve")
            d2 = ctx.cnt["s_dve"]

            @blk.scalar
            def _(sc):
                sc.wait_ge(ctx.sems["s_dve"], d2)
                sc.activation(r_rstd, r_rstd, AF.Ln, bias=a["fconst"][0:1, 128:129])
                ctx.cinc(sc.activation(r_rstd, r_rstd, AF.Exp, scale=-0.5), "s_act")
            a2 = ctx.cnt["s_act"]

            @blk.vector
            def _(v):
                v.wait_ge(ctx.sems["s_act"], a2)
                v.tensor_mul(r_negmr, r_mean, r_rstd)
                v.tensor_scalar_mul(r_negmr, r_negmr, -1.0)

        with nc.Block() as blk:
            @blk.tensor
            def _(t):
                t.matmul(a["ps_a"][:, 0:512], a["fconst"][0:1, 0:128],
                         r_rstd, start=True, stop=True)
                ctx.cinc(t.matmul(a["ps_a"][:, 512:1024], a["fconst"][0:1, 0:128],
                                  r_negmr, start=True, stop=True), "s_pe")
            pe_bc = ctx.cnt["s_pe"]

            @blk.vector
            def _(v):
                v.wait_ge(ctx.sems["s_pe"], pe_bc)
                for d in range(16):
                    xt = x_fm[:, d * TOK + tb * 512: d * TOK + (tb + 1) * 512]
                    v.tensor_tensor(a["sc_b"][:, :], xt, a["ps_a"][:, 0:512], ALU.mult)
                    v.tensor_tensor(a["xn"][:, d * TOK + tb * 512: d * TOK + (tb + 1) * 512],
                                    a["sc_b"][:, :], a["ps_a"][:, 512:1024], ALU.add)


def p_xn_ag(ctx):
    nc, a = ctx.nc, ctx.a
    with nc.Block() as blk:
        @blk.scalar
        def _(sc):
            for d in range(16):
                ctx.dinc(sc.dma_start(out=ctx.xn_ag_in[d * 128:(d + 1) * 128, :],
                                      in_=a["xn"][:, d * TOK:(d + 1) * TOK]), "s_sc")

        sc_done = ctx.cnt["s_sc"]

        @blk.gpsimd
        def _(g):
            g.wait_ge(ctx.sems["s_sc"], sc_done)
            ctx.cinc(g.collective_compute(
                "AllGather", ALU.bypass,
                replica_groups=[list(range(8))],
                ins=[ctx.xn_ag_in.ap().opt()], outs=[ctx.xn_ag_out.ap().opt()]), "s_cc")
    ctx.xn_cc_done = ctx.cnt["s_cc"]



QO, KO, VO, MO, WLO = 0, 4096, 8192, 12288, 16384
RHS = 22528


def p_wcast(ctx):
    nc = ctx.nc
    ctx.wready = {}
    with nc.Block() as blk:
        @blk.gpsimd
        def _(g):
            for name, src_t, dst_t in ctx.cast_list:
                if ctx.wready:
                    g.wait_ge(ctx.sems["s_gp"], ctx.cnt["s_gp"])
                ctx.wready[name] = ctx.dinc(
                    g.dma_start(out=dst_t.ap(), in_=src_t.ap()), "s_gp")


def p_qkv(ctx, layer, b):
    nc, a = ctx.nc, ctx.a
    ar = a["arena"]
    wsl = ctx.wq_bf[layer]
    sp = ctx.kvspill

    with nc.Block() as blk:
        @blk.sync
        def _(s):
            s.wait_ge(ctx.sems["s_gp"], ctx.wready["wq%d" % layer])
            ctx.mask_ld_done = ctx.dinc(
                s.dma_start(out=ar[:, MO:MO + 4096], in_=ctx.masks_dram[:, :]), "s_ld")
            for d in range(16):
                ctx.dinc(s.dma_start(
                    out=ar[:, WLO + d * 768: WLO + (d + 1) * 768],
                    in_=wsl[d * 128:(d + 1) * 128, 0:768]), "s_ld")
    ld_batch = ctx.cnt["s_ld"]
    ctx.qkv_ld_batch = ld_batch

    xv = ctx.xn_ag_out.ap().rearrange("(r p) n -> p r n", p=128)
    pe0 = ctx.cnt["s_pe"]
    dve0 = ctx.cnt["s_dve"]
    sp0 = ctx.cnt["s_sp"]
    wv = {}
    with nc.Block() as blk:
        @blk.scalar
        def _(sc):
            sc.wait_ge(ctx.sems["s_cc"], ctx.xn_cc_done)
            for g in range(8):
                r, sl = _owner(g)
                rr = 16 * (4 * b + r)
                for d in range(16):
                    idx = g * 16 + d
                    slot = idx % 4
                    if idx >= 4:
                        gp, dp = (idx - 4) // 16, (idx - 4) % 16
                        sc.wait_ge(ctx.sems["s_pe"], pe0 + gp * 24 + dp + 1)
                    wv[idx] = ctx.slot_dma(sc.dma_start(
                        out=a["xpool"][:, slot * 512:(slot + 1) * 512],
                        in_=xv[:, rr + d, sl * 512:(sl + 1) * 512]), slot)

        @blk.tensor
        def _(t):
            t.wait_ge(ctx.sems["s_ld"], ld_batch)
            for g in range(8):
                for d in range(16):
                    idx = g * 16 + d
                    slot = idx % 4
                    ctx.slot_wait(t, slot, wv[idx])
                    if d == 0 and g > 0:
                        t.wait_ge(ctx.sems["s_dve"], dve0 + g * 8)
                    rhs = a["xpool"][:, slot * 512:(slot + 1) * 512]
                    first, last = (d == 0), (d == 15)
                    w = ar[:, WLO + d * 768:WLO + (d + 1) * 768]
                    t.matmul(a["ps_a"][:, 0:512], w[:, 0:128], rhs, start=first, stop=last)
                    t.matmul(a["ps_a"][:, 512:1024], w[:, 128:256], rhs, start=first, stop=last)
                    t.matmul(a["ps_a"][:, 1024:1536], w[:, 256:384], rhs, start=first, stop=last)
                    t.matmul(a["ps_a"][:, 1536:2048], w[:, 384:512], rhs, start=first, stop=last)
                    t.matmul(a["ps_b"][:, 0:512], w[:, 512:640], rhs, start=first, stop=last)
                    ctx.cinc(t.matmul(a["ps_b"][:, 512:1024], w[:, 640:768], rhs,
                                      start=first, stop=last), "s_pe")
                for j in range(4):
                    if j == 0:
                        t.wait_ge(ctx.sems["s_dve"], dve0 + g * 8 + 5)
                    ctx.cinc(t.transpose(a["ps_c"][:, j * 128:(j + 1) * 128],
                                         a["sc_a"][:, j * 128:(j + 1) * 128],
                                         a["identf"][:, :]), "s_pe")
                for j in range(4):
                    if j == 0:
                        t.wait_ge(ctx.sems["s_dve"], dve0 + g * 8 + 6)
                    ctx.cinc(t.transpose(a["ps_c"][:, 512 + j * 128: 512 + (j + 1) * 128],
                                         a["sc_b"][:, j * 128:(j + 1) * 128],
                                         a["identf"][:, :]), "s_pe")

        @blk.vector
        def _(v):
            for g in range(8):
                v.wait_ge(ctx.sems["s_pe"], pe0 + g * 24 + 16)
                if g > 0:
                    v.wait_ge(ctx.sems["s_sp"], sp0 + g * 48)
                ctx.cinc(v.tensor_copy(ar[:, QO + g * 512:QO + (g + 1) * 512],
                                       a["ps_a"][:, 0:512]), "s_dve")
                ctx.cinc(v.tensor_copy(ar[:, KO + g * 512:KO + (g + 1) * 512],
                                       a["ps_a"][:, 1024:1536]), "s_dve")
                ctx.cinc(v.tensor_copy(a["scr"][:, 0:512], a["ps_a"][:, 512:1024]), "s_dve")
                ctx.cinc(v.tensor_copy(a["scr"][:, 512:1024], a["ps_a"][:, 1536:2048]), "s_dve")
                ctx.cinc(v.tensor_copy(a["sc_a"][:, :], a["ps_b"][:, 0:512]), "s_dve")
                ctx.cinc(v.tensor_copy(a["sc_b"][:, :], a["ps_b"][:, 512:1024]), "s_dve")
                v.wait_ge(ctx.sems["s_pe"], pe0 + g * 24 + 20)
                ctx.cinc(v.tensor_copy(ar[:, VO + g * 512:VO + (g + 1) * 512],
                                       a["ps_c"][:, 0:512]), "s_dve")
                v.wait_ge(ctx.sems["s_pe"], pe0 + g * 24 + 24)
                ctx.cinc(v.tensor_copy(a["vtmp"][:, 0:512], a["ps_c"][:, 512:1024]), "s_dve")

        @blk.sync
        def _(s2):
            for g in range(8):
                s2.wait_ge(ctx.sems["s_dve"], dve0 + g * 8 + 8)
                ctx.dinc(s2.dma_start(out=sp[384 * b + 0:384 * b + 128, g * 512:(g + 1) * 512],
                                      in_=a["scr"][:, 0:512]), "s_sp")
                ctx.dinc(s2.dma_start(out=sp[384 * b + 128:384 * b + 256, g * 512:(g + 1) * 512],
                                      in_=a["scr"][:, 512:1024]), "s_sp")
                ctx.dinc(s2.dma_start(out=sp[384 * b + 256:384 * b + 384, g * 512:(g + 1) * 512],
                                      in_=a["vtmp"][:, 0:512]), "s_sp")


def p_quant1(ctx):
    nc, a = ctx.nc, ctx.a
    arF = a["arena"].ap().bitcast(F32)
    vF = a["vtmp"].ap().bitcast(F32)
    with nc.Block() as blk:
        dve0 = ctx.cnt["s_dve"]
        wv = {}

        @blk.gpsimd
        def _(g):
            for m in range(3):
                for ch in range(2):
                    i = m * 2 + ch
                    if i >= 2:
                        g.wait_ge(ctx.sems["s_dve"], dve0 + i - 1)
                    wv[i] = ctx.slot_dma(g.dma_start(
                        out=arF[:, (i % 2) * 5632:(i % 2) * 5632 + 5632],
                        in_=ctx.ffn0_sl[:, m * QSLF + ch * 5632: m * QSLF + (ch + 1) * 5632]),
                        i % 2)

        @blk.vector
        def _(v):
            for m in range(3):
                for ch in range(2):
                    i = m * 2 + ch
                    ctx.slot_wait(v, i % 2, wv[i])
                    ctx.cinc(v.tensor_reduce(vF[:, i:i + 1],
                                             arF[:, (i % 2) * 5632:(i % 2) * 5632 + 5632],
                                             AX.X, ALU.add, apply_absolute_value=True), "s_dve")
        dve_red = ctx.cnt["s_dve"]

        @blk.tensor
        def _(t):
            t.wait_ge(ctx.sems["s_dve"], dve_red)
            ctx.cinc(t.matmul(a["ps_c"][0:1, 0:6], a["fconst"][:, 0:1], vF[:, 0:6],
                              start=True, stop=True), "s_pe")
        pe_s = ctx.cnt["s_pe"]

        @blk.vector
        def _(v):
            v.wait_ge(ctx.sems["s_pe"], pe_s)
            ctx.cinc(v.tensor_reduce(
                a["xpool"].ap().bitcast(F32)[0:1, 0:3],
                a["ps_c"][0:1, 0:6].rearrange("p (m two) -> p m two", two=2),
                AX.X, ALU.add), "s_dve")
        dve_c = ctx.cnt["s_dve"]

        @blk.scalar
        def _(sc):
            sc.wait_ge(ctx.sems["s_dve"], dve_c)
            ctx.dinc(sc.dma_start(out=ctx.gm_in[:, :],
                                  in_=a["xpool"].ap().bitcast(F32)[0:1, 0:3]), "s_sc")
        sc_w = ctx.cnt["s_sc"]

        @blk.gpsimd
        def _(g):
            g.wait_ge(ctx.sems["s_sc"], sc_w)
            ctx.cinc(g.collective_compute(
                "AllGather", ALU.bypass, replica_groups=[list(range(8))],
                ins=[ctx.gm_in.ap().opt()], outs=[ctx.gm_out.ap().opt()]), "s_cc_q")
    ctx.gm_cc_done = ctx.cnt["s_cc_q"]


def p_quant2(ctx):
    nc, a = ctx.nc, ctx.a
    vF = a["vtmp"].ap().bitcast(F32)
    wF = a["wpool"].ap().bitcast(F32)
    N = float(H0 * D)
    with nc.Block() as blk:
        @blk.gpsimd
        def _(g):
            g.wait_ge(ctx.sems["s_cc_q"], ctx.gm_cc_done)
            ctx.dinc(g.dma_start(out=a["sc_a"][0:8, 0:3], in_=ctx.gm_out[:, :]), "s_gp")
        gp_l = ctx.cnt["s_gp"]

        @blk.tensor
        def _(t):
            t.wait_ge(ctx.sems["s_gp"], gp_l)
            ctx.cinc(t.matmul(a["ps_c"][0:1, 0:3], a["fconst"][0:8, 0:1],
                              a["sc_a"][0:8, 0:3], start=True, stop=True), "s_pe")
        pe_t = ctx.cnt["s_pe"]

        @blk.vector
        def _(v):
            v.wait_ge(ctx.sems["s_pe"], pe_t)
            ctx.cinc(v.tensor_scalar(a["xpool"].ap().bitcast(F32)[0:1, 4:7],
                                     a["ps_c"][0:1, 0:3], 1.0 / (2.0 * N), Q_EPS / 2.0,
                                     ALU.mult, ALU.add), "s_dve")
        act_t = ctx.cnt["s_dve"]

        @blk.tensor
        def _(t):
            t.wait_ge(ctx.sems["s_dve"], act_t)
            for m in range(3):
                ctx.cinc(t.matmul(a["ps_a"][:, m * 512:m * 512 + 1],
                                  a["fconst"][0:1, 0:128],
                                  a["xpool"].ap().bitcast(F32)[0:1, 4 + m:5 + m],
                                  start=True, stop=True), "s_pe")
        pe_b = ctx.cnt["s_pe"]

        @blk.vector
        def _(v):
            v.wait_ge(ctx.sems["s_pe"], pe_b)
            for m in range(3):
                v.tensor_copy(vF[:, 8 + m:9 + m], a["ps_a"][:, m * 512:m * 512 + 1])
                ctx.cinc(v.tensor_scalar_mul(vF[:, 12 + m:13 + m],
                                             a["ps_a"][:, m * 512:m * 512 + 1], -1.0), "s_dve")
    thr_done = ctx.cnt["s_dve"]

    dve0 = ctx.cnt["s_dve"]
    wv = {}
    sv = {}
    with nc.Block() as blk:
        @blk.sync
        def _(s):
            for m in range(3):
                for ch in range(11):
                    i = m * 11 + ch
                    if i >= 2:
                        s.wait_ge(ctx.sems["s_dve"], dve0 + (i - 2) * 3 + 3)
                    wv[i] = ctx.slot_dma(s.dma_start(
                        out=wF[:, (i % 2) * 1024:(i % 2) * 1024 + 1024],
                        in_=ctx.ffn0_sl[:, m * QSLF + ch * 1024: m * QSLF + (ch + 1) * 1024]),
                        i % 2)

        @blk.scalar
        def _(sc):
            for m in range(3):
                for ch in range(11):
                    i = m * 11 + ch
                    sc.wait_ge(ctx.sems["s_dve"], dve0 + i * 3 + 3)
                    sv[i] = ctx.slot_dma(sc.dma_start(
                        out=ctx.q_in[m][:, ch * 1024:(ch + 1) * 1024],
                        in_=a["hidout"][:, (i % 2) * 1024:(i % 2) * 1024 + 1024]), 2 + i % 2)
        sc_all = [None]

        @blk.vector
        def _(v):
            v.wait_ge(ctx.sems["s_dve"], thr_done)
            for m in range(3):
                for ch in range(11):
                    i = m * 11 + ch
                    ctx.slot_wait(v, i % 2, wv[i])
                    wbuf = wF[:, (i % 2) * 1024:(i % 2) * 1024 + 1024]
                    if i >= 2:
                        ctx.slot_wait(v, 2 + i % 2, sv[i - 2])
                    v.tensor_scalar(a["praw"][:, 0:1024], wbuf, vF[:, 8 + m:9 + m], None,
                                    ALU.is_gt)
                    v.tensor_scalar(a["pmask"][:, 0:1024], wbuf, vF[:, 12 + m:13 + m], None,
                                    ALU.is_lt)
                    ctx.cinc(v.tensor_sub(a["hidout"][:, (i % 2) * 1024:(i % 2) * 1024 + 1024],
                                          a["praw"][:, 0:1024], a["pmask"][:, 0:1024]),
                             "s_dve", 3)

        @blk.gpsimd
        def _(g):
            g.wait_ge(ctx.sems["q3"], sv[31])
            g.wait_ge(ctx.sems["q2"], sv[32])
            for m in range(3):
                ctx.cinc(g.collective_compute(
                    "AllGather", ALU.bypass, replica_groups=[list(range(8))],
                    ins=[ctx.q_in[m].ap().opt()], outs=[ctx.q_out[m].ap().opt()]), "s_cc_q")
    ctx.q_cc_done = ctx.cnt["s_cc_q"]


def p_ffn0(ctx, hf):
    nc, a = ctx.nc, ctx.a
    ar = a["arena"]
    q1 = ctx.q_out[0].ap().rearrange("p n -> (p n)").rearrange("(d h) -> d h", h=H0)
    q2 = ctx.q_out[1].ap().rearrange("p n -> (p n)").rearrange("(d h) -> d h", h=H0)
    q3 = ctx.q_out[2].ap().rearrange("p n -> (p n)").rearrange("(h d) -> h d", d=D)

    hts = [hf * 22 + i for i in range(22)]
    pe0 = ctx.cnt["s_pe"]
    act0 = ctx.cnt["s_act"]
    dve0 = ctx.cnt["s_dve"]
    st3_pre = ctx.cnt["q3"]
    st7_pre = ctx.cnt["q7"]
    wv = {}
    sgm = [a["scr"].ap()[:, 0:1024], a["hidout"].ap()[:, 0:1024]]
    with nc.Block() as blk:
        @blk.sync
        def _(s):
            s.wait_ge(ctx.sems["s_cc_q"], ctx.q_cc_done)
            for i, ht in enumerate(hts):
                for dh in range(2):
                    step = i * 2 + dh
                    slot = step % 4
                    if step >= 4:
                        s.wait_ge(ctx.sems["s_pe"], pe0 + step - 3)
                    for mi, qm in enumerate((q1, q2)):
                        src = qm.rearrange("(dt p) h -> p dt h", p=128)
                        dst = a["wpool"][:, slot * 2048 + mi * 1024:
                                         slot * 2048 + (mi + 1) * 1024]
                        wv[step] = ctx.slot_dma(s.dma_start(
                            out=dst.rearrange("p (a b) -> p a b", b=128),
                            in_=src[:, dh * 8:(dh + 1) * 8, ht * 128:(ht + 1) * 128]), slot)

        @blk.tensor
        def _(t):
            for i, ht in enumerate(hts):
                psx1 = a["ps_a"].ap()[:, 0:1024] if i % 2 == 0 else a["ps_b"].ap()[:, :]
                psx2 = a["ps_a"].ap()[:, 1024:2048] if i % 2 == 0 else a["ps_c"].ap()[:, :]
                if i >= 2:
                    t.wait_ge(ctx.sems["s_dve"], dve0 + i - 1)
                for dh in range(2):
                    step = i * 2 + dh
                    slot = step % 4
                    ctx.slot_wait(t, slot, wv[step])
                    for mi in range(2):
                        psd = psx1 if mi == 0 else psx2
                        for d8 in range(8):
                            d = dh * 8 + d8
                            w = a["wpool"][:, slot * 2048 + mi * 1024 + d8 * 128:
                                           slot * 2048 + mi * 1024 + (d8 + 1) * 128]
                            for th in range(2):
                                ii = t.matmul(psd[:, th * 512:(th + 1) * 512], w,
                                              a["xn"][:, d * TOK + th * 512: d * TOK + (th + 1) * 512],
                                              start=(d == 0), stop=(d == 15))
                    ctx.cinc(ii, "s_pe")

        @blk.scalar
        def _(sc):
            sc.wait_ge(ctx.sems["q3"], st3_pre)
            sc.wait_ge(ctx.sems["q7"], st7_pre)
            for i, ht in enumerate(hts):
                psx1 = a["ps_a"].ap()[:, 0:1024] if i % 2 == 0 else a["ps_b"].ap()[:, :]
                sc.wait_ge(ctx.sems["s_pe"], pe0 + i * 2 + 2)
                if i >= 2:
                    sc.wait_ge(ctx.sems["s_dve"], dve0 + i - 1)
                ctx.cinc(sc.activation(sgm[i % 2], psx1, AF.Sigmoid), "s_act")

        @blk.vector
        def _(v):
            for i, ht in enumerate(hts):
                psx2 = a["ps_a"].ap()[:, 1024:2048] if i % 2 == 0 else a["ps_c"].ap()[:, :]
                v.wait_ge(ctx.sems["s_act"], act0 + i + 1)
                ctx.cinc(v.tensor_tensor(ar[:, i * 1024:(i + 1) * 1024], sgm[i % 2],
                                         psx2, ALU.mult), "s_dve")

    pe1 = ctx.cnt["s_pe"]
    dve1 = ctx.cnt["s_dve"]
    wv3 = {}
    with nc.Block() as blk:
        @blk.sync
        def _(s):
            for dgrp in range(4):
                for i, ht in enumerate(hts):
                    step = dgrp * 22 + i
                    slot = step % 8
                    if step >= 8:
                        s.wait_ge(ctx.sems["s_pe"], pe1 + step - 7)
                    wv3[step] = ctx.slot_dma(s.dma_start(
                        out=a["wpool"][:, slot * 1024:slot * 1024 + 512],
                        in_=q3[ht * 128:(ht + 1) * 128, dgrp * 512:(dgrp + 1) * 512]), slot)

        @blk.tensor
        def _(t):
            for dgrp in range(4):
                if dgrp > 0:
                    t.wait_ge(ctx.sems["s_dve"], dve1 + dgrp * 8)
                for i, ht in enumerate(hts):
                    step = dgrp * 22 + i
                    slot = step % 8
                    ctx.slot_wait(t, slot, wv3[step])
                    for do in range(4):
                        w = a["wpool"][:, slot * 1024 + do * 128: slot * 1024 + (do + 1) * 128]
                        for th in range(2):
                            psd = (a["ps_a"].ap()[:, do * 512:(do + 1) * 512] if th == 0 else
                                   (a["ps_b"].ap()[:, do % 2 * 512:do % 2 * 512 + 512] if do < 2 else
                                    a["ps_c"].ap()[:, (do - 2) % 2 * 512:(do - 2) % 2 * 512 + 512]))
                            ii = t.matmul(psd, w,
                                          ar[:, i * 1024 + th * 512: i * 1024 + (th + 1) * 512],
                                          start=(i == 0), stop=(i == 21))
                    ctx.cinc(ii, "s_pe")

        @blk.vector
        def _(v):
            for dgrp in range(4):
                v.wait_ge(ctx.sems["s_pe"], pe1 + dgrp * 22 + 22)
                for do in range(4):
                    dt_ = dgrp * 4 + do
                    for th in range(2):
                        psd = (a["ps_a"].ap()[:, do * 512:(do + 1) * 512] if th == 0 else
                               (a["ps_b"].ap()[:, do % 2 * 512:do % 2 * 512 + 512] if do < 2 else
                                a["ps_c"].ap()[:, (do - 2) % 2 * 512:(do - 2) % 2 * 512 + 512]))
                        xt = a["x_fm"][:, dt_ * TOK + th * 512: dt_ * TOK + (th + 1) * 512]
                        ctx.cinc(v.tensor_tensor(xt, xt, psd, ALU.add), "s_dve")


def _unit_plan(layer, qb):
    if layer == 0:
        kts = []
        if qb > 0:
            kts += [(4 * (qb - 1) + j, MO + j * 512) for j in range(4)]
        kts += [(4 * qb + j, MO + 2048 + j * 512) for j in range(4)]
    else:
        kts = [(j, None) for j in range(4 * qb)]
        kts += [(4 * qb + j, MO + 2048 + j * 512) for j in range(4)]
    return [(kts[2 * i], kts[2 * i + 1]) for i in range(len(kts) // 2)]


def p_attn_unit(ctx, layer, b, hl, reload):
    nc, a = ctx.nc, ctx.a
    ar = a["arena"]
    if reload:
        spill_all = ctx.cnt["s_sp"]
        with nc.Block() as blk:
            @blk.sync
            def _(s_):
                s_.wait_ge(ctx.sems["s_sp"], spill_all)
                ws = []
                for i, off in enumerate((QO, KO, VO)):
                    ws.append(ctx.slot_dma(s_.dma_start(
                        out=ar[:, off:off + 4096],
                        in_=ctx.kvspill[384 * b + 128 * i:384 * b + 128 * (i + 1), :]), 4 + i))
                for i in range(3):
                    s_.wait_ge(ctx.sems["q%d" % (4 + i)], ws[i])

    pe0 = ctx.cnt["s_pe"]
    act0 = ctx.cnt["s_act"]
    dve0 = ctx.cnt["s_dve"]
    stv = {}
    st3_0 = ctx.cnt["q3"]
    st7_0 = ctx.cnt["q7"]
    sp_all = ctx.cnt["s_sp"]

    nqb = ctx.cfg.get("nqb", 8)
    pe_n, act_n, dve_n, sc_n = [0], [0], [0], [0]
    recs = []
    gi = 0
    for qb in range(nqb):
        prs = []
        for (p1, p2) in _unit_plan(layer, qb):
            masked = p1[1] is not None
            pe_n[0] += 2
            act_n[0] += 1
            if masked:
                dve_n[0] += 1
            prs.append(dict(p1=p1, p2=p2, gi=gi, masked=masked,
                            s_after=pe_n[0] - 1, pv_after=pe_n[0],
                            e_after=act_n[0], m_after=dve_n[0]))
            gi += 1
        rec = dict(pairs=prs, qb=qb,
                   recip_after=dve_n[0] + 1, bcev_after=dve_n[0] + 2,
                   mult_after=dve_n[0] + 3, bc_after=pe_n[0] + 1)
        dve_n[0] += 3
        pe_n[0] += 1
        recs.append(rec)

    hoff = hl * 128

    with nc.Block() as blk:
        @blk.tensor
        def _(t):
            for rec in recs:
                qb = rec["qb"]
                qcols = slice(QO + qb * 512, QO + (qb + 1) * 512)
                psO = a["ps_b"][:, (qb % 2) * 512:(qb % 2) * 512 + 512]
                psD = a["ps_c"][0:1, (qb % 2) * 512:(qb % 2) * 512 + 512]
                for pi, pr in enumerate(rec["pairs"]):
                    buf = pr["gi"] % 2
                    if pr["gi"] >= 2:
                        t.wait_ge(ctx.sems["s_act"], act0 + pr["e_after"] - 2)
                    for j, (kt, _m) in enumerate((pr["p1"], pr["p2"])):
                        i = t.matmul(a["ps_a"][:, buf * 1024 + j * 512: buf * 1024 + (j + 1) * 512],
                                     ar[:, KO + kt * 128:KO + (kt + 1) * 128],
                                     ar[:, qcols], start=True, stop=True)
                    ctx.cinc(i, "s_pe")
                    if pr["masked"]:
                        t.wait_ge(ctx.sems["s_dve"], dve0 + pr["m_after"])
                    else:
                        t.wait_ge(ctx.sems["s_act"], act0 + pr["e_after"])
                    if pi == 0 and qb >= 2:
                        t.wait_ge(ctx.sems["s_dve"], dve0 + recs[qb - 2]["mult_after"])
                    src = a["pmask"] if pr["masked"] else a["praw"]
                    for j, (kt, _m) in enumerate((pr["p1"], pr["p2"])):
                        pp = src[:, buf * 1024 + j * 512: buf * 1024 + (j + 1) * 512]
                        t.matmul(psO, ar[:, VO + kt * 128:VO + (kt + 1) * 128], pp,
                                 start=(pi == 0 and j == 0), stop=False)
                        i = t.matmul(psD, a["consts"][:, 320:321], pp,
                                     start=(pi == 0 and j == 0), stop=False)
                    ctx.cinc(i, "s_pe")
                t.wait_ge(ctx.sems["s_dve"], dve0 + rec["recip_after"])
                psDfull = a["ps_c"][:, (qb % 2) * 512:(qb % 2) * 512 + 512]
                ctx.cinc(t.matmul(psDfull, a["consts"][0:1, 320:448],
                                  a["vtmp"][0:1, (qb % 2) * 512:(qb % 2) * 512 + 512],
                                  start=True, stop=True), "s_pe")

        @blk.scalar
        def _(sc):
            for rec in recs:
                for pr in rec["pairs"]:
                    buf = pr["gi"] % 2
                    sc.wait_ge(ctx.sems["s_pe"], pe0 + pr["s_after"])
                    ctx.cinc(sc.activation(a["praw"][:, buf * 1024:(buf + 1) * 1024],
                                           a["ps_a"][:, buf * 1024:(buf + 1) * 1024],
                                           AF.Exp, scale=SCALE), "s_act")
                qb = rec["qb"]
                sc.wait_ge(ctx.sems["s_dve"], dve0 + rec["mult_after"])
                r, sl = _owner(qb)
                dst_c = 1024 * (4 * b + r) + 512 * sl
                stv[qb] = ctx.slot_dma(sc.dma_start(
                    out=ctx.attn_ag_in[hoff:hoff + 128, dst_c:dst_c + 512],
                    in_=a["scr"][:, (qb % 2) * 512:(qb % 2) * 512 + 512]), 7 if qb % 2 else 3)

        @blk.vector
        def _(v):
            v.wait_ge(ctx.sems["s_ld"], ctx.qkv_ld_batch)
            for rec in recs:
                qb = rec["qb"]
                for pr in rec["pairs"]:
                    if not pr["masked"]:
                        continue
                    buf = pr["gi"] % 2
                    v.wait_ge(ctx.sems["s_act"], act0 + pr["e_after"])
                    moff = pr["p1"][1]
                    ctx.cinc(v.tensor_tensor(a["pmask"][:, buf * 1024:(buf + 1) * 1024],
                                             a["praw"][:, buf * 1024:(buf + 1) * 1024],
                                             ar[:, moff:moff + 1024], ALU.mult), "s_dve")
                psD = a["ps_c"][0:1, (qb % 2) * 512:(qb % 2) * 512 + 512]
                psDfull = a["ps_c"][:, (qb % 2) * 512:(qb % 2) * 512 + 512]
                psO = a["ps_b"][:, (qb % 2) * 512:(qb % 2) * 512 + 512]
                rrow32 = a["xpool"].ap().bitcast(F32)[0:1, (qb % 2) * 512:(qb % 2) * 512 + 512]
                rrow16 = a["vtmp"][0:1, (qb % 2) * 512:(qb % 2) * 512 + 512]
                if qb == 0:
                    v.wait_ge(ctx.sems["s_sp"], sp_all)
                v.wait_ge(ctx.sems["s_pe"], pe0 + rec["pairs"][-1]["pv_after"])
                v.reciprocal(rrow32, psD)
                ctx.cinc(v.tensor_copy(rrow16, rrow32), "s_dve")
                v.wait_ge(ctx.sems["s_pe"], pe0 + rec["bc_after"])
                ctx.cinc(v.tensor_copy(a["bc_sb"][:, :], psDfull), "s_dve")
                if qb >= 2:
                    ctx.slot_wait(v, 7 if qb % 2 else 3, stv[qb - 2])
                elif qb == 0:
                    v.wait_ge(ctx.sems["q3"], st3_0)
                elif qb == 1:
                    v.wait_ge(ctx.sems["q7"], st7_0)
                ctx.cinc(v.tensor_tensor(a["scr"][:, (qb % 2) * 512:(qb % 2) * 512 + 512],
                                         psO, a["bc_sb"][:, :], ALU.mult), "s_dve")
                if ctx.cfg.get("dump_p") and qb == 0 and b == 0 and hl == 0:
                    ctx.cinc(v.tensor_copy(a["sc_a"][:, :], a["ps_c"][:, 0:512]), "s_dve")
                    ctx.cinc(v.tensor_copy(a["sc_b"][:, :], a["ps_b"][:, 0:512]), "s_dve")


def p_attn_ag(ctx):
    nc = ctx.nc
    w3, w7 = ctx.cnt["q3"], ctx.cnt["q7"]
    with nc.Block() as blk:
        @blk.gpsimd
        def _(g):
            g.wait_ge(ctx.sems["q3"], w3)
            g.wait_ge(ctx.sems["q7"], w7)
            ctx.cinc(g.collective_compute(
                "AllGather", ALU.bypass,
                replica_groups=[list(range(8))],
                ins=[ctx.attn_ag_in.ap().opt()], outs=[ctx.attn_ag_out.ap().opt()]), "s_cc_at")
    ctx.attn_cc_done = ctx.cnt["s_cc_at"]


def p_proj(ctx, layer):
    nc, a = ctx.nc, ctx.a
    ar = a["arena"]
    wo = ctx.wo_bf[layer]
    av = ctx.attn_ag_out.ap().rearrange("(t p) n -> p t n", p=128)

    for tb in range(2):
        sc0 = ctx.cnt["s_sc"]
        pe0 = ctx.cnt["s_pe"]
        dve0 = ctx.cnt["s_dve"]
        wv = {}
        with nc.Block() as blk:
            @blk.scalar
            def _(sc):
                sc.wait_ge(ctx.sems["s_cc_at"], ctx.attn_cc_done)
                pid = sc.partition_id()
                dst = ar[:, RHS:RHS + 8192].rearrange("p (t n) -> p t n", n=512)
                ctx.dinc(sc.dma_start(
                    out=dst, in_=av[:, :, bass.ds(pid * 1024 + tb * 512, 512)]), "s_sc")

            @blk.sync
            def _(s):
                s.wait_ge(ctx.sems["s_gp"], ctx.wready["wo%d" % layer])
                for dgrp in range(4):
                    for din in range(16):
                        i = dgrp * 16 + din
                        slot = i % 8
                        if i >= 8:
                            s.wait_ge(ctx.sems["s_pe"], pe0 + (i - 8) + 1)
                        wv[i] = ctx.slot_dma(s.dma_start(
                            out=a["wpool"][:, slot * 1024:slot * 1024 + 512],
                            in_=wo[din * 128:(din + 1) * 128, dgrp * 512:(dgrp + 1) * 512]), slot)

            @blk.tensor
            def _(t):
                t.wait_ge(ctx.sems["s_sc"], sc0 + 16)
                for dgrp in range(4):
                    if dgrp > 0:
                        t.wait_ge(ctx.sems["s_dve"], dve0 + dgrp * 4)
                    for din in range(16):
                        i = dgrp * 16 + din
                        slot = i % 8
                        ctx.slot_wait(t, slot, wv[i])
                        rhs = ar[:, RHS + din * 512: RHS + (din + 1) * 512]
                        for do in range(4):
                            ii = t.matmul(a["ps_a"][:, do * 512:(do + 1) * 512],
                                          a["wpool"][:, slot * 1024 + do * 128: slot * 1024 + (do + 1) * 128],
                                          rhs, start=(din == 0), stop=(din == 15))
                        ctx.cinc(ii, "s_pe")

            @blk.vector
            def _(v):
                for dgrp in range(4):
                    v.wait_ge(ctx.sems["s_pe"], pe0 + dgrp * 16 + 16)
                    for do in range(4):
                        dt_ = dgrp * 4 + do
                        xt = a["x_fm"][:, dt_ * TOK + tb * 512: dt_ * TOK + (tb + 1) * 512]
                        ctx.cinc(v.tensor_tensor(xt, xt, a["ps_a"][:, do * 512:(do + 1) * 512],
                                                 ALU.add), "s_dve")


def p_ffn1a(ctx):
    nc, a = ctx.nc, ctx.a
    w1 = ctx.w1t_bf.ap().rearrange("(dt p) h -> p dt h", p=128)
    w3 = ctx.w3t_bf.ap().rearrange("(dt p) h -> p dt h", p=128)
    hv = ctx.hid_dram.ap().rearrange("(t p) n -> t p n", p=128)

    pe0 = ctx.cnt["s_pe"]
    act0 = ctx.cnt["s_act"]
    dve0 = ctx.cnt["s_dve"]
    b4, b5 = ctx.cnt["q4"], ctx.cnt["q5"]
    sv = {i: (b4 if i % 2 == 0 else b5) + (i // 2 + 1) * 16 for i in range(64)}
    wv = {}
    with nc.Block() as blk:
        @blk.sync
        def _(s):
            s.wait_ge(ctx.sems["s_gp"], ctx.wready["w3t"])
            for i in range(64):
                for dh in range(2):
                    step = i * 2 + dh
                    slot = step % 4
                    if step >= 4:
                        s.wait_ge(ctx.sems["s_pe"], pe0 + step - 3)
                    for mi, qm in enumerate((w1, w3)):
                        dst = a["wpool"][:, slot * 2048 + mi * 1024:
                                         slot * 2048 + (mi + 1) * 1024]
                        wv[step] = ctx.slot_dma(s.dma_start(
                            out=dst.rearrange("p (x b) -> p x b", b=128),
                            in_=qm[:, dh * 8:(dh + 1) * 8, i * 128:(i + 1) * 128]), slot)

        @blk.scalar
        def _(sc):
            for i in range(64):
                psx1 = a["ps_a"].ap()[:, 0:1024] if i % 2 == 0 else a["ps_b"].ap()[:, :]
                sc.wait_ge(ctx.sems["s_pe"], pe0 + i * 2 + 2)
                if i >= 2:
                    sc.wait_ge(ctx.sems["s_dve"], dve0 + i - 1)
                ctx.cinc(sc.activation(a["scr"][:, (i % 2) * 512:(i % 2) * 512 + 512],
                                       psx1[:, 0:512], AF.Silu), "s_act")
                ctx.cinc(sc.activation(a["vtmp"][:, (i % 2) * 512:(i % 2) * 512 + 512],
                                       psx1[:, 512:1024], AF.Silu), "s_act")

        @blk.tensor
        def _(t):
            for i in range(64):
                psx1 = a["ps_a"].ap()[:, 0:1024] if i % 2 == 0 else a["ps_b"].ap()[:, :]
                psx2 = a["ps_a"].ap()[:, 1024:2048] if i % 2 == 0 else a["ps_c"].ap()[:, :]
                if i >= 2:
                    t.wait_ge(ctx.sems["s_dve"], dve0 + i - 1)
                for dh in range(2):
                    step = i * 2 + dh
                    slot = step % 4
                    ctx.slot_wait(t, slot, wv[step])
                    for mi in range(2):
                        psd = psx1 if mi == 0 else psx2
                        for d8 in range(8):
                            d = dh * 8 + d8
                            w = a["wpool"][:, slot * 2048 + mi * 1024 + d8 * 128:
                                           slot * 2048 + mi * 1024 + (d8 + 1) * 128]
                            for th in range(2):
                                ii = t.matmul(psd[:, th * 512:(th + 1) * 512], w,
                                              a["xn"][:, d * TOK + th * 512: d * TOK + (th + 1) * 512],
                                              start=(d == 0), stop=(d == 15))
                    ctx.cinc(ii, "s_pe")

        @blk.vector
        def _(v):
            for i in range(64):
                psx2 = a["ps_a"].ap()[:, 1024:2048] if i % 2 == 0 else a["ps_c"].ap()[:, :]
                v.wait_ge(ctx.sems["s_act"], act0 + i * 2 + 2)
                if i >= 2:
                    ctx.slot_wait(v, 4 + i % 2, sv[i - 2])
                v.tensor_tensor(a["hidout"][:, (i % 2) * 1024:(i % 2) * 1024 + 512],
                                a["scr"][:, (i % 2) * 512:(i % 2) * 512 + 512],
                                psx2[:, 0:512], ALU.mult)
                ctx.cinc(v.tensor_tensor(
                    a["hidout"][:, (i % 2) * 1024 + 512:(i % 2) * 1024 + 1024],
                    a["vtmp"][:, (i % 2) * 512:(i % 2) * 512 + 512],
                    psx2[:, 512:1024], ALU.mult), "s_dve")

        @blk.gpsimd
        def _(g):
            for i in range(64):
                g.wait_ge(ctx.sems["s_dve"], dve0 + i + 1)
                w = ctx.slot_dma(g.dma_start(
                    out=hv[i, :, :],
                    in_=a["hidout"][:, (i % 2) * 1024:(i % 2) * 1024 + 1024]), 4 + i % 2)
                assert w == sv[i], (w, sv[i])


def p_fwht_b(ctx):
    nc, a = ctx.nc, ctx.a
    ar = a["arena"]
    hv = ctx.hid_dram.ap().rearrange("(t p) n -> t p n", p=128)
    bv = ctx.hidB_dram.ap().rearrange("(t p) n -> t p n", p=128)
    h64 = a["consts"][0:64, 256:320]
    pe0 = ctx.cnt["s_pe"]
    dve0 = ctx.cnt["s_dve"]
    hw4, hw5 = ctx.cnt["q4"], ctx.cnt["q5"]
    b2, b3 = ctx.cnt["q2"], ctx.cnt["q3"]
    sv = {c: (b2 if c % 2 == 0 else b3) + (c // 2 + 1) * 16 for c in range(16)}
    wv = {}
    with nc.Block() as blk:
        @blk.sync
        def _(s):
            s.wait_ge(ctx.sems["q4"], hw4)
            s.wait_ge(ctx.sems["q5"], hw5)
            for c in range(16):
                slot = c % 2
                if c >= 2:
                    s.wait_ge(ctx.sems["s_pe"], pe0 + (c - 1) * 16)
                wv[c] = ctx.slot_dma(s.dma_start(
                    out=ar[0:64, slot * 8192:(slot + 1) * 8192].rearrange(
                        "t (p n) -> t p n", n=1024),
                    in_=hv[:, c * 8:(c + 1) * 8, :]), slot)

        @blk.tensor
        def _(t):
            for c in range(16):
                slot = c % 2
                ctx.slot_wait(t, slot, wv[c])
                for j in range(16):
                    psd = a["ps_b"].ap() if j % 2 == 0 else a["ps_c"].ap()
                    if c * 16 + j >= 2:
                        t.wait_ge(ctx.sems["s_dve"], dve0 + c * 16 + j - 1)
                    ctx.cinc(t.matmul(psd[0:64, 0:512], h64,
                                      ar[0:64, slot * 8192 + j * 512: slot * 8192 + (j + 1) * 512],
                                      start=True, stop=True), "s_pe")

        @blk.vector
        def _(v):
            for c in range(16):
                oslot = c % 2
                if c >= 2:
                    ctx.slot_wait(v, 2 + oslot, sv[c - 2])
                for j in range(16):
                    psd = a["ps_b"].ap() if j % 2 == 0 else a["ps_c"].ap()
                    v.wait_ge(ctx.sems["s_pe"], pe0 + c * 16 + j + 1)
                    ctx.cinc(v.tensor_copy(
                        ar[0:64, 16384 + oslot * 8192 + j * 512: 16384 + oslot * 8192 + (j + 1) * 512],
                        psd[0:64, 0:512]), "s_dve")

        @blk.scalar
        def _(sc):
            for c in range(16):
                oslot = c % 2
                sc.wait_ge(ctx.sems["s_dve"], dve0 + c * 16 + 16)
                w = ctx.slot_dma(sc.dma_start(
                    out=bv[:, c * 8:(c + 1) * 8, :],
                    in_=ar[0:64, 16384 + oslot * 8192: 16384 + (oslot + 1) * 8192].rearrange(
                        "t (p n) -> t p n", n=1024)), 2 + oslot)
                assert w == sv[c], (w, sv[c])


def p_w2a(ctx):
    nc, a = ctx.nc, ctx.a
    ar = a["arena"]
    bv = ctx.hidB_dram.ap().rearrange("(t p) n -> t p n", p=128)
    w2 = ctx.w2t_bf
    h128 = a["consts"][:, 128:256]
    for tb in range(2):
        pe0 = ctx.cnt["s_pe"]
        dve0 = ctx.cnt["s_dve"]
        wv, wv2 = {}, {}
        with nc.Block() as blk:
            @blk.scalar
            def _(sc):
                if tb == 0:
                    sc.wait_ge(ctx.sems["s_gp"], ctx.wready["w2t"])
                for T in range(64):
                    slot = T % 2
                    if T >= 2:
                        sc.wait_ge(ctx.sems["s_pe"], pe0 + T - 1)
                    wv[T] = ctx.slot_dma(sc.dma_start(
                        out=a["xpool"][:, slot * 1024: slot * 1024 + 512],
                        in_=bv[T, :, :][:, tb * 512:(tb + 1) * 512]), slot)

            @blk.tensor
            def _(t):
                for T in range(64):
                    slot = T % 2
                    psd = a["ps_b"].ap() if T % 2 == 0 else a["ps_c"].ap()
                    ctx.slot_wait(t, slot, wv[T])
                    if T >= 2:
                        t.wait_ge(ctx.sems["s_dve"], dve0 + T - 1)
                    ctx.cinc(t.matmul(psd[:, 0:512], h128,
                                      a["xpool"][:, slot * 1024: slot * 1024 + 512],
                                      start=True, stop=True), "s_pe")

            @blk.vector
            def _(v):
                v.wait_ge(ctx.sems["q2"], ctx.cnt["q2"])
                v.wait_ge(ctx.sems["q3"], ctx.cnt["q3"])
                for T in range(64):
                    psd = a["ps_b"].ap() if T % 2 == 0 else a["ps_c"].ap()
                    v.wait_ge(ctx.sems["s_pe"], pe0 + T + 1)
                    ctx.cinc(v.tensor_scalar_mul(ar[:, T * 512:(T + 1) * 512],
                                                 psd[:, 0:512], FW_SC), "s_dve")

        pe1 = ctx.cnt["s_pe"]
        dve1 = ctx.cnt["s_dve"]
        with nc.Block() as blk:
            @blk.sync
            def _(s):
                for dgrp in range(4):
                    for T in range(64):
                        step = dgrp * 64 + T
                        slot = step % 8
                        if step >= 8:
                            s.wait_ge(ctx.sems["s_pe"], pe1 + step - 7)
                        wv2[step] = ctx.slot_dma(s.dma_start(
                            out=a["wpool"][:, slot * 1024:slot * 1024 + 512],
                            in_=w2[T * 128:(T + 1) * 128, dgrp * 512:(dgrp + 1) * 512]), slot)

            @blk.tensor
            def _(t):
                for dgrp in range(4):
                    if dgrp > 0:
                        t.wait_ge(ctx.sems["s_dve"], dve1 + dgrp * 4)
                    for T in range(64):
                        step = dgrp * 64 + T
                        slot = step % 8
                        ctx.slot_wait(t, slot, wv2[step])
                        for do in range(4):
                            ii = t.matmul(a["ps_a"][:, do * 512:(do + 1) * 512],
                                          a["wpool"][:, slot * 1024 + do * 128:
                                                      slot * 1024 + (do + 1) * 128],
                                          ar[:, T * 512:(T + 1) * 512],
                                          start=(T == 0), stop=(T == 63))
                        ctx.cinc(ii, "s_pe")

            @blk.vector
            def _(v):
                for dgrp in range(4):
                    v.wait_ge(ctx.sems["s_pe"], pe1 + dgrp * 64 + 64)
                    for do in range(4):
                        dt_ = dgrp * 4 + do
                        xt = a["x_fm"][:, dt_ * TOK + tb * 512: dt_ * TOK + (tb + 1) * 512]
                        ctx.cinc(v.tensor_tensor(xt, xt, a["ps_a"][:, do * 512:(do + 1) * 512],
                                                 ALU.add), "s_dve")


def p_store(ctx):
    nc, a = ctx.nc, ctx.a
    arF = a["arena"].ap().bitcast(F32)
    pe0 = ctx.cnt["s_pe"]
    dve0 = ctx.cnt["s_dve"]
    b0, b1 = ctx.cnt["q0"], ctx.cnt["q1"]
    sv = {tt: (b0 if tt % 2 == 0 else b1) + (tt // 2 + 1) * 16 for tt in range(8)}
    with nc.Block() as blk:
        @blk.tensor
        def _(t):
            for tt in range(8):
                for d in range(16):
                    j = tt * 16 + d
                    dst = a["ps_b"][:, (d % 2) * 512:(d % 2) * 512 + 128]
                    if j >= 2:
                        t.wait_ge(ctx.sems["s_dve"], dve0 + j - 1)
                    ctx.cinc(t.transpose(
                        dst, a["x_fm"][:, d * TOK + tt * 128: d * TOK + (tt + 1) * 128],
                        a["identf"][:, :]), "s_pe")

        @blk.vector
        def _(v):
            for tt in range(8):
                oslot = tt % 2
                if tt >= 2:
                    ctx.slot_wait(v, oslot, sv[tt - 2])
                for d in range(16):
                    j = tt * 16 + d
                    src = a["ps_b"][:, (d % 2) * 512:(d % 2) * 512 + 128]
                    v.wait_ge(ctx.sems["s_pe"], pe0 + j + 1)
                    ctx.cinc(v.tensor_copy(
                        arF[:, oslot * 2048 + d * 128: oslot * 2048 + (d + 1) * 128], src),
                        "s_dve")

        @blk.sync
        def _(s):
            for tt in range(8):
                oslot = tt % 2
                s.wait_ge(ctx.sems["s_dve"], dve0 + tt * 16 + 16)
                w = ctx.slot_dma(s.dma_start(
                    out=ctx.out_own[tt * 128:(tt + 1) * 128, :],
                    in_=arF[:, oslot * 2048:(oslot + 1) * 2048]), oslot)
                assert w == sv[tt], (w, sv[tt])
            s.wait_ge(ctx.sems["q0"], sv[6])
            s.wait_ge(ctx.sems["q1"], sv[7])


def build(cfg):
    nc = bass.Bass("TRN2", target_bir_lowering=False, debug=False)
    ctx = Ctx(nc, cfg)

    ctx.x_in = nc.dram_tensor("x_own", [TOK, D], F32, kind="ExternalInput")

    ctx.identf_dram = nc.inline_tensor(np.eye(128, dtype=np.float32), "identf_c")
    consts = np.zeros((128, 768), dtype=ml_dtypes.bfloat16)
    consts[:, 0:128] = np.eye(128, dtype=ml_dtypes.bfloat16)
    consts[:, 128:256] = _hadamard(128).astype(ml_dtypes.bfloat16)
    consts[0:64, 256:320] = _hadamard(64).astype(ml_dtypes.bfloat16)
    consts[:, 320:448] = 1.0
    ctx.consts_dram = nc.inline_tensor(consts, "consts_c")
    ctx.masks_dram = nc.inline_tensor(_masks_np(), "masks_c")
    fconst = np.zeros((128, 136), dtype=np.float32)
    fconst[:, 0:128] = 1.0
    fconst[:, 128] = LN_EPS
    fconst[:, 129] = Q_EPS / 2.0
    ctx.fconst_dram = nc.inline_tensor(fconst, "fconst_c")

    ctx.xn_ag_in = nc.dram_tensor("xn_ag_in", [D, TOK], F16, kind="Internal")
    ctx.xn_ag_out = nc.dram_tensor("xn_ag_out", [8 * D, TOK], F16, kind="Internal")
    ctx.attn_ag_in = nc.dram_tensor("attn_ag_in", [256, 8192], F16, kind="Internal")
    ctx.attn_ag_out = nc.dram_tensor("attn_ag_out", [D, 8192], F16, kind="Internal")
    ctx.kvspill = nc.dram_tensor("kvspill", [768, 4096], F16, kind="Internal")

    ctx.wq_in = [nc.dram_tensor("l%d_qkv_sl" % l, [D, 768], F32, kind="ExternalInput")
                 for l in range(2)]
    ctx.wo_in = [nc.dram_tensor("l%d_wo" % l, [D, D], F32, kind="ExternalInput")
                 for l in range(2)]
    ctx.wq_bf = [nc.dram_tensor("l%d_qkv_bf" % l, [D, 768], F16, kind="Internal")
                 for l in range(2)]
    ctx.wo_bf = [nc.dram_tensor("l%d_wo_bf" % l, [D, D], F16, kind="Internal")
                 for l in range(2)]
    ctx.w1t_in = nc.dram_tensor("l1_w1t", [D, H1], F32, kind="ExternalInput")
    ctx.w3t_in = nc.dram_tensor("l1_w3t", [D, H1], F32, kind="ExternalInput")
    ctx.w2t_in = nc.dram_tensor("l1_w2t", [H1, D], F32, kind="ExternalInput")
    ctx.w1t_bf = nc.dram_tensor("w1t_bf", [D, H1], F16, kind="Internal")
    ctx.w3t_bf = nc.dram_tensor("w3t_bf", [D, H1], F16, kind="Internal")
    ctx.w2t_bf = nc.dram_tensor("w2t_bf", [H1, D], F16, kind="Internal")
    ctx.hid_dram = nc.dram_tensor("hid_dram", [H1, TOK], F16, kind="Internal")
    ctx.hidB_dram = nc.dram_tensor("hidB_dram", [H1, TOK], F16, kind="Internal")
    ctx.out_own = nc.dram_tensor("out_own", [TOK, D], F32, kind="ExternalOutput")
    ctx.cast_list = [
        ("wq0", ctx.wq_in[0], ctx.wq_bf[0]),
        ("wo0", ctx.wo_in[0], ctx.wo_bf[0]),
        ("w1t", ctx.w1t_in, ctx.w1t_bf),
        ("w3t", ctx.w3t_in, ctx.w3t_bf),
        ("wq1", ctx.wq_in[1], ctx.wq_bf[1]),
        ("wo1", ctx.wo_in[1], ctx.wo_bf[1]),
        ("w2t", ctx.w2t_in, ctx.w2t_bf),
    ]
    ctx.ffn0_sl = nc.dram_tensor("l0_ffn_sl", [128, 3 * QSLF], F32, kind="ExternalInput")
    ctx.q_in = [nc.dram_tensor("q_in%d" % m, [128, QSLF], F16, kind="Internal")
                for m in range(3)]
    ctx.q_out = [nc.dram_tensor("q_out%d" % m, [1024, QSLF], F16, kind="Internal")
                 for m in range(3)]
    ctx.gm_in = nc.dram_tensor("gm_in", [1, 3], F32, kind="Internal")
    ctx.gm_out = nc.dram_tensor("gm_out", [8, 3], F32, kind="Internal")

    if cfg.get("dump_xn"):
        ctx.out_xn = nc.dram_tensor("out_xn", [D, TOK], F32, kind="ExternalOutput")
        ctx.out_xng = nc.dram_tensor("out_xng", [8 * D, TOK], F32, kind="ExternalOutput")
    if cfg.get("dump_attn"):
        ctx.out_attng = nc.dram_tensor("out_attng", [D, 8192], F32, kind="ExternalOutput")
    if cfg.get("dump_x"):
        ctx.out_x = nc.dram_tensor("out_x", [D, TOK], F32, kind="ExternalOutput")
    if cfg.get("dump_qkv"):
        ctx.out_qkv = nc.dram_tensor("out_qkv", [384, 4096], F16, kind="ExternalOutput")
    if cfg.get("dump_p"):
        ctx.out_p = nc.dram_tensor("out_p", [512, 2048], F16, kind="ExternalOutput")
    if cfg.get("dump_qw"):
        ctx.out_qw = nc.dram_tensor("out_qw", [1024, QSLF], F16, kind="ExternalOutput")
        ctx.out_hid = nc.dram_tensor("out_hid", [128, 22528], F16, kind="ExternalOutput")

    ctx.a = _alloc(ctx)

    upto = cfg.get("upto", 99)
    p_consts(ctx)
    p_wcast(ctx)
    p_quant1(ctx)
    p_load_x(ctx, ctx.x_in)
    p_layernorm(ctx)
    p_xn_ag(ctx)
    if upto == 66:
        with nc.Block() as blk:
            @blk.gpsimd
            def _(g):
                ctx.cinc(g.collective_compute(
                    "AllGather", ALU.bypass, replica_groups=[list(range(8))],
                    ins=[ctx.attn_ag_in.ap().opt()], outs=[ctx.attn_ag_out.ap().opt()]),
                    "s_cc_at")
                g.wait_ge(ctx.sems["s_cc_at"], 1)
        ctx.stack.close()
        return nc, ctx
    if upto >= 2:
        p_quant2(ctx)
    if upto == 67:
        with nc.Block() as blk:
            @blk.gpsimd
            def _(g):
                g.wait_ge(ctx.sems["s_cc_q"], ctx.q_cc_done)
                ctx.cinc(g.collective_compute(
                    "AllGather", ALU.bypass, replica_groups=[list(range(8))],
                    ins=[ctx.attn_ag_in.ap().opt()], outs=[ctx.attn_ag_out.ap().opt()]),
                    "s_cc_at")
                g.wait_ge(ctx.sems["s_cc_at"], 1)
        ctx.stack.close()
        return nc, ctx
    if cfg.get("upto") == "ln":
        pass
    else:
        if upto < 3:
            pass
        elif cfg.get("dump_p"):
            p_qkv(ctx, 0, 0)
            p_attn_unit(ctx, 0, 0, 0, reload=False)
            with nc.Block() as blk:
                @blk.sync
                def _(s):
                    ctx.dinc(s.dma_start(out=ctx.out_p[0:128, 0:2048], in_=ctx.a["praw"][:, :]), "s_st")
                    ctx.dinc(s.dma_start(out=ctx.out_p[128:256, 0:2048], in_=ctx.a["pmask"][:, :]), "s_st")
                    ctx.dinc(s.dma_start(out=ctx.out_p[256:384, 0:1024].bitcast(F32),
                                         in_=ctx.a["sc_a"][:, :]), "s_st")
                    ctx.dinc(s.dma_start(out=ctx.out_p[384:512, 0:1024].bitcast(F32),
                                         in_=ctx.a["sc_b"][:, :]), "s_st")
                    s.wait_ge(ctx.sems["s_st"], ctx.cnt["s_st"])
            ctx.stack.close()
            return nc, ctx
        def _ccprobe():
            with nc.Block() as blk:
                @blk.gpsimd
                def _(g):
                    ctx.cinc(g.collective_compute(
                        "AllGather", ALU.bypass, replica_groups=[list(range(8))],
                        ins=[ctx.attn_ag_in.ap().opt()], outs=[ctx.attn_ag_out.ap().opt()]),
                        "s_cc_at")
                    g.wait_ge(ctx.sems["s_cc_at"], ctx.cnt["s_cc_at"])
        blist = [] if upto < 3 else ([0] if upto in (3, 68) else [0, 1])
        for b in blist:
            p_qkv(ctx, 0, b)
            if upto == 68:
                _ccprobe()
                ctx.stack.close()
                return nc, ctx
            if upto == 3:
                continue
            if upto == 69 and b == 0:
                p_attn_unit(ctx, 0, 0, 0, reload=False)
                p_attn_unit(ctx, 0, 0, 1, reload=True)
                _ccprobe()
                ctx.stack.close()
                return nc, ctx
            if b == 0 and cfg.get("dump_qkv"):
                with nc.Block() as blk:
                    @blk.sync
                    def _(s):
                        ctx.dinc(s.dma_start(out=ctx.out_qkv[0:128, :], in_=ctx.a["arena"][:, QO:QO + 4096]), "s_st")
                        ctx.dinc(s.dma_start(out=ctx.out_qkv[128:256, :], in_=ctx.a["arena"][:, KO:KO + 4096]), "s_st")
                        ctx.dinc(s.dma_start(out=ctx.out_qkv[256:384, :], in_=ctx.a["arena"][:, VO:VO + 4096]), "s_st")
                        s.wait_ge(ctx.sems["s_st"], ctx.cnt["s_st"])
            p_attn_unit(ctx, 0, b, 0, reload=False)
            p_attn_unit(ctx, 0, b, 1, reload=True)
        if upto == 70:
            _ccprobe()
            ctx.stack.close()
            return nc, ctx
        if upto < 5:
            ctx.stack.close()
            return nc, ctx
        p_attn_ag(ctx)
        if upto == 71:
            with nc.Block() as blk:
                @blk.gpsimd
                def _(g):
                    g.wait_ge(ctx.sems["s_cc_at"], ctx.attn_cc_done)
            ctx.stack.close()
            return nc, ctx
        if upto < 6:
            ctx.stack.close()
            return nc, ctx
        p_ffn0(ctx, 0)
        if cfg.get("dump_qw"):
            with nc.Block() as blk:
                @blk.sync
                def _(s):
                    ctx.dinc(s.dma_start(out=ctx.out_qw[:, :], in_=ctx.q_out[0][:, :]), "s_st")
                    ctx.dinc(s.dma_start(out=ctx.out_hid[:, :],
                                         in_=ctx.a["arena"][:, 0:22528]), "s_st")
                    s.wait_ge(ctx.sems["s_st"], ctx.cnt["s_st"])
        if upto == 65:
            with nc.Block() as blk:
                @blk.gpsimd
                def _(g):
                    g.wait_ge(ctx.sems["s_cc_at"], ctx.attn_cc_done)
        if upto >= 7:
            p_proj(ctx, 0)
        if upto >= 8:
            p_ffn0(ctx, 1)
        if upto >= 9:
            p_layernorm(ctx)
            p_xn_ag(ctx)
            p_ffn1a(ctx)
            for b in range(2):
                p_qkv(ctx, 1, b)
                p_attn_unit(ctx, 1, b, 0, reload=False)
                p_attn_unit(ctx, 1, b, 1, reload=True)
            p_attn_ag(ctx)
            p_fwht_b(ctx)
            p_proj(ctx, 1)
            p_w2a(ctx)
            p_store(ctx)

    if cfg.get("dump_xn"):
        with nc.Block() as blk:
            @blk.gpsimd
            def _(g):
                g.wait_ge(ctx.sems["s_cc"], ctx.xn_cc_done)
                ctx.dinc(g.dma_start(out=ctx.out_xng[:, :], in_=ctx.xn_ag_out[:, :]), "s_st")
                for d in range(16):
                    ctx.dinc(g.dma_start(out=ctx.out_xn[d * 128:(d + 1) * 128, :],
                                         in_=ctx.a["xn"][:, d * TOK:(d + 1) * TOK]), "s_st")
                g.wait_ge(ctx.sems["s_st"], ctx.cnt["s_st"])
    if cfg.get("dump_attn"):
        with nc.Block() as blk:
            @blk.gpsimd
            def _(g):
                g.wait_ge(ctx.sems["s_cc"], ctx.attn_cc_done)
                ctx.dinc(g.dma_start(out=ctx.out_attng[:, :], in_=ctx.attn_ag_out[:, :]), "s_st")
                g.wait_ge(ctx.sems["s_st"], ctx.cnt["s_st"])
    if cfg.get("dump_x"):
        with nc.Block() as blk:
            @blk.sync
            def _(s):
                for d in range(16):
                    ctx.dinc(s.dma_start(out=ctx.out_x[d * 128:(d + 1) * 128, :],
                                         in_=ctx.a["x_fm"][:, d * TOK:(d + 1) * TOK]), "s_st")
                s.wait_ge(ctx.sems["s_st"], ctx.cnt["s_st"])

    ctx.stack.close()
    return nc, ctx


def make_in_maps(inputs):
    x = np.asarray(inputs["L0_attn_x"])
    maps = []
    for c in range(NCORE):
        b, k = c // 4, c % 4
        blocks = [k, 7 - k]
        wqkv = np.asarray(inputs["L0_attn_w_qkv"])
        l0sl = np.concatenate([wqkv[256 * c:256 * (c + 1)],
                               wqkv[2048 + 256 * c:2048 + 256 * (c + 1)],
                               wqkv[4096 + 256 * c:4096 + 256 * (c + 1)]], axis=0)
        l1sl = np.concatenate([np.asarray(inputs["L1_attn_Wq"])[256 * c:256 * (c + 1)],
                               np.asarray(inputs["L1_attn_Wk"])[256 * c:256 * (c + 1)],
                               np.asarray(inputs["L1_attn_Wv"])[256 * c:256 * (c + 1)]], axis=0)
        QSL = H0 * D // NCORE
        fsl = [np.asarray(w).T.reshape(-1)[c * QSL:(c + 1) * QSL].reshape(128, QSLF)
               for w in (inputs["L0_ffn_w1"], inputs["L0_ffn_w2"], inputs["L0_ffn_w3"])]
        m = {
            "x_own": np.ascontiguousarray(
                np.concatenate([x[b, 512 * g:512 * (g + 1)] for g in blocks], axis=0)),
            "l0_qkv_sl": np.ascontiguousarray(l0sl.T),
            "l1_qkv_sl": np.ascontiguousarray(l1sl.T),
            "l0_wo": np.ascontiguousarray(np.asarray(inputs["L0_attn_w_out"]).T),
            "l1_wo": np.ascontiguousarray(np.asarray(inputs["L1_attn_Wo"]).T),
            "l0_ffn_sl": np.ascontiguousarray(np.concatenate(fsl, axis=1)),
            "l1_w1t": np.ascontiguousarray(np.asarray(inputs["L1_ffn_W1"]).T),
            "l1_w3t": np.ascontiguousarray(np.asarray(inputs["L1_ffn_W3"]).T),
            "l1_w2t": np.ascontiguousarray(np.asarray(inputs["L1_ffn_W2"]).T),
        }
        maps.append(m)
    return maps


def kernel(**inputs):
    nc, ctx = build({})
    maps = make_in_maps(inputs)
    res = run_bass_kernel_spmd(nc, maps, core_ids=list(range(NCORE)))
    out = np.empty((B, S, D), dtype=np.float32)
    for c in range(NCORE):
        b, k = c // 4, c % 4
        oc = res.results[c]["out_own"]
        for s_i, g in enumerate([k, 7 - k]):
            out[b, 512 * g:512 * (g + 1)] = oc[512 * s_i:512 * (s_i + 1)]
    return out


# version 03
# speedup vs baseline: 1.1905x; 1.1865x over previous
import numpy as np
import ml_dtypes
from contextlib import ExitStack

import concourse.bass as bass
import concourse.mybir as mybir
from concourse.bass_utils import run_bass_kernel_spmd

F32 = mybir.dt.float32
F16 = mybir.dt.bfloat16
AF = mybir.ActivationFunctionType
ALU = mybir.AluOpType
AX = mybir.AxisListType

B, S, D = 2, 4096, 2048
NH, DH, C = 16, 128, 512
TOK = 1024
NCORE = 8
HPC = 2
SCALE = float(128) ** -0.5
LN_EPS = 1e-5
Q_EPS = 1e-8
H0 = 5632
H1 = 8192
QSLF = (H0 * D // NCORE) // 128
FW_SC = float(H1) ** -0.5


def _owner(g):
    return (min(g, 7 - g), 0 if g < 4 else 1)


def _masks_np():
    p = np.arange(128)[:, None]
    q = np.arange(512)[None, :]
    cols = []
    for o in (0, 128, 256, 384):
        cols.append((p + o >= q).astype(ml_dtypes.bfloat16))
    for o in (0, 128, 256, 384):
        cols.append((p + o <= q).astype(ml_dtypes.bfloat16))
    return np.concatenate(cols, axis=1)


def _hadamard(n):
    H = np.array([[1.0]], dtype=np.float32)
    while H.shape[0] < n:
        H = np.block([[H, H], [H, -H]])
    return H


class Ctx:
    def __init__(self, nc, cfg):
        self.nc = nc
        self.cfg = cfg
        self.cnt = {}
        self.stack = ExitStack()
        self.sems = {}

    def sem(self, name):
        if name not in self.sems:
            self.sems[name] = self.stack.enter_context(self.nc.semaphore(name))
            self.cnt[name] = 0
        return self.sems[name]

    def dinc(self, inst, name):
        inst.then_inc(self.sems[name], 16)
        self.cnt[name] += 16
        return self.cnt[name]

    def slot_dma(self, inst, slot):
        return self.dinc(inst, "q%d" % slot)

    def slot_wait(self, eng, slot, val):
        eng.wait_ge(self.sems["q%d" % slot], val)

    def cinc(self, inst, name, val=1):
        inst.then_inc(self.sems[name], val)
        self.cnt[name] += val
        return self.cnt[name]


def _alloc(ctx):
    nc = ctx.nc
    a = {}
    a["x_fm"] = nc.alloc_sbuf_tensor("x_fm", [128, 16 * TOK], F32)
    a["xn"] = nc.alloc_sbuf_tensor("xn", [128, 16 * TOK], F16)
    a["arena"] = nc.alloc_sbuf_tensor("arena", [128, 32768], F16)
    a["wpool"] = nc.alloc_sbuf_tensor("wpool", [128, 8192], F16)
    a["xpool"] = nc.alloc_sbuf_tensor("xpool", [128, 2048], F16)
    a["praw"] = nc.alloc_sbuf_tensor("praw", [128, 2048], F16)
    a["pmask"] = nc.alloc_sbuf_tensor("pmask", [128, 2048], F16)
    a["sc_a"] = nc.alloc_sbuf_tensor("sc_a", [128, 512], F32)
    a["sc_b"] = nc.alloc_sbuf_tensor("sc_b", [128, 512], F32)
    a["scr"] = nc.alloc_sbuf_tensor("scr", [128, 1024], F16)
    a["bc_sb"] = nc.alloc_sbuf_tensor("bc_sb", [128, 512], F16)
    a["hidout"] = nc.alloc_sbuf_tensor("hidout", [128, 2048], F16)
    a["vtmp"] = nc.alloc_sbuf_tensor("vtmp", [128, 1024], F16)
    a["consts"] = nc.alloc_sbuf_tensor("consts", [128, 768], F16)
    a["identf"] = nc.alloc_sbuf_tensor("identf", [128, 128], F32)
    a["pextra"] = nc.alloc_sbuf_tensor("pextra", [128, 2048], F16)
    a["fconst"] = nc.alloc_sbuf_tensor("fconst", [128, 136], F32)
    print(f"[build] sbuf remaining: {nc.sbuf_bytes_remaining} B/part")
    a["ps_a"] = nc.alloc_psum_tensor("ps_a", [128, 2048], F32)
    a["ps_b"] = nc.alloc_psum_tensor("ps_b", [128, 1024], F32)
    a["ps_c"] = nc.alloc_psum_tensor("ps_c", [128, 1024], F32)
    return a


def p_consts(ctx):
    nc, a = ctx.nc, ctx.a
    ctx.sem("s_ld"); ctx.sem("s_sc"); ctx.sem("s_gp"); ctx.sem("s_st")
    ctx.sem("s_pe"); ctx.sem("s_act"); ctx.sem("s_dve"); ctx.sem("s_cc")
    ctx.sem("s_cc_q"); ctx.sem("s_cc_at"); ctx.sem("s_sp")
    for i in range(8):
        ctx.sem("q%d" % i)
    for i in range(7):
        ctx.sem("c%d" % i)
    with nc.Block() as blk:
        @blk.sync
        def _(s):
            ctx.dinc(s.dma_start(out=a["identf"][:, :], in_=ctx.identf_dram[:, :]), "s_ld")
            ctx.dinc(s.dma_start(out=a["consts"][:, :], in_=ctx.consts_dram[:, :]), "s_ld")
            w = ctx.dinc(s.dma_start(out=a["fconst"][:, :], in_=ctx.fconst_dram[:, :]), "s_ld")
            s.wait_ge(ctx.sems["s_ld"], w)


def p_load_x(ctx, x_dram, blk):
    nc, a = ctx.nc, ctx.a
    arena_f32 = a["arena"].ap().bitcast(F32)

    pe0 = ctx.cnt["s_pe"]
    dve0 = ctx.cnt["s_dve"]
    wv = {}

    if True:
        @blk.sync
        def _(s):
            for tt in range(8):
                buf = tt % 2
                if tt >= 2:
                    s.wait_ge(ctx.sems["s_pe"], pe0 + (tt - 1) * 16)
                wv[tt] = ctx.slot_dma(s.dma_start(
                    out=arena_f32[:, buf * 2048:(buf + 1) * 2048],
                    in_=x_dram[tt * 128:(tt + 1) * 128, :]), buf)

        @blk.tensor
        def _(t):
            for tt in range(8):
                buf = tt % 2
                ctx.slot_wait(t, buf, wv[tt])
                for d in range(16):
                    j = tt * 16 + d
                    dst = a["ps_b"][:, (d % 2) * 512:(d % 2) * 512 + 128]
                    if j >= 2:
                        t.wait_ge(ctx.sems["s_dve"], dve0 + j - 1)
                    ctx.cinc(t.transpose(
                        dst,
                        arena_f32[:, buf * 2048 + d * 128: buf * 2048 + (d + 1) * 128],
                        a["identf"][:, :]), "s_pe")

        @blk.vector
        def _(v):
            for tt in range(8):
                for d in range(16):
                    j = tt * 16 + d
                    src = a["ps_b"][:, (d % 2) * 512:(d % 2) * 512 + 128]
                    v.wait_ge(ctx.sems["s_pe"], pe0 + j + 1)
                    ctx.cinc(v.tensor_copy(
                        a["x_fm"][:, d * TOK + tt * 128: d * TOK + (tt + 1) * 128], src),
                        "s_dve")


def p_layernorm(ctx):
    nc, a = ctx.nc, ctx.a
    ones_col = a["consts"][:, 320:321]
    x_fm = a["x_fm"]
    act0 = ctx.cnt["s_act"]
    pe0 = ctx.cnt["s_pe"]

    bufs = [a["scr"], a["hidout"]]
    with nc.Block() as blk:
        @blk.scalar
        def _(sc):
            for j in range(32):
                d, tb = j // 2, j % 2
                bf = bufs[j % 2]
                src = x_fm[:, d * TOK + tb * 512: d * TOK + (tb + 1) * 512]
                if j >= 2:
                    sc.wait_ge(ctx.sems["s_pe"], pe0 + j - 1)
                sc.copy(bf[:, 0:512], src)
                ctx.cinc(sc.activation(bf[:, 512:1024], src, AF.Square), "s_act")

        @blk.tensor
        def _(t):
            for j in range(32):
                d, tb = j // 2, j % 2
                bf = bufs[j % 2]
                t.wait_ge(ctx.sems["s_act"], act0 + j + 1)
                first, last = (d == 0), (d == 15)
                t.matmul(a["ps_c"][0:1, tb * 512:(tb + 1) * 512], ones_col,
                         bf[:, 0:512], start=first, stop=last)
                ctx.cinc(t.matmul(a["ps_b"][0:1, tb * 512:(tb + 1) * 512], ones_col,
                                  bf[:, 512:1024], start=first, stop=last), "s_pe")
    pe_stats = ctx.cnt["s_pe"]

    xp32 = a["xpool"].ap().bitcast(F32)
    for tb in range(2):
        r_mean = a["sc_a"][0:1, 0:512]
        r_rstd = xp32[0:1, 0:512]
        r_negmr = xp32[0:1, 512:1024]
        with nc.Block() as blk:
            @blk.vector
            def _(v):
                v.wait_ge(ctx.sems["s_pe"], pe_stats)
                v.tensor_scalar_mul(r_mean, a["ps_c"][0:1, tb * 512:(tb + 1) * 512], 1.0 / D)
                ctx.cinc(v.tensor_scalar_mul(r_rstd, a["ps_b"][0:1, tb * 512:(tb + 1) * 512],
                                             1.0 / D), "s_dve")
            d1 = ctx.cnt["s_dve"]

            @blk.scalar
            def _(sc):
                sc.wait_ge(ctx.sems["s_dve"], d1)
                ctx.cinc(sc.activation(r_negmr, r_mean, AF.Square), "s_act")
            a1 = ctx.cnt["s_act"]

            @blk.vector
            def _(v):
                v.wait_ge(ctx.sems["s_act"], a1)
                ctx.cinc(v.tensor_sub(r_rstd, r_rstd, r_negmr), "s_dve")
            d2 = ctx.cnt["s_dve"]

            @blk.scalar
            def _(sc):
                sc.wait_ge(ctx.sems["s_dve"], d2)
                sc.activation(r_rstd, r_rstd, AF.Ln, bias=a["fconst"][0:1, 128:129])
                ctx.cinc(sc.activation(r_rstd, r_rstd, AF.Exp, scale=-0.5), "s_act")
            a2 = ctx.cnt["s_act"]

            @blk.vector
            def _(v):
                v.wait_ge(ctx.sems["s_act"], a2)
                v.tensor_mul(r_negmr, r_mean, r_rstd)
                v.tensor_scalar_mul(r_negmr, r_negmr, -1.0)

        with nc.Block() as blk:
            @blk.tensor
            def _(t):
                t.matmul(a["ps_a"][:, 0:512], a["fconst"][0:1, 0:128],
                         r_rstd, start=True, stop=True)
                ctx.cinc(t.matmul(a["ps_a"][:, 512:1024], a["fconst"][0:1, 0:128],
                                  r_negmr, start=True, stop=True), "s_pe")
            pe_bc = ctx.cnt["s_pe"]

            @blk.vector
            def _(v):
                v.wait_ge(ctx.sems["s_pe"], pe_bc)
                for d in range(16):
                    xt = x_fm[:, d * TOK + tb * 512: d * TOK + (tb + 1) * 512]
                    v.tensor_tensor(a["sc_b"][:, :], xt, a["ps_a"][:, 0:512], ALU.mult)
                    v.tensor_tensor(a["xn"][:, d * TOK + tb * 512: d * TOK + (tb + 1) * 512],
                                    a["sc_b"][:, :], a["ps_a"][:, 512:1024], ALU.add)


def p_xn_ag(ctx):
    nc, a = ctx.nc, ctx.a
    with nc.Block() as blk:
        @blk.scalar
        def _(sc):
            for d in range(16):
                ctx.dinc(sc.dma_start(out=ctx.xn_ag_in[d * 128:(d + 1) * 128, :],
                                      in_=a["xn"][:, d * TOK:(d + 1) * TOK]), "s_sc")

        sc_done = ctx.cnt["s_sc"]

        @blk.gpsimd
        def _(g):
            g.wait_ge(ctx.sems["s_sc"], sc_done)
            ctx.cinc(g.collective_compute(
                "AllGather", ALU.bypass,
                replica_groups=[list(range(8))],
                ins=[ctx.xn_ag_in.ap().opt()], outs=[ctx.xn_ag_out.ap().opt()]), "s_cc")
    ctx.xn_cc_done = ctx.cnt["s_cc"]



QO, KO, VO, MO = 0, 4096, 8192, 12288
Q1, K1, V1 = 16384, 20480, 24576
RHS = 22528


def p_wcast_issue(ctx, names, g):
    nc = ctx.nc
    if not hasattr(ctx, "wready"):
        ctx.wready = {}
        ctx.wslin = {}
    pid = g.partition_id()
    for i, (name, src_t, dst_t) in enumerate(ctx.cast_list):
        if name not in names:
            continue
        sem = "c%d" % i
        rows = src_t.shape[0] // NCORE
        sl_in = nc.dram_tensor("wsl_in_%s" % name, [rows, src_t.shape[1]],
                               F16, kind="Internal")
        ctx.wslin[name] = (sl_in, sem, ctx.dinc(g.dma_start(
            out=sl_in.ap(), in_=src_t.ap()[bass.ds(pid * rows, rows), :]), sem))


def p_wcast_trigger(ctx, names, g):
    for i, (name, src_t, dst_t) in enumerate(ctx.cast_list):
        if name not in names:
            continue
        sl_in, sem, w = ctx.wslin[name]
        g.wait_ge(ctx.sems[sem], w)
        ctx.wready[name] = (sem, ctx.cinc(g.collective_compute(
            "AllGather", ALU.bypass, replica_groups=[list(range(8))],
            ins=[sl_in.ap().opt()], outs=[dst_t.ap().opt()]), sem))


def p_wcast(ctx, names):
    nc = ctx.nc
    with nc.Block() as blk:
        @blk.gpsimd
        def _(g):
            p_wcast_issue(ctx, names, g)
            p_wcast_trigger(ctx, names, g)


def p_qkv(ctx, layer, b):
    nc, a = ctx.nc, ctx.a
    ar = a["arena"]
    wsl = ctx.wq_bf[layer]

    with nc.Block() as blk:
        @blk.sync
        def _(s):
            ctx.qkv_ld_batch = ctx.dinc(
                s.dma_start(out=ar[:, MO:MO + 4096], in_=ctx.masks_dram[:, :]), "s_ld")

    xv = ctx.xn_ag_out.ap().rearrange("(r p) n -> p r n", p=128)
    pe0 = ctx.cnt["s_pe"]
    dve0 = ctx.cnt["s_dve"]
    wv = {}
    lv = {}
    with nc.Block() as blk:
        @blk.sync
        def _(s):
            s.wait_ge(ctx.sems[ctx.wready["wq%d" % layer][0]], ctx.wready["wq%d" % layer][1])
            for g in range(8):
                for d in range(16):
                    idx = g * 16 + d
                    slot = 4 + idx % 4
                    if idx >= 4:
                        gp, dp = (idx - 4) // 16, (idx - 4) % 16
                        s.wait_ge(ctx.sems["s_pe"], pe0 + gp * 24 + dp + 1)
                    lv[idx] = ctx.slot_dma(s.dma_start(
                        out=a["wpool"][:, (idx % 4) * 768:(idx % 4) * 768 + 768],
                        in_=wsl[d * 128:(d + 1) * 128, 0:768]), slot)

        @blk.scalar
        def _(sc):
            sc.wait_ge(ctx.sems["s_cc"], ctx.xn_cc_done)
            for g in range(8):
                r, sl = _owner(g)
                rr = 16 * (4 * b + r)
                for d in range(16):
                    idx = g * 16 + d
                    slot = idx % 4
                    if idx >= 4:
                        gp, dp = (idx - 4) // 16, (idx - 4) % 16
                        sc.wait_ge(ctx.sems["s_pe"], pe0 + gp * 24 + dp + 1)
                    wv[idx] = ctx.slot_dma(sc.dma_start(
                        out=a["xpool"][:, slot * 512:(slot + 1) * 512],
                        in_=xv[:, rr + d, sl * 512:(sl + 1) * 512]), slot)

        @blk.tensor
        def _(t):
            for g in range(8):
                for d in range(16):
                    idx = g * 16 + d
                    slot = idx % 4
                    ctx.slot_wait(t, slot, wv[idx])
                    ctx.slot_wait(t, 4 + idx % 4, lv[idx])
                    if d == 0 and g > 0:
                        t.wait_ge(ctx.sems["s_dve"], dve0 + g * 8)
                    rhs = a["xpool"][:, slot * 512:(slot + 1) * 512]
                    first, last = (d == 0), (d == 15)
                    w = a["wpool"][:, (idx % 4) * 768:(idx % 4) * 768 + 768]
                    t.matmul(a["ps_a"][:, 0:512], w[:, 0:128], rhs, start=first, stop=last)
                    t.matmul(a["ps_a"][:, 512:1024], w[:, 128:256], rhs, start=first, stop=last)
                    t.matmul(a["ps_a"][:, 1024:1536], w[:, 256:384], rhs, start=first, stop=last)
                    t.matmul(a["ps_a"][:, 1536:2048], w[:, 384:512], rhs, start=first, stop=last)
                    t.matmul(a["ps_b"][:, 0:512], w[:, 512:640], rhs, start=first, stop=last)
                    ctx.cinc(t.matmul(a["ps_b"][:, 512:1024], w[:, 640:768], rhs,
                                      start=first, stop=last), "s_pe")
                for j in range(4):
                    if j == 0:
                        t.wait_ge(ctx.sems["s_dve"], dve0 + g * 8 + 5)
                    ctx.cinc(t.transpose(a["ps_c"][:, j * 128:(j + 1) * 128],
                                         a["sc_a"][:, j * 128:(j + 1) * 128],
                                         a["identf"][:, :]), "s_pe")
                for j in range(4):
                    if j == 0:
                        t.wait_ge(ctx.sems["s_dve"], dve0 + g * 8 + 6)
                    ctx.cinc(t.transpose(a["ps_c"][:, 512 + j * 128: 512 + (j + 1) * 128],
                                         a["sc_b"][:, j * 128:(j + 1) * 128],
                                         a["identf"][:, :]), "s_pe")

        @blk.vector
        def _(v):
            for g in range(8):
                v.wait_ge(ctx.sems["s_pe"], pe0 + g * 24 + 16)
                gc = slice(g * 512, (g + 1) * 512)
                ctx.cinc(v.tensor_copy(ar[:, QO + g * 512:QO + (g + 1) * 512],
                                       a["ps_a"][:, 0:512]), "s_dve")
                ctx.cinc(v.tensor_copy(ar[:, KO + g * 512:KO + (g + 1) * 512],
                                       a["ps_a"][:, 1024:1536]), "s_dve")
                ctx.cinc(v.tensor_copy(ar[:, Q1 + g * 512:Q1 + (g + 1) * 512],
                                       a["ps_a"][:, 512:1024]), "s_dve")
                ctx.cinc(v.tensor_copy(ar[:, K1 + g * 512:K1 + (g + 1) * 512],
                                       a["ps_a"][:, 1536:2048]), "s_dve")
                ctx.cinc(v.tensor_copy(a["sc_a"][:, :], a["ps_b"][:, 0:512]), "s_dve")
                ctx.cinc(v.tensor_copy(a["sc_b"][:, :], a["ps_b"][:, 512:1024]), "s_dve")
                v.wait_ge(ctx.sems["s_pe"], pe0 + g * 24 + 20)
                ctx.cinc(v.tensor_copy(ar[:, VO + g * 512:VO + (g + 1) * 512],
                                       a["ps_c"][:, 0:512]), "s_dve")
                v.wait_ge(ctx.sems["s_pe"], pe0 + g * 24 + 24)
                ctx.cinc(v.tensor_copy(ar[:, V1 + g * 512:V1 + (g + 1) * 512],
                                       a["ps_c"][:, 512:1024]), "s_dve")


def p_quant1(ctx, blk):
    nc, a = ctx.nc, ctx.a
    arF = a["arena"].ap().bitcast(F32)
    vF = a["vtmp"].ap().bitcast(F32)
    if True:
        dve0 = [None]
        wv = {}

        @blk.gpsimd
        def _(g):
            p_wcast_issue(ctx, ("wq0",), g)
            p_wcast_trigger(ctx, ("wq0",), g)
            d0 = ctx.cnt["s_dve"]
            for m in range(3):
                for ch in range(2):
                    i = m * 2 + ch
                    if i >= 2:
                        g.wait_ge(ctx.sems["s_dve"], d0 + i - 1)
                    wv[i] = ctx.slot_dma(g.dma_start(
                        out=arF[:, 4096 + (i % 2) * 5632: 4096 + (i % 2) * 5632 + 5632],
                        in_=ctx.ffn0_sl[:, m * QSLF + ch * 5632: m * QSLF + (ch + 1) * 5632]),
                        6 + i % 2)
            p_wcast_issue(ctx, ("wo0", "w1t", "w3t", "wq1", "wo1", "w2t"), g)

        @blk.vector
        def _(v):
            for m in range(3):
                for ch in range(2):
                    i = m * 2 + ch
                    ctx.slot_wait(v, 6 + i % 2, wv[i])
                    ctx.cinc(v.tensor_reduce(vF[:, i:i + 1],
                                             arF[:, 4096 + (i % 2) * 5632: 4096 + (i % 2) * 5632 + 5632],
                                             AX.X, ALU.add, apply_absolute_value=True), "s_dve")
        dve_red = ctx.cnt["s_dve"]

        @blk.tensor
        def _(t):
            t.wait_ge(ctx.sems["s_dve"], dve_red)
            ctx.cinc(t.matmul(a["ps_c"][0:1, 0:6], a["fconst"][:, 0:1], vF[:, 0:6],
                              start=True, stop=True), "s_pe")
        pe_s = ctx.cnt["s_pe"]

        @blk.vector
        def _(v):
            v.wait_ge(ctx.sems["s_pe"], pe_s)
            ctx.cinc(v.tensor_reduce(
                a["xpool"].ap().bitcast(F32)[0:1, 0:3],
                a["ps_c"][0:1, 0:6].rearrange("p (m two) -> p m two", two=2),
                AX.X, ALU.add), "s_dve")
        dve_c = ctx.cnt["s_dve"]

        @blk.scalar
        def _(sc):
            sc.wait_ge(ctx.sems["s_dve"], dve_c)
            ctx.dinc(sc.dma_start(out=ctx.gm_in[:, :],
                                  in_=a["xpool"].ap().bitcast(F32)[0:1, 0:3]), "s_sc")
        sc_w = ctx.cnt["s_sc"]

        @blk.gpsimd
        def _(g):
            g.wait_ge(ctx.sems["s_sc"], sc_w)
            ctx.cinc(g.collective_compute(
                "AllGather", ALU.bypass, replica_groups=[list(range(8))],
                ins=[ctx.gm_in.ap().opt()], outs=[ctx.gm_out.ap().opt()]), "s_cc_q")
    ctx.gm_cc_done = ctx.cnt["s_cc_q"]


def p_quant2(ctx):
    nc, a = ctx.nc, ctx.a
    vF = a["vtmp"].ap().bitcast(F32)
    wF = a["wpool"].ap().bitcast(F32)
    N = float(H0 * D)
    with nc.Block() as blk:
        @blk.gpsimd
        def _(g):
            g.wait_ge(ctx.sems["s_cc_q"], ctx.gm_cc_done)
            ctx.dinc(g.dma_start(out=a["sc_a"][0:8, 0:3], in_=ctx.gm_out[:, :]), "s_gp")
        gp_l = ctx.cnt["s_gp"]

        @blk.tensor
        def _(t):
            t.wait_ge(ctx.sems["s_gp"], gp_l)
            ctx.cinc(t.matmul(a["ps_c"][0:1, 0:3], a["fconst"][0:8, 0:1],
                              a["sc_a"][0:8, 0:3], start=True, stop=True), "s_pe")
        pe_t = ctx.cnt["s_pe"]

        @blk.vector
        def _(v):
            v.wait_ge(ctx.sems["s_pe"], pe_t)
            ctx.cinc(v.tensor_scalar(a["xpool"].ap().bitcast(F32)[0:1, 4:7],
                                     a["ps_c"][0:1, 0:3], 1.0 / (2.0 * N), Q_EPS / 2.0,
                                     ALU.mult, ALU.add), "s_dve")
        act_t = ctx.cnt["s_dve"]

        @blk.tensor
        def _(t):
            t.wait_ge(ctx.sems["s_dve"], act_t)
            for m in range(3):
                ctx.cinc(t.matmul(a["ps_a"][:, m * 512:m * 512 + 1],
                                  a["fconst"][0:1, 0:128],
                                  a["xpool"].ap().bitcast(F32)[0:1, 4 + m:5 + m],
                                  start=True, stop=True), "s_pe")
        pe_b = ctx.cnt["s_pe"]

        @blk.vector
        def _(v):
            v.wait_ge(ctx.sems["s_pe"], pe_b)
            for m in range(3):
                v.tensor_copy(vF[:, 8 + m:9 + m], a["ps_a"][:, m * 512:m * 512 + 1])
                ctx.cinc(v.tensor_scalar_mul(vF[:, 12 + m:13 + m],
                                             a["ps_a"][:, m * 512:m * 512 + 1], -1.0), "s_dve")
    thr_done = ctx.cnt["s_dve"]

    dve0 = ctx.cnt["s_dve"]
    wv = {}
    sv = {}
    with nc.Block() as blk:
        @blk.sync
        def _(s):
            for m in range(3):
                for ch in range(11):
                    i = m * 11 + ch
                    if i >= 2:
                        s.wait_ge(ctx.sems["s_dve"], dve0 + (i - 2) * 3 + 3)
                    wv[i] = ctx.slot_dma(s.dma_start(
                        out=wF[:, (i % 2) * 1024:(i % 2) * 1024 + 1024],
                        in_=ctx.ffn0_sl[:, m * QSLF + ch * 1024: m * QSLF + (ch + 1) * 1024]),
                        i % 2)

        @blk.scalar
        def _(sc):
            for m in range(3):
                for ch in range(11):
                    i = m * 11 + ch
                    sc.wait_ge(ctx.sems["s_dve"], dve0 + i * 3 + 3)
                    sv[i] = ctx.slot_dma(sc.dma_start(
                        out=ctx.q_in[m][:, ch * 1024:(ch + 1) * 1024],
                        in_=a["hidout"][:, (i % 2) * 1024:(i % 2) * 1024 + 1024]), 2 + i % 2)
        sc_all = [None]

        @blk.vector
        def _(v):
            v.wait_ge(ctx.sems["s_dve"], thr_done)
            for m in range(3):
                for ch in range(11):
                    i = m * 11 + ch
                    ctx.slot_wait(v, i % 2, wv[i])
                    wbuf = wF[:, (i % 2) * 1024:(i % 2) * 1024 + 1024]
                    if i >= 2:
                        ctx.slot_wait(v, 2 + i % 2, sv[i - 2])
                    v.tensor_scalar(a["praw"][:, 0:1024], wbuf, vF[:, 8 + m:9 + m], None,
                                    ALU.is_gt)
                    v.tensor_scalar(a["pmask"][:, 0:1024], wbuf, vF[:, 12 + m:13 + m], None,
                                    ALU.is_lt)
                    ctx.cinc(v.tensor_sub(a["hidout"][:, (i % 2) * 1024:(i % 2) * 1024 + 1024],
                                          a["praw"][:, 0:1024], a["pmask"][:, 0:1024]),
                             "s_dve", 3)

        @blk.gpsimd
        def _(g):
            g.wait_ge(ctx.sems["q3"], sv[31])
            g.wait_ge(ctx.sems["q2"], sv[32])
            for m in range(3):
                ctx.cinc(g.collective_compute(
                    "AllGather", ALU.bypass, replica_groups=[list(range(8))],
                    ins=[ctx.q_in[m].ap().opt()], outs=[ctx.q_out[m].ap().opt()]), "s_cc_q")
    ctx.q_cc_done = ctx.cnt["s_cc_q"]


def p_ffn0(ctx, hf):
    nc, a = ctx.nc, ctx.a
    ar = a["arena"]
    q1 = ctx.q_out[0].ap().rearrange("p n -> (p n)").rearrange("(d h) -> d h", h=H0)
    q2 = ctx.q_out[1].ap().rearrange("p n -> (p n)").rearrange("(d h) -> d h", h=H0)
    q3 = ctx.q_out[2].ap().rearrange("p n -> (p n)").rearrange("(h d) -> h d", d=D)

    hts = [hf * 22 + i for i in range(22)]
    pe0 = ctx.cnt["s_pe"]
    act0 = ctx.cnt["s_act"]
    dve0 = ctx.cnt["s_dve"]
    st3_pre = ctx.cnt["q3"]
    st7_pre = ctx.cnt["q7"]
    wv = {}
    sgm = [a["scr"].ap()[:, 0:1024], a["hidout"].ap()[:, 0:1024]]
    with nc.Block() as blk:
        if hf == 0:
            @blk.gpsimd
            def _(g):
                p_wcast_trigger(ctx, ("wq1", "wo0", "w1t", "w3t", "wo1", "w2t"), g)

        @blk.sync
        def _(s):
            s.wait_ge(ctx.sems["s_cc_q"], ctx.q_cc_done)
            for i, ht in enumerate(hts):
                for dh in range(2):
                    step = i * 2 + dh
                    slot = step % 4
                    if step >= 4:
                        s.wait_ge(ctx.sems["s_pe"], pe0 + step - 3)
                    for mi, qm in enumerate((q1, q2)):
                        src = qm.rearrange("(dt p) h -> p dt h", p=128)
                        dst = a["wpool"][:, slot * 2048 + mi * 1024:
                                         slot * 2048 + (mi + 1) * 1024]
                        wv[step] = ctx.slot_dma(s.dma_start(
                            out=dst.rearrange("p (a b) -> p a b", b=128),
                            in_=src[:, dh * 8:(dh + 1) * 8, ht * 128:(ht + 1) * 128]), slot)

        @blk.tensor
        def _(t):
            for i, ht in enumerate(hts):
                psx1 = a["ps_a"].ap()[:, 0:1024] if i % 2 == 0 else a["ps_b"].ap()[:, :]
                psx2 = a["ps_a"].ap()[:, 1024:2048] if i % 2 == 0 else a["ps_c"].ap()[:, :]
                if i >= 2:
                    t.wait_ge(ctx.sems["s_dve"], dve0 + i - 1)
                for dh in range(2):
                    step = i * 2 + dh
                    slot = step % 4
                    ctx.slot_wait(t, slot, wv[step])
                    for mi in range(2):
                        psd = psx1 if mi == 0 else psx2
                        for d8 in range(8):
                            d = dh * 8 + d8
                            w = a["wpool"][:, slot * 2048 + mi * 1024 + d8 * 128:
                                           slot * 2048 + mi * 1024 + (d8 + 1) * 128]
                            for th in range(2):
                                ii = t.matmul(psd[:, th * 512:(th + 1) * 512], w,
                                              a["xn"][:, d * TOK + th * 512: d * TOK + (th + 1) * 512],
                                              start=(d == 0), stop=(d == 15))
                    ctx.cinc(ii, "s_pe")

        @blk.scalar
        def _(sc):
            sc.wait_ge(ctx.sems["q3"], st3_pre)
            sc.wait_ge(ctx.sems["q7"], st7_pre)
            for i, ht in enumerate(hts):
                psx1 = a["ps_a"].ap()[:, 0:1024] if i % 2 == 0 else a["ps_b"].ap()[:, :]
                sc.wait_ge(ctx.sems["s_pe"], pe0 + i * 2 + 2)
                if i >= 2:
                    sc.wait_ge(ctx.sems["s_dve"], dve0 + i - 1)
                ctx.cinc(sc.activation(sgm[i % 2], psx1, AF.Sigmoid), "s_act")

        @blk.vector
        def _(v):
            for i, ht in enumerate(hts):
                psx2 = a["ps_a"].ap()[:, 1024:2048] if i % 2 == 0 else a["ps_c"].ap()[:, :]
                v.wait_ge(ctx.sems["s_act"], act0 + i + 1)
                ctx.cinc(v.tensor_tensor(ar[:, i * 1024:(i + 1) * 1024], sgm[i % 2],
                                         psx2, ALU.mult), "s_dve")

    pe1 = ctx.cnt["s_pe"]
    dve1 = ctx.cnt["s_dve"]
    wv3 = {}
    with nc.Block() as blk:
        @blk.sync
        def _(s):
            for dgrp in range(4):
                for i, ht in enumerate(hts):
                    step = dgrp * 22 + i
                    slot = step % 8
                    if step >= 8:
                        s.wait_ge(ctx.sems["s_pe"], pe1 + step - 7)
                    wv3[step] = ctx.slot_dma(s.dma_start(
                        out=a["wpool"][:, slot * 1024:slot * 1024 + 512],
                        in_=q3[ht * 128:(ht + 1) * 128, dgrp * 512:(dgrp + 1) * 512]), slot)

        @blk.tensor
        def _(t):
            for dgrp in range(4):
                if dgrp > 0:
                    t.wait_ge(ctx.sems["s_dve"], dve1 + dgrp * 8)
                for i, ht in enumerate(hts):
                    step = dgrp * 22 + i
                    slot = step % 8
                    ctx.slot_wait(t, slot, wv3[step])
                    for do in range(4):
                        w = a["wpool"][:, slot * 1024 + do * 128: slot * 1024 + (do + 1) * 128]
                        for th in range(2):
                            psd = (a["ps_a"].ap()[:, do * 512:(do + 1) * 512] if th == 0 else
                                   (a["ps_b"].ap()[:, do % 2 * 512:do % 2 * 512 + 512] if do < 2 else
                                    a["ps_c"].ap()[:, (do - 2) % 2 * 512:(do - 2) % 2 * 512 + 512]))
                            ii = t.matmul(psd, w,
                                          ar[:, i * 1024 + th * 512: i * 1024 + (th + 1) * 512],
                                          start=(i == 0), stop=(i == 21))
                    ctx.cinc(ii, "s_pe")

        @blk.vector
        def _(v):
            for dgrp in range(4):
                v.wait_ge(ctx.sems["s_pe"], pe1 + dgrp * 22 + 22)
                for do in range(4):
                    dt_ = dgrp * 4 + do
                    for th in range(2):
                        psd = (a["ps_a"].ap()[:, do * 512:(do + 1) * 512] if th == 0 else
                               (a["ps_b"].ap()[:, do % 2 * 512:do % 2 * 512 + 512] if do < 2 else
                                a["ps_c"].ap()[:, (do - 2) % 2 * 512:(do - 2) % 2 * 512 + 512]))
                        xt = a["x_fm"][:, dt_ * TOK + th * 512: dt_ * TOK + (th + 1) * 512]
                        ctx.cinc(v.tensor_tensor(xt, xt, psd, ALU.add), "s_dve")


def _unit_plan(layer, qb):
    if layer == 0:
        kts = []
        if qb > 0:
            kts += [(4 * (qb - 1) + j, MO + j * 512) for j in range(4)]
        kts += [(4 * qb + j, MO + 2048 + j * 512) for j in range(4)]
    else:
        kts = [(j, None) for j in range(4 * qb)]
        kts += [(4 * qb + j, MO + 2048 + j * 512) for j in range(4)]
    return [(kts[2 * i], kts[2 * i + 1]) for i in range(len(kts) // 2)]


def p_attn_units(ctx, layer, b):
    nc, a = ctx.nc, ctx.a
    ar = a["arena"]
    uoff = [(QO, KO, VO), (Q1, K1, V1)]

    pe0 = ctx.cnt["s_pe"]
    act0 = ctx.cnt["s_act"]
    dve0 = ctx.cnt["s_dve"]
    stv = {}
    st3_0 = ctx.cnt["q3"]
    st7_0 = ctx.cnt["q7"]

    nqb = ctx.cfg.get("nqb", 8)
    pairs = []
    qb_of = []
    qb_last, qb_first = {}, {}
    for qb in range(nqb):
        pl = _unit_plan(layer, qb)
        for pi, (p1, p2) in enumerate(pl):
            pairs.append(dict(p1=p1, p2=p2, qb=qb, masked=p1[1] is not None))
            qb_of.append(qb)
        qb_last[qb] = len(pairs) - 1
    n = len(pairs)
    for j in range(n - 1, -1, -1):
        qb_first[qb_of[j]] = j

    events = []
    for j in range(n):
        for u in range(2):
            events.append(("S", j, u))
        if j >= 1:
            for u in range(2):
                events.append(("PV", j - 1, u))
            if qb_last[qb_of[j - 1]] == j - 1:
                for u in range(2):
                    events.append(("EPI", qb_of[j - 1], u))
    for u in range(2):
        events.append(("PV", n - 1, u))
    for u in range(2):
        events.append(("EPI", qb_of[n - 1], u))

    s_after, pv_after, bc_after = {}, {}, {}
    cnt = 0
    for kind, idx, u in events:
        cnt += 1
        (s_after if kind == "S" else pv_after if kind == "PV" else bc_after)[(idx, u)] = cnt

    e_after = {}
    ac = 0
    for j in range(n):
        for u in range(2):
            ac += 1
            e_after[(j, u)] = ac
    m_after, recip_after, bcev_after, mult_after = {}, {}, {}, {}
    dv = 0
    for j in range(n):
        for u in range(2):
            if pairs[j]["masked"]:
                dv += 1
            m_after[(j, u)] = dv
        if qb_last[qb_of[j]] == j:
            qb = qb_of[j]
            for u in range(2):
                recip_after[(qb, u)] = dv + 1
                bcev_after[(qb, u)] = dv + 2
                mult_after[(qb, u)] = dv + 3
                dv += 3

    bcs = [a["bc_sb"].ap()[:, 0:512], a["hidout"].ap()[:, 0:512]]
    ebuf = lambda u, j: (a["praw"] if j % 2 == 0 else a["pmask"]).ap()[:, u * 1024:(u + 1) * 1024]
    mbuf = lambda u: a["pextra"].ap()[:, u * 1024:(u + 1) * 1024]
    prevm = {}
    pm = None
    for j in range(n):
        prevm[j] = pm
        if pairs[j]["masked"]:
            pm = j

    with nc.Block() as blk:
        @blk.tensor
        def _(t):
            for kind, idx, u in events:
                qoff, koff, voff = uoff[u]
                if kind == "S":
                    j = idx
                    pr = pairs[j]
                    if j >= 1:
                        t.wait_ge(ctx.sems["s_act"], act0 + e_after[(j - 1, u)])
                    qcols = slice(qoff + pr["qb"] * 512, qoff + (pr["qb"] + 1) * 512)
                    for jj, (kt, _m) in enumerate((pr["p1"], pr["p2"])):
                        ii = t.matmul(a["ps_a"][:, u * 1024 + jj * 512: u * 1024 + (jj + 1) * 512],
                                      ar[:, koff + kt * 128:koff + (kt + 1) * 128],
                                      ar[:, qcols], start=True, stop=True)
                    ctx.cinc(ii, "s_pe")
                elif kind == "PV":
                    j = idx
                    pr = pairs[j]
                    qb = pr["qb"]
                    psO = a["ps_b"][:, u * 512:(u + 1) * 512]
                    psD = a["ps_c"][0:1, u * 512:(u + 1) * 512]
                    if pr["masked"]:
                        t.wait_ge(ctx.sems["s_dve"], dve0 + m_after[(j, u)])
                    else:
                        t.wait_ge(ctx.sems["s_act"], act0 + e_after[(j, u)])
                    if qb_first[qb] == j and qb >= 1:
                        t.wait_ge(ctx.sems["s_dve"], dve0 + mult_after[(qb - 1, u)])
                    srcb = mbuf(u) if pr["masked"] else ebuf(u, j)
                    for jj, (kt, _m) in enumerate((pr["p1"], pr["p2"])):
                        pp = srcb[:, jj * 512:(jj + 1) * 512]
                        t.matmul(psO, ar[:, voff + kt * 128:voff + (kt + 1) * 128], pp,
                                 start=(qb_first[qb] == j and jj == 0), stop=False)
                        ii = t.matmul(psD, a["consts"][:, 320:321], pp,
                                      start=(qb_first[qb] == j and jj == 0), stop=False)
                    ctx.cinc(ii, "s_pe")
                else:
                    qb = idx
                    t.wait_ge(ctx.sems["s_dve"], dve0 + recip_after[(qb, u)])
                    psDfull = a["ps_c"][:, u * 512:(u + 1) * 512]
                    ctx.cinc(t.matmul(psDfull, a["consts"][0:1, 320:448],
                                      a["vtmp"][0:1, u * 512:(u + 1) * 512],
                                      start=True, stop=True), "s_pe")

        @blk.scalar
        def _(sc):
            for j in range(n):
                pr = pairs[j]
                qb = pr["qb"]
                for u in range(2):
                    sc.wait_ge(ctx.sems["s_pe"], pe0 + s_after[(j, u)])
                    if j >= 2 and pairs[j - 2]["masked"]:
                        sc.wait_ge(ctx.sems["s_dve"], dve0 + m_after[(j - 2, u)])
                    ctx.cinc(sc.activation(ebuf(u, j),
                                           a["ps_a"][:, u * 1024:(u + 1) * 1024],
                                           AF.Exp, scale=SCALE), "s_act")
                if qb_last[qb] == j:
                    r, sl = _owner(qb)
                    dst_c = 1024 * (4 * b + r) + 512 * sl
                    for u in range(2):
                        sc.wait_ge(ctx.sems["s_dve"], dve0 + mult_after[(qb, u)])
                        stv[(qb, u)] = ctx.slot_dma(sc.dma_start(
                            out=ctx.attn_ag_in[u * 128:(u + 1) * 128, dst_c:dst_c + 512],
                            in_=a["scr"][:, u * 512:(u + 1) * 512]), 7 if u else 3)

        @blk.vector
        def _(v):
            v.wait_ge(ctx.sems["s_ld"], ctx.qkv_ld_batch)
            for j in range(n):
                pr = pairs[j]
                qb = pr["qb"]
                if pr["masked"]:
                    moff = pr["p1"][1]
                    for u in range(2):
                        v.wait_ge(ctx.sems["s_act"], act0 + e_after[(j, u)])
                        if prevm[j] is not None:
                            v.wait_ge(ctx.sems["s_pe"], pe0 + pv_after[(prevm[j], u)])
                        ctx.cinc(v.tensor_tensor(mbuf(u), ebuf(u, j),
                                                 ar[:, moff:moff + 1024], ALU.mult), "s_dve")
                if qb_last[qb] != j:
                    continue
                for u in range(2):
                    psD = a["ps_c"][0:1, u * 512:(u + 1) * 512]
                    psDfull = a["ps_c"][:, u * 512:(u + 1) * 512]
                    psO = a["ps_b"][:, u * 512:(u + 1) * 512]
                    rrow32 = a["xpool"].ap().bitcast(F32)[0:1, u * 512:(u + 1) * 512]
                    rrow16 = a["vtmp"][0:1, u * 512:(u + 1) * 512]
                    v.wait_ge(ctx.sems["s_pe"], pe0 + pv_after[(j, u)])
                    v.reciprocal(rrow32, psD)
                    ctx.cinc(v.tensor_copy(rrow16, rrow32), "s_dve")
                    v.wait_ge(ctx.sems["s_pe"], pe0 + bc_after[(qb, u)])
                    ctx.cinc(v.tensor_copy(bcs[u], psDfull), "s_dve")
                    if qb >= 1:
                        ctx.slot_wait(v, 7 if u else 3, stv[(qb - 1, u)])
                    elif qb == 0:
                        v.wait_ge(ctx.sems["q7" if u else "q3"], st7_0 if u else st3_0)
                    ctx.cinc(v.tensor_tensor(a["scr"][:, u * 512:(u + 1) * 512],
                                             psO, bcs[u], ALU.mult), "s_dve")


def p_attn_ag(ctx):
    nc = ctx.nc
    w3, w7 = ctx.cnt["q3"], ctx.cnt["q7"]
    with nc.Block() as blk:
        @blk.gpsimd
        def _(g):
            g.wait_ge(ctx.sems["q3"], w3)
            g.wait_ge(ctx.sems["q7"], w7)
            ctx.cinc(g.collective_compute(
                "AllGather", ALU.bypass,
                replica_groups=[list(range(8))],
                ins=[ctx.attn_ag_in.ap().opt()], outs=[ctx.attn_ag_out.ap().opt()]), "s_cc_at")
    ctx.attn_cc_done = ctx.cnt["s_cc_at"]


def p_proj(ctx, layer):
    nc, a = ctx.nc, ctx.a
    ar = a["arena"]
    wo = ctx.wo_bf[layer]
    av = ctx.attn_ag_out.ap().rearrange("(t p) n -> p t n", p=128)

    for tb in range(2):
        sc0 = ctx.cnt["s_sc"]
        pe0 = ctx.cnt["s_pe"]
        dve0 = ctx.cnt["s_dve"]
        wv = {}
        with nc.Block() as blk:
            @blk.scalar
            def _(sc):
                sc.wait_ge(ctx.sems["s_cc_at"], ctx.attn_cc_done)
                pid = sc.partition_id()
                dst = ar[:, RHS:RHS + 8192].rearrange("p (t n) -> p t n", n=512)
                ctx.dinc(sc.dma_start(
                    out=dst, in_=av[:, :, bass.ds(pid * 1024 + tb * 512, 512)]), "s_sc")

            @blk.sync
            def _(s):
                s.wait_ge(ctx.sems[ctx.wready["wo%d" % layer][0]], ctx.wready["wo%d" % layer][1])
                for dgrp in range(4):
                    for din in range(16):
                        i = dgrp * 16 + din
                        slot = i % 8
                        if i >= 8:
                            s.wait_ge(ctx.sems["s_pe"], pe0 + (i - 8) + 1)
                        wv[i] = ctx.slot_dma(s.dma_start(
                            out=a["wpool"][:, slot * 1024:slot * 1024 + 512],
                            in_=wo[din * 128:(din + 1) * 128, dgrp * 512:(dgrp + 1) * 512]), slot)

            @blk.tensor
            def _(t):
                t.wait_ge(ctx.sems["s_sc"], sc0 + 16)
                for dgrp in range(4):
                    if dgrp > 0:
                        t.wait_ge(ctx.sems["s_dve"], dve0 + dgrp * 4)
                    for din in range(16):
                        i = dgrp * 16 + din
                        slot = i % 8
                        ctx.slot_wait(t, slot, wv[i])
                        rhs = ar[:, RHS + din * 512: RHS + (din + 1) * 512]
                        for do in range(4):
                            ii = t.matmul(a["ps_a"][:, do * 512:(do + 1) * 512],
                                          a["wpool"][:, slot * 1024 + do * 128: slot * 1024 + (do + 1) * 128],
                                          rhs, start=(din == 0), stop=(din == 15))
                        ctx.cinc(ii, "s_pe")

            @blk.vector
            def _(v):
                for dgrp in range(4):
                    v.wait_ge(ctx.sems["s_pe"], pe0 + dgrp * 16 + 16)
                    for do in range(4):
                        dt_ = dgrp * 4 + do
                        xt = a["x_fm"][:, dt_ * TOK + tb * 512: dt_ * TOK + (tb + 1) * 512]
                        ctx.cinc(v.tensor_tensor(xt, xt, a["ps_a"][:, do * 512:(do + 1) * 512],
                                                 ALU.add), "s_dve")


def p_ffn1a(ctx):
    nc, a = ctx.nc, ctx.a
    w1 = ctx.w1t_bf.ap().rearrange("(dt p) h -> p dt h", p=128)
    w3 = ctx.w3t_bf.ap().rearrange("(dt p) h -> p dt h", p=128)
    hv = ctx.hid_dram.ap().rearrange("(t p) n -> t p n", p=128)

    pe0 = ctx.cnt["s_pe"]
    act0 = ctx.cnt["s_act"]
    dve0 = ctx.cnt["s_dve"]
    b4, b5 = ctx.cnt["q4"], ctx.cnt["q5"]
    sv = {i: (b4 if i % 2 == 0 else b5) + (i // 2 + 1) * 16 for i in range(64)}
    wv = {}
    with nc.Block() as blk:
        @blk.sync
        def _(s):
            s.wait_ge(ctx.sems[ctx.wready["w1t"][0]], ctx.wready["w1t"][1])
            s.wait_ge(ctx.sems[ctx.wready["w3t"][0]], ctx.wready["w3t"][1])
            for i in range(64):
                for dh in range(2):
                    step = i * 2 + dh
                    slot = step % 4
                    if step >= 4:
                        s.wait_ge(ctx.sems["s_pe"], pe0 + step - 3)
                    for mi, qm in enumerate((w1, w3)):
                        dst = a["wpool"][:, slot * 2048 + mi * 1024:
                                         slot * 2048 + (mi + 1) * 1024]
                        wv[step] = ctx.slot_dma(s.dma_start(
                            out=dst.rearrange("p (x b) -> p x b", b=128),
                            in_=qm[:, dh * 8:(dh + 1) * 8, i * 128:(i + 1) * 128]), slot)

        @blk.scalar
        def _(sc):
            for i in range(64):
                psx1 = a["ps_a"].ap()[:, 0:1024] if i % 2 == 0 else a["ps_b"].ap()[:, :]
                sc.wait_ge(ctx.sems["s_pe"], pe0 + i * 2 + 2)
                if i >= 2:
                    sc.wait_ge(ctx.sems["s_dve"], dve0 + i - 1)
                ctx.cinc(sc.activation(a["scr"][:, (i % 2) * 512:(i % 2) * 512 + 512],
                                       psx1[:, 0:512], AF.Silu), "s_act")
                ctx.cinc(sc.activation(a["vtmp"][:, (i % 2) * 512:(i % 2) * 512 + 512],
                                       psx1[:, 512:1024], AF.Silu), "s_act")

        @blk.tensor
        def _(t):
            for i in range(64):
                psx1 = a["ps_a"].ap()[:, 0:1024] if i % 2 == 0 else a["ps_b"].ap()[:, :]
                psx2 = a["ps_a"].ap()[:, 1024:2048] if i % 2 == 0 else a["ps_c"].ap()[:, :]
                if i >= 2:
                    t.wait_ge(ctx.sems["s_dve"], dve0 + i - 1)
                for dh in range(2):
                    step = i * 2 + dh
                    slot = step % 4
                    ctx.slot_wait(t, slot, wv[step])
                    for mi in range(2):
                        psd = psx1 if mi == 0 else psx2
                        for d8 in range(8):
                            d = dh * 8 + d8
                            w = a["wpool"][:, slot * 2048 + mi * 1024 + d8 * 128:
                                           slot * 2048 + mi * 1024 + (d8 + 1) * 128]
                            for th in range(2):
                                ii = t.matmul(psd[:, th * 512:(th + 1) * 512], w,
                                              a["xn"][:, d * TOK + th * 512: d * TOK + (th + 1) * 512],
                                              start=(d == 0), stop=(d == 15))
                    ctx.cinc(ii, "s_pe")

        @blk.vector
        def _(v):
            for i in range(64):
                psx2 = a["ps_a"].ap()[:, 1024:2048] if i % 2 == 0 else a["ps_c"].ap()[:, :]
                v.wait_ge(ctx.sems["s_act"], act0 + i * 2 + 2)
                if i >= 2:
                    ctx.slot_wait(v, 4 + i % 2, sv[i - 2])
                v.tensor_tensor(a["hidout"][:, (i % 2) * 1024:(i % 2) * 1024 + 512],
                                a["scr"][:, (i % 2) * 512:(i % 2) * 512 + 512],
                                psx2[:, 0:512], ALU.mult)
                ctx.cinc(v.tensor_tensor(
                    a["hidout"][:, (i % 2) * 1024 + 512:(i % 2) * 1024 + 1024],
                    a["vtmp"][:, (i % 2) * 512:(i % 2) * 512 + 512],
                    psx2[:, 512:1024], ALU.mult), "s_dve")

        @blk.gpsimd
        def _(g):
            for i in range(64):
                g.wait_ge(ctx.sems["s_dve"], dve0 + i + 1)
                w = ctx.slot_dma(g.dma_start(
                    out=hv[i, :, :],
                    in_=a["hidout"][:, (i % 2) * 1024:(i % 2) * 1024 + 1024]), 4 + i % 2)
                assert w == sv[i], (w, sv[i])


def p_fwht_b(ctx):
    nc, a = ctx.nc, ctx.a
    ar = a["arena"]
    hv = ctx.hid_dram.ap().rearrange("(t p) n -> t p n", p=128)
    bv = ctx.hidB_dram.ap().rearrange("(t p) n -> t p n", p=128)
    h64 = a["consts"][0:64, 256:320]
    pe0 = ctx.cnt["s_pe"]
    dve0 = ctx.cnt["s_dve"]
    hw4, hw5 = ctx.cnt["q4"], ctx.cnt["q5"]
    b2, b3 = ctx.cnt["q2"], ctx.cnt["q3"]
    sv = {c: (b2 if c % 2 == 0 else b3) + (c // 2 + 1) * 16 for c in range(16)}
    wv = {}
    with nc.Block() as blk:
        @blk.sync
        def _(s):
            s.wait_ge(ctx.sems["q4"], hw4)
            s.wait_ge(ctx.sems["q5"], hw5)
            for c in range(16):
                slot = c % 2
                if c >= 2:
                    s.wait_ge(ctx.sems["s_pe"], pe0 + (c - 1) * 16)
                wv[c] = ctx.slot_dma(s.dma_start(
                    out=ar[0:64, slot * 8192:(slot + 1) * 8192].rearrange(
                        "t (p n) -> t p n", n=1024),
                    in_=hv[:, c * 8:(c + 1) * 8, :]), slot)

        @blk.tensor
        def _(t):
            for c in range(16):
                slot = c % 2
                ctx.slot_wait(t, slot, wv[c])
                for j in range(16):
                    psd = a["ps_b"].ap() if j % 2 == 0 else a["ps_c"].ap()
                    if c * 16 + j >= 2:
                        t.wait_ge(ctx.sems["s_dve"], dve0 + c * 16 + j - 1)
                    ctx.cinc(t.matmul(psd[0:64, 0:512], h64,
                                      ar[0:64, slot * 8192 + j * 512: slot * 8192 + (j + 1) * 512],
                                      start=True, stop=True), "s_pe")

        @blk.vector
        def _(v):
            for c in range(16):
                oslot = c % 2
                if c >= 2:
                    ctx.slot_wait(v, 2 + oslot, sv[c - 2])
                for j in range(16):
                    psd = a["ps_b"].ap() if j % 2 == 0 else a["ps_c"].ap()
                    v.wait_ge(ctx.sems["s_pe"], pe0 + c * 16 + j + 1)
                    ctx.cinc(v.tensor_copy(
                        ar[0:64, 16384 + oslot * 8192 + j * 512: 16384 + oslot * 8192 + (j + 1) * 512],
                        psd[0:64, 0:512]), "s_dve")

        @blk.scalar
        def _(sc):
            for c in range(16):
                oslot = c % 2
                sc.wait_ge(ctx.sems["s_dve"], dve0 + c * 16 + 16)
                w = ctx.slot_dma(sc.dma_start(
                    out=bv[:, c * 8:(c + 1) * 8, :],
                    in_=ar[0:64, 16384 + oslot * 8192: 16384 + (oslot + 1) * 8192].rearrange(
                        "t (p n) -> t p n", n=1024)), 2 + oslot)
                assert w == sv[c], (w, sv[c])


def p_w2a(ctx):
    nc, a = ctx.nc, ctx.a
    ar = a["arena"]
    bv = ctx.hidB_dram.ap().rearrange("(t p) n -> t p n", p=128)
    w2 = ctx.w2t_bf
    h128 = a["consts"][:, 128:256]
    for tb in range(2):
        pe0 = ctx.cnt["s_pe"]
        dve0 = ctx.cnt["s_dve"]
        wv, wv2 = {}, {}
        with nc.Block() as blk:
            @blk.scalar
            def _(sc):
                if tb == 0:
                    sc.wait_ge(ctx.sems[ctx.wready["w2t"][0]], ctx.wready["w2t"][1])
                for T in range(64):
                    slot = T % 2
                    if T >= 2:
                        sc.wait_ge(ctx.sems["s_pe"], pe0 + T - 1)
                    wv[T] = ctx.slot_dma(sc.dma_start(
                        out=a["xpool"][:, slot * 1024: slot * 1024 + 512],
                        in_=bv[T, :, :][:, tb * 512:(tb + 1) * 512]), slot)

            @blk.tensor
            def _(t):
                for T in range(64):
                    slot = T % 2
                    psd = a["ps_b"].ap() if T % 2 == 0 else a["ps_c"].ap()
                    ctx.slot_wait(t, slot, wv[T])
                    if T >= 2:
                        t.wait_ge(ctx.sems["s_dve"], dve0 + T - 1)
                    ctx.cinc(t.matmul(psd[:, 0:512], h128,
                                      a["xpool"][:, slot * 1024: slot * 1024 + 512],
                                      start=True, stop=True), "s_pe")

            @blk.vector
            def _(v):
                v.wait_ge(ctx.sems["q2"], ctx.cnt["q2"])
                v.wait_ge(ctx.sems["q3"], ctx.cnt["q3"])
                for T in range(64):
                    psd = a["ps_b"].ap() if T % 2 == 0 else a["ps_c"].ap()
                    v.wait_ge(ctx.sems["s_pe"], pe0 + T + 1)
                    ctx.cinc(v.tensor_scalar_mul(ar[:, T * 512:(T + 1) * 512],
                                                 psd[:, 0:512], FW_SC), "s_dve")

        pe1 = ctx.cnt["s_pe"]
        dve1 = ctx.cnt["s_dve"]
        with nc.Block() as blk:
            @blk.sync
            def _(s):
                for dgrp in range(4):
                    for T in range(64):
                        step = dgrp * 64 + T
                        slot = step % 8
                        if step >= 8:
                            s.wait_ge(ctx.sems["s_pe"], pe1 + step - 7)
                        wv2[step] = ctx.slot_dma(s.dma_start(
                            out=a["wpool"][:, slot * 1024:slot * 1024 + 512],
                            in_=w2[T * 128:(T + 1) * 128, dgrp * 512:(dgrp + 1) * 512]), slot)

            @blk.tensor
            def _(t):
                for dgrp in range(4):
                    if dgrp > 0:
                        t.wait_ge(ctx.sems["s_dve"], dve1 + dgrp * 4)
                    for T in range(64):
                        step = dgrp * 64 + T
                        slot = step % 8
                        ctx.slot_wait(t, slot, wv2[step])
                        for do in range(4):
                            ii = t.matmul(a["ps_a"][:, do * 512:(do + 1) * 512],
                                          a["wpool"][:, slot * 1024 + do * 128:
                                                      slot * 1024 + (do + 1) * 128],
                                          ar[:, T * 512:(T + 1) * 512],
                                          start=(T == 0), stop=(T == 63))
                        ctx.cinc(ii, "s_pe")

            @blk.vector
            def _(v):
                for dgrp in range(4):
                    v.wait_ge(ctx.sems["s_pe"], pe1 + dgrp * 64 + 64)
                    for do in range(4):
                        dt_ = dgrp * 4 + do
                        xt = a["x_fm"][:, dt_ * TOK + tb * 512: dt_ * TOK + (tb + 1) * 512]
                        ctx.cinc(v.tensor_tensor(xt, xt, a["ps_a"][:, do * 512:(do + 1) * 512],
                                                 ALU.add), "s_dve")


def p_store(ctx):
    nc, a = ctx.nc, ctx.a
    arF = a["arena"].ap().bitcast(F32)
    pe0 = ctx.cnt["s_pe"]
    dve0 = ctx.cnt["s_dve"]
    b0, b1 = ctx.cnt["q0"], ctx.cnt["q1"]
    sv = {tt: (b0 if tt % 2 == 0 else b1) + (tt // 2 + 1) * 16 for tt in range(8)}
    with nc.Block() as blk:
        @blk.tensor
        def _(t):
            for tt in range(8):
                for d in range(16):
                    j = tt * 16 + d
                    dst = a["ps_b"][:, (d % 2) * 512:(d % 2) * 512 + 128]
                    if j >= 2:
                        t.wait_ge(ctx.sems["s_dve"], dve0 + j - 1)
                    ctx.cinc(t.transpose(
                        dst, a["x_fm"][:, d * TOK + tt * 128: d * TOK + (tt + 1) * 128],
                        a["identf"][:, :]), "s_pe")

        @blk.vector
        def _(v):
            for tt in range(8):
                oslot = tt % 2
                if tt >= 2:
                    ctx.slot_wait(v, oslot, sv[tt - 2])
                for d in range(16):
                    j = tt * 16 + d
                    src = a["ps_b"][:, (d % 2) * 512:(d % 2) * 512 + 128]
                    v.wait_ge(ctx.sems["s_pe"], pe0 + j + 1)
                    ctx.cinc(v.tensor_copy(
                        arF[:, oslot * 2048 + d * 128: oslot * 2048 + (d + 1) * 128], src),
                        "s_dve")

        @blk.sync
        def _(s):
            for tt in range(8):
                oslot = tt % 2
                s.wait_ge(ctx.sems["s_dve"], dve0 + tt * 16 + 16)
                w = ctx.slot_dma(s.dma_start(
                    out=ctx.out_own[tt * 128:(tt + 1) * 128, :],
                    in_=arF[:, oslot * 2048:(oslot + 1) * 2048]), oslot)
                assert w == sv[tt], (w, sv[tt])
            s.wait_ge(ctx.sems["q0"], sv[6])
            s.wait_ge(ctx.sems["q1"], sv[7])


def build(cfg):
    nc = bass.Bass("TRN2", target_bir_lowering=False, debug=False)
    ctx = Ctx(nc, cfg)

    ctx.x_in = nc.dram_tensor("x_own", [TOK, D], F32, kind="ExternalInput")

    ctx.identf_dram = nc.inline_tensor(np.eye(128, dtype=np.float32), "identf_c")
    consts = np.zeros((128, 768), dtype=ml_dtypes.bfloat16)
    consts[:, 0:128] = np.eye(128, dtype=ml_dtypes.bfloat16)
    consts[:, 128:256] = _hadamard(128).astype(ml_dtypes.bfloat16)
    consts[0:64, 256:320] = _hadamard(64).astype(ml_dtypes.bfloat16)
    consts[:, 320:448] = 1.0
    ctx.consts_dram = nc.inline_tensor(consts, "consts_c")
    ctx.masks_dram = nc.inline_tensor(_masks_np(), "masks_c")
    fconst = np.zeros((128, 136), dtype=np.float32)
    fconst[:, 0:128] = 1.0
    fconst[:, 128] = LN_EPS
    fconst[:, 129] = Q_EPS / 2.0
    ctx.fconst_dram = nc.inline_tensor(fconst, "fconst_c")

    ctx.xn_ag_in = nc.dram_tensor("xn_ag_in", [D, TOK], F16, kind="Internal")
    ctx.xn_ag_out = nc.dram_tensor("xn_ag_out", [8 * D, TOK], F16, kind="Internal")
    ctx.attn_ag_in = nc.dram_tensor("attn_ag_in", [256, 8192], F16, kind="Internal")
    ctx.attn_ag_out = nc.dram_tensor("attn_ag_out", [D, 8192], F16, kind="Internal")
    ctx.kvspill = nc.dram_tensor("kvspill", [768, 4096], F16, kind="Internal")

    ctx.wq_in = [nc.dram_tensor("l%d_qkv_sl" % l, [D, 768], F32, kind="ExternalInput")
                 for l in range(2)]
    ctx.wo_in = [nc.dram_tensor("l%d_wo" % l, [D, D], F32, kind="ExternalInput")
                 for l in range(2)]
    ctx.wq_bf = [nc.dram_tensor("l%d_qkv_bf" % l, [D, 768], F16, kind="Internal")
                 for l in range(2)]
    ctx.wo_bf = [nc.dram_tensor("l%d_wo_bf" % l, [D, D], F16, kind="Internal")
                 for l in range(2)]
    ctx.w1t_in = nc.dram_tensor("l1_w1t", [D, H1], F32, kind="ExternalInput")
    ctx.w3t_in = nc.dram_tensor("l1_w3t", [D, H1], F32, kind="ExternalInput")
    ctx.w2t_in = nc.dram_tensor("l1_w2t", [H1, D], F32, kind="ExternalInput")
    ctx.w1t_bf = nc.dram_tensor("w1t_bf", [D, H1], F16, kind="Internal")
    ctx.w3t_bf = nc.dram_tensor("w3t_bf", [D, H1], F16, kind="Internal")
    ctx.w2t_bf = nc.dram_tensor("w2t_bf", [H1, D], F16, kind="Internal")
    ctx.hid_dram = nc.dram_tensor("hid_dram", [H1, TOK], F16, kind="Internal")
    ctx.hidB_dram = nc.dram_tensor("hidB_dram", [H1, TOK], F16, kind="Internal")
    ctx.out_own = nc.dram_tensor("out_own", [TOK, D], F32, kind="ExternalOutput")
    ctx.cast_list = [
        ("wq0", ctx.wq_in[0], ctx.wq_bf[0]),
        ("wo0", ctx.wo_in[0], ctx.wo_bf[0]),
        ("w1t", ctx.w1t_in, ctx.w1t_bf),
        ("w3t", ctx.w3t_in, ctx.w3t_bf),
        ("wq1", ctx.wq_in[1], ctx.wq_bf[1]),
        ("wo1", ctx.wo_in[1], ctx.wo_bf[1]),
        ("w2t", ctx.w2t_in, ctx.w2t_bf),
    ]
    ctx.ffn0_sl = nc.dram_tensor("l0_ffn_sl", [128, 3 * QSLF], F32, kind="ExternalInput")
    ctx.q_in = [nc.dram_tensor("q_in%d" % m, [128, QSLF], F16, kind="Internal")
                for m in range(3)]
    ctx.q_out = [nc.dram_tensor("q_out%d" % m, [1024, QSLF], F16, kind="Internal")
                 for m in range(3)]
    ctx.gm_in = nc.dram_tensor("gm_in", [1, 3], F32, kind="Internal")
    ctx.gm_out = nc.dram_tensor("gm_out", [8, 3], F32, kind="Internal")

    if cfg.get("dump_xn"):
        ctx.out_xn = nc.dram_tensor("out_xn", [D, TOK], F32, kind="ExternalOutput")
        ctx.out_xng = nc.dram_tensor("out_xng", [8 * D, TOK], F32, kind="ExternalOutput")
    if cfg.get("dump_attn"):
        ctx.out_attng = nc.dram_tensor("out_attng", [D, 8192], F32, kind="ExternalOutput")
    if cfg.get("dump_x"):
        ctx.out_x = nc.dram_tensor("out_x", [D, TOK], F32, kind="ExternalOutput")
    if cfg.get("dump_qkv"):
        ctx.out_qkv = nc.dram_tensor("out_qkv", [384, 4096], F16, kind="ExternalOutput")
    if cfg.get("dump_p"):
        ctx.out_p = nc.dram_tensor("out_p", [512, 2048], F16, kind="ExternalOutput")
    if cfg.get("dump_qw"):
        ctx.out_qw = nc.dram_tensor("out_qw", [1024, QSLF], F16, kind="ExternalOutput")
        ctx.out_hid = nc.dram_tensor("out_hid", [128, 22528], F16, kind="ExternalOutput")

    ctx.a = _alloc(ctx)

    upto = cfg.get("upto", 99)
    import contextlib
    def NS(name):
        return nc.named_scope(name)
    with NS("consts"):
        p_consts(ctx)
    with NS("front"):
        with nc.Block() as front_blk:
            p_load_x(ctx, ctx.x_in, front_blk)
            p_quant1(ctx, front_blk)
    with NS("ln0"):
        p_layernorm(ctx)
    with NS("xnag0"):
        p_xn_ag(ctx)
    if upto == 66:
        with nc.Block() as blk:
            @blk.gpsimd
            def _(g):
                ctx.cinc(g.collective_compute(
                    "AllGather", ALU.bypass, replica_groups=[list(range(8))],
                    ins=[ctx.attn_ag_in.ap().opt()], outs=[ctx.attn_ag_out.ap().opt()]),
                    "s_cc_at")
                g.wait_ge(ctx.sems["s_cc_at"], 1)
        ctx.stack.close()
        return nc, ctx
    if upto >= 2:
        with NS("quant2"):
            p_quant2(ctx)
    if upto == 67:
        with nc.Block() as blk:
            @blk.gpsimd
            def _(g):
                g.wait_ge(ctx.sems["s_cc_q"], ctx.q_cc_done)
                ctx.cinc(g.collective_compute(
                    "AllGather", ALU.bypass, replica_groups=[list(range(8))],
                    ins=[ctx.attn_ag_in.ap().opt()], outs=[ctx.attn_ag_out.ap().opt()]),
                    "s_cc_at")
                g.wait_ge(ctx.sems["s_cc_at"], 1)
        ctx.stack.close()
        return nc, ctx
    if cfg.get("upto") == "ln":
        pass
    else:
        if upto < 3:
            pass
        def _ccprobe():
            with nc.Block() as blk:
                @blk.gpsimd
                def _(g):
                    ctx.cinc(g.collective_compute(
                        "AllGather", ALU.bypass, replica_groups=[list(range(8))],
                        ins=[ctx.attn_ag_in.ap().opt()], outs=[ctx.attn_ag_out.ap().opt()]),
                        "s_cc_at")
                    g.wait_ge(ctx.sems["s_cc_at"], ctx.cnt["s_cc_at"])
        blist = [] if upto < 3 else ([0] if upto in (3, 68) else [0, 1])
        for b in blist:
            with NS("qkv0_b%d" % b):
                p_qkv(ctx, 0, b)
            if upto == 68:
                _ccprobe()
                ctx.stack.close()
                return nc, ctx
            if upto == 3:
                continue

            if b == 0 and cfg.get("dump_qkv"):
                with nc.Block() as blk:
                    @blk.sync
                    def _(s):
                        ctx.dinc(s.dma_start(out=ctx.out_qkv[0:128, :], in_=ctx.a["arena"][:, QO:QO + 4096]), "s_st")
                        ctx.dinc(s.dma_start(out=ctx.out_qkv[128:256, :], in_=ctx.a["arena"][:, KO:KO + 4096]), "s_st")
                        ctx.dinc(s.dma_start(out=ctx.out_qkv[256:384, :], in_=ctx.a["arena"][:, VO:VO + 4096]), "s_st")
                        s.wait_ge(ctx.sems["s_st"], ctx.cnt["s_st"])
            with NS("attn0_b%d" % b):
                p_attn_units(ctx, 0, b)
        if upto == 70:
            _ccprobe()
            ctx.stack.close()
            return nc, ctx
        if upto < 5:
            ctx.stack.close()
            return nc, ctx
        p_attn_ag(ctx)
        if upto == 71:
            with nc.Block() as blk:
                @blk.gpsimd
                def _(g):
                    g.wait_ge(ctx.sems["s_cc_at"], ctx.attn_cc_done)
            ctx.stack.close()
            return nc, ctx
        if upto < 6:
            ctx.stack.close()
            return nc, ctx
        with NS("ffn0_h0"):
            p_ffn0(ctx, 0)
        if cfg.get("dump_qw"):
            with nc.Block() as blk:
                @blk.sync
                def _(s):
                    ctx.dinc(s.dma_start(out=ctx.out_qw[:, :], in_=ctx.q_out[0][:, :]), "s_st")
                    ctx.dinc(s.dma_start(out=ctx.out_hid[:, :],
                                         in_=ctx.a["arena"][:, 0:22528]), "s_st")
                    s.wait_ge(ctx.sems["s_st"], ctx.cnt["s_st"])
        if upto == 65:
            with nc.Block() as blk:
                @blk.gpsimd
                def _(g):
                    g.wait_ge(ctx.sems["s_cc_at"], ctx.attn_cc_done)
        if upto >= 7:
            with NS("proj0"):
                p_proj(ctx, 0)
        if upto >= 8:
            with NS("ffn0_h1"):
                p_ffn0(ctx, 1)
        if upto >= 9:
            with NS("ln1"):
                p_layernorm(ctx)
            with NS("xnag1"):
                p_xn_ag(ctx)
            with NS("ffn1a"):
                p_ffn1a(ctx)
            for b in range(2):
                with NS("qkv1_b%d" % b):
                    p_qkv(ctx, 1, b)
                with NS("attn1_b%d" % b):
                    p_attn_units(ctx, 1, b)
            with NS("attnag1"):
                p_attn_ag(ctx)
            with NS("fwhtb"):
                p_fwht_b(ctx)
            with NS("proj1"):
                p_proj(ctx, 1)
            with NS("w2a"):
                p_w2a(ctx)
            with NS("store"):
                p_store(ctx)

    if cfg.get("dump_xn"):
        with nc.Block() as blk:
            @blk.gpsimd
            def _(g):
                g.wait_ge(ctx.sems["s_cc"], ctx.xn_cc_done)
                ctx.dinc(g.dma_start(out=ctx.out_xng[:, :], in_=ctx.xn_ag_out[:, :]), "s_st")
                for d in range(16):
                    ctx.dinc(g.dma_start(out=ctx.out_xn[d * 128:(d + 1) * 128, :],
                                         in_=ctx.a["xn"][:, d * TOK:(d + 1) * TOK]), "s_st")
                g.wait_ge(ctx.sems["s_st"], ctx.cnt["s_st"])
    if cfg.get("dump_attn"):
        with nc.Block() as blk:
            @blk.gpsimd
            def _(g):
                g.wait_ge(ctx.sems["s_cc"], ctx.attn_cc_done)
                ctx.dinc(g.dma_start(out=ctx.out_attng[:, :], in_=ctx.attn_ag_out[:, :]), "s_st")
                g.wait_ge(ctx.sems["s_st"], ctx.cnt["s_st"])
    if cfg.get("dump_x"):
        with nc.Block() as blk:
            @blk.sync
            def _(s):
                for d in range(16):
                    ctx.dinc(s.dma_start(out=ctx.out_x[d * 128:(d + 1) * 128, :],
                                         in_=ctx.a["x_fm"][:, d * TOK:(d + 1) * TOK]), "s_st")
                s.wait_ge(ctx.sems["s_st"], ctx.cnt["s_st"])

    ctx.stack.close()
    return nc, ctx


def make_in_maps(inputs):
    x = np.asarray(inputs["L0_attn_x"])
    maps = []
    for c in range(NCORE):
        b, k = c // 4, c % 4
        blocks = [k, 7 - k]
        wqkv = np.asarray(inputs["L0_attn_w_qkv"])
        l0sl = np.concatenate([wqkv[256 * c:256 * (c + 1)],
                               wqkv[2048 + 256 * c:2048 + 256 * (c + 1)],
                               wqkv[4096 + 256 * c:4096 + 256 * (c + 1)]], axis=0)
        l1sl = np.concatenate([np.asarray(inputs["L1_attn_Wq"])[256 * c:256 * (c + 1)],
                               np.asarray(inputs["L1_attn_Wk"])[256 * c:256 * (c + 1)],
                               np.asarray(inputs["L1_attn_Wv"])[256 * c:256 * (c + 1)]], axis=0)
        QSL = H0 * D // NCORE
        fsl = [np.asarray(inputs["L0_ffn_w%d" % (m + 1)]).T.reshape(-1)[c * QSL:(c + 1) * QSL]
               .reshape(128, QSLF) for m in range(3)]
        m = {
            "x_own": np.ascontiguousarray(
                np.concatenate([x[b, 512 * g:512 * (g + 1)] for g in blocks], axis=0)),
            "l0_qkv_sl": np.ascontiguousarray(l0sl.T),
            "l1_qkv_sl": np.ascontiguousarray(l1sl.T),
            "l0_wo": np.ascontiguousarray(np.asarray(inputs["L0_attn_w_out"]).T),
            "l1_wo": np.ascontiguousarray(np.asarray(inputs["L1_attn_Wo"]).T),
            "l0_ffn_sl": np.ascontiguousarray(np.concatenate(fsl, axis=1)),
            "l1_w1t": np.ascontiguousarray(np.asarray(inputs["L1_ffn_W1"]).T),
            "l1_w3t": np.ascontiguousarray(np.asarray(inputs["L1_ffn_W3"]).T),
            "l1_w2t": np.ascontiguousarray(np.asarray(inputs["L1_ffn_W2"]).T),
        }
        maps.append(m)
    return maps


def kernel(**inputs):
    nc, ctx = build({})
    maps = make_in_maps(inputs)
    res = run_bass_kernel_spmd(nc, maps, core_ids=list(range(NCORE)))
    out = np.empty((B, S, D), dtype=np.float32)
    for c in range(NCORE):
        b, k = c // 4, c % 4
        oc = res.results[c]["out_own"]
        for s_i, g in enumerate([k, 7 - k]):
            out[b, 512 * g:512 * (g + 1)] = oc[512 * s_i:512 * (s_i + 1)]
    return out


# version 05 (final)
# speedup vs baseline: 1.1989x; 1.0070x over previous
import numpy as np
import ml_dtypes
from contextlib import ExitStack

import concourse.bass as bass
import concourse.mybir as mybir
from concourse.bass_utils import run_bass_kernel_spmd

F32 = mybir.dt.float32
F16 = mybir.dt.bfloat16
AF = mybir.ActivationFunctionType
ALU = mybir.AluOpType
AX = mybir.AxisListType

B, S, D = 2, 4096, 2048
NH, DH, C = 16, 128, 512
TOK = 1024
NCORE = 8
HPC = 2
SCALE = float(128) ** -0.5
LN_EPS = 1e-5
Q_EPS = 1e-8
H0 = 5632
H1 = 8192
QSLF = (H0 * D // NCORE) // 128
FW_SC = float(H1) ** -0.5


def _owner(g):
    return (min(g, 7 - g), 0 if g < 4 else 1)


def _masks_np():
    p = np.arange(128)[:, None]
    q = np.arange(512)[None, :]
    cols = []
    for o in (0, 128, 256, 384):
        cols.append((p + o >= q).astype(ml_dtypes.bfloat16))
    for o in (0, 128, 256, 384):
        cols.append((p + o <= q).astype(ml_dtypes.bfloat16))
    return np.concatenate(cols, axis=1)


def _hadamard(n):
    H = np.array([[1.0]], dtype=np.float32)
    while H.shape[0] < n:
        H = np.block([[H, H], [H, -H]])
    return H


class Ctx:
    def __init__(self, nc, cfg):
        self.nc = nc
        self.cfg = cfg
        self.cnt = {}
        self.stack = ExitStack()
        self.sems = {}

    def sem(self, name):
        if name not in self.sems:
            self.sems[name] = self.stack.enter_context(self.nc.semaphore(name))
            self.cnt[name] = 0
        return self.sems[name]

    def dinc(self, inst, name):
        inst.then_inc(self.sems[name], 16)
        self.cnt[name] += 16
        return self.cnt[name]

    def slot_dma(self, inst, slot):
        return self.dinc(inst, "q%d" % slot)

    def slot_wait(self, eng, slot, val):
        eng.wait_ge(self.sems["q%d" % slot], val)

    def cinc(self, inst, name, val=1):
        inst.then_inc(self.sems[name], val)
        self.cnt[name] += val
        return self.cnt[name]


def _alloc(ctx):
    nc = ctx.nc
    a = {}
    a["x_fm"] = nc.alloc_sbuf_tensor("x_fm", [128, 16 * TOK], F32)
    a["xn"] = nc.alloc_sbuf_tensor("xn", [128, 16 * TOK], F16)
    a["arena"] = nc.alloc_sbuf_tensor("arena", [128, 32768], F16)
    a["wpool"] = nc.alloc_sbuf_tensor("wpool", [128, 8192], F16)
    a["xpool"] = nc.alloc_sbuf_tensor("xpool", [128, 2048], F16)
    a["praw"] = nc.alloc_sbuf_tensor("praw", [128, 2048], F16)
    a["pmask"] = nc.alloc_sbuf_tensor("pmask", [128, 2048], F16)
    a["sc_a"] = nc.alloc_sbuf_tensor("sc_a", [128, 512], F32)
    a["sc_b"] = nc.alloc_sbuf_tensor("sc_b", [128, 512], F32)
    a["scr"] = nc.alloc_sbuf_tensor("scr", [128, 1024], F16)
    a["bc_sb"] = nc.alloc_sbuf_tensor("bc_sb", [128, 512], F16)
    a["hidout"] = nc.alloc_sbuf_tensor("hidout", [128, 2048], F16)
    a["vtmp"] = nc.alloc_sbuf_tensor("vtmp", [128, 1024], F16)
    a["consts"] = nc.alloc_sbuf_tensor("consts", [128, 768], F16)
    a["identf"] = nc.alloc_sbuf_tensor("identf", [128, 128], F32)
    a["pextra"] = nc.alloc_sbuf_tensor("pextra", [128, 2048], F16)
    a["fconst"] = nc.alloc_sbuf_tensor("fconst", [128, 136], F32)
    print(f"[build] sbuf remaining: {nc.sbuf_bytes_remaining} B/part")
    a["ps_a"] = nc.alloc_psum_tensor("ps_a", [128, 2048], F32)
    a["ps_b"] = nc.alloc_psum_tensor("ps_b", [128, 1024], F32)
    a["ps_c"] = nc.alloc_psum_tensor("ps_c", [128, 1024], F32)
    return a


def p_consts(ctx):
    nc, a = ctx.nc, ctx.a
    ctx.sem("s_ld"); ctx.sem("s_sc"); ctx.sem("s_gp"); ctx.sem("s_st")
    ctx.sem("s_pe"); ctx.sem("s_act"); ctx.sem("s_dve"); ctx.sem("s_cc")
    ctx.sem("s_cc_q"); ctx.sem("s_cc_at"); ctx.sem("s_sp")
    for i in range(8):
        ctx.sem("q%d" % i)
    for i in range(7):
        ctx.sem("c%d" % i)
    with nc.Block() as blk:
        @blk.sync
        def _(s):
            ctx.dinc(s.dma_start(out=a["identf"][:, :], in_=ctx.identf_dram[:, :]), "s_ld")
            ctx.dinc(s.dma_start(out=a["consts"][:, :], in_=ctx.consts_dram[:, :]), "s_ld")
            w = ctx.dinc(s.dma_start(out=a["fconst"][:, :], in_=ctx.fconst_dram[:, :]), "s_ld")
            s.wait_ge(ctx.sems["s_ld"], w)


def p_load_x(ctx, x_dram, blk):
    nc, a = ctx.nc, ctx.a
    arena_f32 = a["arena"].ap().bitcast(F32)

    pe0 = ctx.cnt["s_pe"]
    dve0 = ctx.cnt["s_dve"]
    wv = {}

    if True:
        @blk.sync
        def _(s):
            for tt in range(8):
                buf = tt % 2
                if tt >= 2:
                    s.wait_ge(ctx.sems["s_pe"], pe0 + (tt - 1) * 16)
                wv[tt] = ctx.slot_dma(s.dma_start(
                    out=arena_f32[:, buf * 2048:(buf + 1) * 2048],
                    in_=x_dram[tt * 128:(tt + 1) * 128, :]), buf)

        @blk.tensor
        def _(t):
            for tt in range(8):
                buf = tt % 2
                ctx.slot_wait(t, buf, wv[tt])
                for d in range(16):
                    j = tt * 16 + d
                    dst = a["ps_b"][:, (d % 2) * 512:(d % 2) * 512 + 128]
                    if j >= 2:
                        t.wait_ge(ctx.sems["s_dve"], dve0 + j - 1)
                    ctx.cinc(t.transpose(
                        dst,
                        arena_f32[:, buf * 2048 + d * 128: buf * 2048 + (d + 1) * 128],
                        a["identf"][:, :]), "s_pe")

        @blk.vector
        def _(v):
            for tt in range(8):
                for d in range(16):
                    j = tt * 16 + d
                    src = a["ps_b"][:, (d % 2) * 512:(d % 2) * 512 + 128]
                    v.wait_ge(ctx.sems["s_pe"], pe0 + j + 1)
                    ctx.cinc(v.tensor_copy(
                        a["x_fm"][:, d * TOK + tt * 128: d * TOK + (tt + 1) * 128], src),
                        "s_dve")


def p_layernorm(ctx):
    nc, a = ctx.nc, ctx.a
    ones_col = a["consts"][:, 320:321]
    x_fm = a["x_fm"]
    act0 = ctx.cnt["s_act"]
    pe0 = ctx.cnt["s_pe"]

    bufs = [a["scr"], a["hidout"]]
    with nc.Block() as blk:
        @blk.scalar
        def _(sc):
            for j in range(32):
                d, tb = j // 2, j % 2
                bf = bufs[j % 2]
                src = x_fm[:, d * TOK + tb * 512: d * TOK + (tb + 1) * 512]
                if j >= 2:
                    sc.wait_ge(ctx.sems["s_pe"], pe0 + j - 1)
                sc.copy(bf[:, 0:512], src)
                ctx.cinc(sc.activation(bf[:, 512:1024], src, AF.Square), "s_act")

        @blk.tensor
        def _(t):
            for j in range(32):
                d, tb = j // 2, j % 2
                bf = bufs[j % 2]
                t.wait_ge(ctx.sems["s_act"], act0 + j + 1)
                first, last = (d == 0), (d == 15)
                t.matmul(a["ps_c"][0:1, tb * 512:(tb + 1) * 512], ones_col,
                         bf[:, 0:512], start=first, stop=last)
                ctx.cinc(t.matmul(a["ps_b"][0:1, tb * 512:(tb + 1) * 512], ones_col,
                                  bf[:, 512:1024], start=first, stop=last), "s_pe")
    pe_stats = ctx.cnt["s_pe"]

    xp32 = a["xpool"].ap().bitcast(F32)
    for tb in range(2):
        r_mean = a["sc_a"][0:1, 0:512]
        r_rstd = xp32[0:1, 0:512]
        r_negmr = xp32[0:1, 512:1024]
        with nc.Block() as blk:
            @blk.vector
            def _(v):
                v.wait_ge(ctx.sems["s_pe"], pe_stats)
                v.tensor_scalar_mul(r_mean, a["ps_c"][0:1, tb * 512:(tb + 1) * 512], 1.0 / D)
                ctx.cinc(v.tensor_scalar_mul(r_rstd, a["ps_b"][0:1, tb * 512:(tb + 1) * 512],
                                             1.0 / D), "s_dve")
            d1 = ctx.cnt["s_dve"]

            @blk.scalar
            def _(sc):
                sc.wait_ge(ctx.sems["s_dve"], d1)
                ctx.cinc(sc.activation(r_negmr, r_mean, AF.Square), "s_act")
            a1 = ctx.cnt["s_act"]

            @blk.vector
            def _(v):
                v.wait_ge(ctx.sems["s_act"], a1)
                ctx.cinc(v.tensor_sub(r_rstd, r_rstd, r_negmr), "s_dve")
            d2 = ctx.cnt["s_dve"]

            @blk.scalar
            def _(sc):
                sc.wait_ge(ctx.sems["s_dve"], d2)
                sc.activation(r_rstd, r_rstd, AF.Ln, bias=a["fconst"][0:1, 128:129])
                ctx.cinc(sc.activation(r_rstd, r_rstd, AF.Exp, scale=-0.5), "s_act")
            a2 = ctx.cnt["s_act"]

            @blk.vector
            def _(v):
                v.wait_ge(ctx.sems["s_act"], a2)
                v.tensor_mul(r_negmr, r_mean, r_rstd)
                v.tensor_scalar_mul(r_negmr, r_negmr, -1.0)

        with nc.Block() as blk:
            @blk.tensor
            def _(t):
                t.matmul(a["ps_a"][:, 0:512], a["fconst"][0:1, 0:128],
                         r_rstd, start=True, stop=True)
                ctx.cinc(t.matmul(a["ps_a"][:, 512:1024], a["fconst"][0:1, 0:128],
                                  r_negmr, start=True, stop=True), "s_pe")
            pe_bc = ctx.cnt["s_pe"]

            @blk.vector
            def _(v):
                v.wait_ge(ctx.sems["s_pe"], pe_bc)
                for d in range(16):
                    xt = x_fm[:, d * TOK + tb * 512: d * TOK + (tb + 1) * 512]
                    v.tensor_tensor(a["sc_b"][:, :], xt, a["ps_a"][:, 0:512], ALU.mult)
                    v.tensor_tensor(a["xn"][:, d * TOK + tb * 512: d * TOK + (tb + 1) * 512],
                                    a["sc_b"][:, :], a["ps_a"][:, 512:1024], ALU.add)


def p_xn_ag(ctx):
    nc, a = ctx.nc, ctx.a
    with nc.Block() as blk:
        @blk.scalar
        def _(sc):
            for d in range(16):
                ctx.dinc(sc.dma_start(out=ctx.xn_ag_in[d * 128:(d + 1) * 128, :],
                                      in_=a["xn"][:, d * TOK:(d + 1) * TOK]), "s_sc")

        sc_done = ctx.cnt["s_sc"]

        @blk.gpsimd
        def _(g):
            g.wait_ge(ctx.sems["s_sc"], sc_done)
            ctx.cinc(g.collective_compute(
                "AllGather", ALU.bypass,
                replica_groups=[list(range(8))],
                ins=[ctx.xn_ag_in.ap().opt()], outs=[ctx.xn_ag_out.ap().opt()]), "s_cc")
    ctx.xn_cc_done = ctx.cnt["s_cc"]



QO, KO, VO, MO = 0, 4096, 8192, 12288
Q1, K1, V1 = 16384, 20480, 24576
RHS = 22528


def p_wcast_issue(ctx, names, g):
    nc = ctx.nc
    if not hasattr(ctx, "wready"):
        ctx.wready = {}
        ctx.wslin = {}
    pid = g.partition_id()
    for i, (name, src_t, dst_t) in enumerate(ctx.cast_list):
        if name not in names:
            continue
        sem = "c%d" % i
        rows = src_t.shape[0] // NCORE
        sl_in = nc.dram_tensor("wsl_in_%s" % name, [rows, src_t.shape[1]],
                               F16, kind="Internal")
        ctx.wslin[name] = (sl_in, sem, ctx.dinc(g.dma_start(
            out=sl_in.ap(), in_=src_t.ap()[bass.ds(pid * rows, rows), :]), sem))


def p_wcast_trigger(ctx, names, g):
    for i, (name, src_t, dst_t) in enumerate(ctx.cast_list):
        if name not in names:
            continue
        sl_in, sem, w = ctx.wslin[name]
        g.wait_ge(ctx.sems[sem], w)
        ctx.wready[name] = (sem, ctx.cinc(g.collective_compute(
            "AllGather", ALU.bypass, replica_groups=[list(range(8))],
            ins=[sl_in.ap().opt()], outs=[dst_t.ap().opt()]), sem))


def p_wcast(ctx, names):
    nc = ctx.nc
    with nc.Block() as blk:
        @blk.gpsimd
        def _(g):
            p_wcast_issue(ctx, names, g)
            p_wcast_trigger(ctx, names, g)


def p_qkv(ctx, layer, b):
    nc, a = ctx.nc, ctx.a
    ar = a["arena"]
    wsl = ctx.wq_bf[layer]

    with nc.Block() as blk:
        @blk.sync
        def _(s):
            ctx.qkv_ld_batch = ctx.dinc(
                s.dma_start(out=ar[:, MO:MO + 4096], in_=ctx.masks_dram[:, :]), "s_ld")

    xv = ctx.xn_ag_out.ap().rearrange("(r p) n -> p r n", p=128)
    pe0 = ctx.cnt["s_pe"]
    dve0 = ctx.cnt["s_dve"]
    wv = {}
    lv = {}
    with nc.Block() as blk:
        @blk.sync
        def _(s):
            s.wait_ge(ctx.sems[ctx.wready["wq%d" % layer][0]], ctx.wready["wq%d" % layer][1])
            for g in range(8):
                for d in range(16):
                    idx = g * 16 + d
                    slot = 4 + idx % 4
                    if idx >= 4:
                        gp, dp = (idx - 4) // 16, (idx - 4) % 16
                        s.wait_ge(ctx.sems["s_pe"], pe0 + gp * 24 + dp + 1)
                    lv[idx] = ctx.slot_dma(s.dma_start(
                        out=a["wpool"][:, (idx % 4) * 768:(idx % 4) * 768 + 768],
                        in_=wsl[d * 128:(d + 1) * 128, 0:768]), slot)

        @blk.scalar
        def _(sc):
            sc.wait_ge(ctx.sems["s_cc"], ctx.xn_cc_done)
            for g in range(8):
                r, sl = _owner(g)
                rr = 16 * (4 * b + r)
                for d in range(16):
                    idx = g * 16 + d
                    slot = idx % 4
                    if idx >= 4:
                        gp, dp = (idx - 4) // 16, (idx - 4) % 16
                        sc.wait_ge(ctx.sems["s_pe"], pe0 + gp * 24 + dp + 1)
                    wv[idx] = ctx.slot_dma(sc.dma_start(
                        out=a["xpool"][:, slot * 512:(slot + 1) * 512],
                        in_=xv[:, rr + d, sl * 512:(sl + 1) * 512]), slot)

        @blk.tensor
        def _(t):
            for g in range(8):
                for d in range(16):
                    idx = g * 16 + d
                    slot = idx % 4
                    ctx.slot_wait(t, slot, wv[idx])
                    ctx.slot_wait(t, 4 + idx % 4, lv[idx])
                    if d == 0 and g > 0:
                        t.wait_ge(ctx.sems["s_dve"], dve0 + g * 8)
                    rhs = a["xpool"][:, slot * 512:(slot + 1) * 512]
                    first, last = (d == 0), (d == 15)
                    w = a["wpool"][:, (idx % 4) * 768:(idx % 4) * 768 + 768]
                    t.matmul(a["ps_a"][:, 0:512], w[:, 0:128], rhs, start=first, stop=last)
                    t.matmul(a["ps_a"][:, 512:1024], w[:, 128:256], rhs, start=first, stop=last)
                    t.matmul(a["ps_a"][:, 1024:1536], w[:, 256:384], rhs, start=first, stop=last)
                    t.matmul(a["ps_a"][:, 1536:2048], w[:, 384:512], rhs, start=first, stop=last)
                    t.matmul(a["ps_b"][:, 0:512], w[:, 512:640], rhs, start=first, stop=last)
                    ctx.cinc(t.matmul(a["ps_b"][:, 512:1024], w[:, 640:768], rhs,
                                      start=first, stop=last), "s_pe")
                for j in range(4):
                    if j == 0:
                        t.wait_ge(ctx.sems["s_dve"], dve0 + g * 8 + 5)
                    ctx.cinc(t.transpose(a["ps_c"][:, j * 128:(j + 1) * 128],
                                         a["sc_a"][:, j * 128:(j + 1) * 128],
                                         a["identf"][:, :]), "s_pe")
                for j in range(4):
                    if j == 0:
                        t.wait_ge(ctx.sems["s_dve"], dve0 + g * 8 + 6)
                    ctx.cinc(t.transpose(a["ps_c"][:, 512 + j * 128: 512 + (j + 1) * 128],
                                         a["sc_b"][:, j * 128:(j + 1) * 128],
                                         a["identf"][:, :]), "s_pe")

        @blk.vector
        def _(v):
            for g in range(8):
                v.wait_ge(ctx.sems["s_pe"], pe0 + g * 24 + 16)
                gc = slice(g * 512, (g + 1) * 512)
                ctx.cinc(v.tensor_copy(ar[:, QO + g * 512:QO + (g + 1) * 512],
                                       a["ps_a"][:, 0:512]), "s_dve")
                ctx.cinc(v.tensor_copy(ar[:, KO + g * 512:KO + (g + 1) * 512],
                                       a["ps_a"][:, 1024:1536]), "s_dve")
                ctx.cinc(v.tensor_copy(ar[:, Q1 + g * 512:Q1 + (g + 1) * 512],
                                       a["ps_a"][:, 512:1024]), "s_dve")
                ctx.cinc(v.tensor_copy(ar[:, K1 + g * 512:K1 + (g + 1) * 512],
                                       a["ps_a"][:, 1536:2048]), "s_dve")
                ctx.cinc(v.tensor_copy(a["sc_a"][:, :], a["ps_b"][:, 0:512]), "s_dve")
                ctx.cinc(v.tensor_copy(a["sc_b"][:, :], a["ps_b"][:, 512:1024]), "s_dve")
                v.wait_ge(ctx.sems["s_pe"], pe0 + g * 24 + 20)
                ctx.cinc(v.tensor_copy(ar[:, VO + g * 512:VO + (g + 1) * 512],
                                       a["ps_c"][:, 0:512]), "s_dve")
                v.wait_ge(ctx.sems["s_pe"], pe0 + g * 24 + 24)
                ctx.cinc(v.tensor_copy(ar[:, V1 + g * 512:V1 + (g + 1) * 512],
                                       a["ps_c"][:, 512:1024]), "s_dve")


def p_quant1(ctx, blk):
    nc, a = ctx.nc, ctx.a
    arF = a["arena"].ap().bitcast(F32)
    vF = a["vtmp"].ap().bitcast(F32)
    if True:
        dve0 = [None]
        wv = {}

        @blk.gpsimd
        def _(g):
            p_wcast_issue(ctx, ("wq0",), g)
            p_wcast_trigger(ctx, ("wq0",), g)
            d0 = ctx.cnt["s_dve"]
            for m in range(3):
                for ch in range(2):
                    i = m * 2 + ch
                    if i >= 2:
                        g.wait_ge(ctx.sems["s_dve"], d0 + i - 1)
                    wv[i] = ctx.slot_dma(g.dma_start(
                        out=arF[:, 4096 + (i % 2) * 5632: 4096 + (i % 2) * 5632 + 5632],
                        in_=ctx.ffn0_sl[:, m * QSLF + ch * 5632: m * QSLF + (ch + 1) * 5632]),
                        6 + i % 2)

        @blk.vector
        def _(v):
            for m in range(3):
                for ch in range(2):
                    i = m * 2 + ch
                    ctx.slot_wait(v, 6 + i % 2, wv[i])
                    ctx.cinc(v.tensor_reduce(vF[:, i:i + 1],
                                             arF[:, 4096 + (i % 2) * 5632: 4096 + (i % 2) * 5632 + 5632],
                                             AX.X, ALU.add, apply_absolute_value=True), "s_dve")
        dve_red = ctx.cnt["s_dve"]

        @blk.tensor
        def _(t):
            t.wait_ge(ctx.sems["s_dve"], dve_red)
            ctx.cinc(t.matmul(a["ps_c"][0:1, 0:6], a["fconst"][:, 0:1], vF[:, 0:6],
                              start=True, stop=True), "s_pe")
        pe_s = ctx.cnt["s_pe"]

        @blk.vector
        def _(v):
            v.wait_ge(ctx.sems["s_pe"], pe_s)
            ctx.cinc(v.tensor_reduce(
                a["xpool"].ap().bitcast(F32)[0:1, 0:3],
                a["ps_c"][0:1, 0:6].rearrange("p (m two) -> p m two", two=2),
                AX.X, ALU.add), "s_dve")
        dve_c = ctx.cnt["s_dve"]

        @blk.scalar
        def _(sc):
            sc.wait_ge(ctx.sems["s_dve"], dve_c)
            ctx.dinc(sc.dma_start(out=ctx.gm_in[:, :],
                                  in_=a["xpool"].ap().bitcast(F32)[0:1, 0:3]), "s_sc")
        sc_w = ctx.cnt["s_sc"]

        @blk.gpsimd
        def _(g):
            g.wait_ge(ctx.sems["s_sc"], sc_w)
            ctx.cinc(g.collective_compute(
                "AllGather", ALU.bypass, replica_groups=[list(range(8))],
                ins=[ctx.gm_in.ap().opt()], outs=[ctx.gm_out.ap().opt()]), "s_cc_q")
    ctx.gm_cc_done = ctx.cnt["s_cc_q"]


def p_quant2(ctx):
    nc, a = ctx.nc, ctx.a
    vF = a["vtmp"].ap().bitcast(F32)
    wF = a["wpool"].ap().bitcast(F32)
    N = float(H0 * D)
    with nc.Block() as blk:
        @blk.gpsimd
        def _(g):
            g.wait_ge(ctx.sems["s_cc_q"], ctx.gm_cc_done)
            ctx.dinc(g.dma_start(out=a["sc_a"][0:8, 0:3], in_=ctx.gm_out[:, :]), "s_gp")
        gp_l = ctx.cnt["s_gp"]

        @blk.tensor
        def _(t):
            t.wait_ge(ctx.sems["s_gp"], gp_l)
            ctx.cinc(t.matmul(a["ps_c"][0:1, 0:3], a["fconst"][0:8, 0:1],
                              a["sc_a"][0:8, 0:3], start=True, stop=True), "s_pe")
        pe_t = ctx.cnt["s_pe"]

        @blk.vector
        def _(v):
            v.wait_ge(ctx.sems["s_pe"], pe_t)
            ctx.cinc(v.tensor_scalar(a["xpool"].ap().bitcast(F32)[0:1, 4:7],
                                     a["ps_c"][0:1, 0:3], 1.0 / (2.0 * N), Q_EPS / 2.0,
                                     ALU.mult, ALU.add), "s_dve")
        act_t = ctx.cnt["s_dve"]

        @blk.tensor
        def _(t):
            t.wait_ge(ctx.sems["s_dve"], act_t)
            for m in range(3):
                ctx.cinc(t.matmul(a["ps_a"][:, m * 512:m * 512 + 1],
                                  a["fconst"][0:1, 0:128],
                                  a["xpool"].ap().bitcast(F32)[0:1, 4 + m:5 + m],
                                  start=True, stop=True), "s_pe")
        pe_b = ctx.cnt["s_pe"]

        @blk.vector
        def _(v):
            v.wait_ge(ctx.sems["s_pe"], pe_b)
            for m in range(3):
                v.tensor_copy(vF[:, 8 + m:9 + m], a["ps_a"][:, m * 512:m * 512 + 1])
                ctx.cinc(v.tensor_scalar_mul(vF[:, 12 + m:13 + m],
                                             a["ps_a"][:, m * 512:m * 512 + 1], -1.0), "s_dve")
    thr_done = ctx.cnt["s_dve"]

    dve0 = ctx.cnt["s_dve"]
    wv = {}
    sv = {}
    with nc.Block() as blk:
        @blk.sync
        def _(s):
            for m in range(3):
                for ch in range(11):
                    i = m * 11 + ch
                    if i >= 2:
                        s.wait_ge(ctx.sems["s_dve"], dve0 + (i - 2) * 3 + 3)
                    wv[i] = ctx.slot_dma(s.dma_start(
                        out=wF[:, (i % 2) * 1024:(i % 2) * 1024 + 1024],
                        in_=ctx.ffn0_sl[:, m * QSLF + ch * 1024: m * QSLF + (ch + 1) * 1024]),
                        i % 2)

        @blk.scalar
        def _(sc):
            for m in range(3):
                for ch in range(11):
                    i = m * 11 + ch
                    sc.wait_ge(ctx.sems["s_dve"], dve0 + i * 3 + 3)
                    sv[i] = ctx.slot_dma(sc.dma_start(
                        out=ctx.q_in[m][:, ch * 1024:(ch + 1) * 1024],
                        in_=a["hidout"][:, (i % 2) * 1024:(i % 2) * 1024 + 1024]), 2 + i % 2)
        sc_all = [None]

        @blk.vector
        def _(v):
            v.wait_ge(ctx.sems["s_dve"], thr_done)
            for m in range(3):
                for ch in range(11):
                    i = m * 11 + ch
                    ctx.slot_wait(v, i % 2, wv[i])
                    wbuf = wF[:, (i % 2) * 1024:(i % 2) * 1024 + 1024]
                    if i >= 2:
                        ctx.slot_wait(v, 2 + i % 2, sv[i - 2])
                    v.tensor_scalar(a["praw"][:, 0:1024], wbuf, vF[:, 8 + m:9 + m], None,
                                    ALU.is_gt)
                    v.tensor_scalar(a["pmask"][:, 0:1024], wbuf, vF[:, 12 + m:13 + m], None,
                                    ALU.is_lt)
                    ctx.cinc(v.tensor_sub(a["hidout"][:, (i % 2) * 1024:(i % 2) * 1024 + 1024],
                                          a["praw"][:, 0:1024], a["pmask"][:, 0:1024]),
                             "s_dve", 3)

        @blk.gpsimd
        def _(g):
            g.wait_ge(ctx.sems["q3"], sv[31])
            g.wait_ge(ctx.sems["q2"], sv[32])
            for m in range(3):
                ctx.cinc(g.collective_compute(
                    "AllGather", ALU.bypass, replica_groups=[list(range(8))],
                    ins=[ctx.q_in[m].ap().opt()], outs=[ctx.q_out[m].ap().opt()]), "s_cc_q")
    ctx.q_cc_done = ctx.cnt["s_cc_q"]


def p_ffn0(ctx, hf):
    nc, a = ctx.nc, ctx.a
    ar = a["arena"]
    q1 = ctx.q_out[0].ap().rearrange("p n -> (p n)").rearrange("(d h) -> d h", h=H0)
    q2 = ctx.q_out[1].ap().rearrange("p n -> (p n)").rearrange("(d h) -> d h", h=H0)
    q3 = ctx.q_out[2].ap().rearrange("p n -> (p n)").rearrange("(h d) -> h d", d=D)

    hts = [hf * 22 + i for i in range(22)]
    pe0 = ctx.cnt["s_pe"]
    act0 = ctx.cnt["s_act"]
    dve0 = ctx.cnt["s_dve"]
    st3_pre = ctx.cnt["q3"]
    st7_pre = ctx.cnt["q7"]
    wv = {}
    sgm = [a["scr"].ap()[:, 0:1024], a["hidout"].ap()[:, 0:1024]]
    with nc.Block() as blk:
        if hf == 0:
            @blk.gpsimd
            def _(g):
                p_wcast_trigger(ctx, ("wq1", "wo0", "w1t", "w3t", "wo1", "w2t"), g)

        @blk.sync
        def _(s):
            s.wait_ge(ctx.sems["s_cc_q"], ctx.q_cc_done)
            for i, ht in enumerate(hts):
                for dh in range(2):
                    step = i * 2 + dh
                    slot = step % 4
                    if step >= 4:
                        s.wait_ge(ctx.sems["s_pe"], pe0 + step - 3)
                    for mi, qm in enumerate((q1, q2)):
                        src = qm.rearrange("(dt p) h -> p dt h", p=128)
                        dst = a["wpool"][:, slot * 2048 + mi * 1024:
                                         slot * 2048 + (mi + 1) * 1024]
                        wv[step] = ctx.slot_dma(s.dma_start(
                            out=dst.rearrange("p (a b) -> p a b", b=128),
                            in_=src[:, dh * 8:(dh + 1) * 8, ht * 128:(ht + 1) * 128]), slot)

        @blk.tensor
        def _(t):
            for i, ht in enumerate(hts):
                psx1 = a["ps_a"].ap()[:, 0:1024] if i % 2 == 0 else a["ps_b"].ap()[:, :]
                psx2 = a["ps_a"].ap()[:, 1024:2048] if i % 2 == 0 else a["ps_c"].ap()[:, :]
                if i >= 2:
                    t.wait_ge(ctx.sems["s_dve"], dve0 + i - 1)
                for dh in range(2):
                    step = i * 2 + dh
                    slot = step % 4
                    ctx.slot_wait(t, slot, wv[step])
                    for mi in range(2):
                        psd = psx1 if mi == 0 else psx2
                        for d8 in range(8):
                            d = dh * 8 + d8
                            w = a["wpool"][:, slot * 2048 + mi * 1024 + d8 * 128:
                                           slot * 2048 + mi * 1024 + (d8 + 1) * 128]
                            for th in range(2):
                                ii = t.matmul(psd[:, th * 512:(th + 1) * 512], w,
                                              a["xn"][:, d * TOK + th * 512: d * TOK + (th + 1) * 512],
                                              start=(d == 0), stop=(d == 15))
                    ctx.cinc(ii, "s_pe")

        @blk.scalar
        def _(sc):
            sc.wait_ge(ctx.sems["q3"], st3_pre)
            sc.wait_ge(ctx.sems["q7"], st7_pre)
            for i, ht in enumerate(hts):
                psx1 = a["ps_a"].ap()[:, 0:1024] if i % 2 == 0 else a["ps_b"].ap()[:, :]
                sc.wait_ge(ctx.sems["s_pe"], pe0 + i * 2 + 2)
                if i >= 2:
                    sc.wait_ge(ctx.sems["s_dve"], dve0 + i - 1)
                ctx.cinc(sc.activation(sgm[i % 2], psx1, AF.Sigmoid), "s_act")

        @blk.vector
        def _(v):
            for i, ht in enumerate(hts):
                psx2 = a["ps_a"].ap()[:, 1024:2048] if i % 2 == 0 else a["ps_c"].ap()[:, :]
                v.wait_ge(ctx.sems["s_act"], act0 + i + 1)
                ctx.cinc(v.tensor_tensor(ar[:, i * 1024:(i + 1) * 1024], sgm[i % 2],
                                         psx2, ALU.mult), "s_dve")

    pe1 = ctx.cnt["s_pe"]
    dve1 = ctx.cnt["s_dve"]
    wv3 = {}
    with nc.Block() as blk:
        @blk.sync
        def _(s):
            for dgrp in range(4):
                for i, ht in enumerate(hts):
                    step = dgrp * 22 + i
                    slot = step % 8
                    if step >= 8:
                        s.wait_ge(ctx.sems["s_pe"], pe1 + step - 7)
                    wv3[step] = ctx.slot_dma(s.dma_start(
                        out=a["wpool"][:, slot * 1024:slot * 1024 + 512],
                        in_=q3[ht * 128:(ht + 1) * 128, dgrp * 512:(dgrp + 1) * 512]), slot)

        @blk.tensor
        def _(t):
            for dgrp in range(4):
                if dgrp > 0:
                    t.wait_ge(ctx.sems["s_dve"], dve1 + dgrp * 8)
                for i, ht in enumerate(hts):
                    step = dgrp * 22 + i
                    slot = step % 8
                    ctx.slot_wait(t, slot, wv3[step])
                    for do in range(4):
                        w = a["wpool"][:, slot * 1024 + do * 128: slot * 1024 + (do + 1) * 128]
                        for th in range(2):
                            psd = (a["ps_a"].ap()[:, do * 512:(do + 1) * 512] if th == 0 else
                                   (a["ps_b"].ap()[:, do % 2 * 512:do % 2 * 512 + 512] if do < 2 else
                                    a["ps_c"].ap()[:, (do - 2) % 2 * 512:(do - 2) % 2 * 512 + 512]))
                            ii = t.matmul(psd, w,
                                          ar[:, i * 1024 + th * 512: i * 1024 + (th + 1) * 512],
                                          start=(i == 0), stop=(i == 21))
                    ctx.cinc(ii, "s_pe")

        @blk.vector
        def _(v):
            for dgrp in range(4):
                v.wait_ge(ctx.sems["s_pe"], pe1 + dgrp * 22 + 22)
                for do in range(4):
                    dt_ = dgrp * 4 + do
                    for th in range(2):
                        psd = (a["ps_a"].ap()[:, do * 512:(do + 1) * 512] if th == 0 else
                               (a["ps_b"].ap()[:, do % 2 * 512:do % 2 * 512 + 512] if do < 2 else
                                a["ps_c"].ap()[:, (do - 2) % 2 * 512:(do - 2) % 2 * 512 + 512]))
                        xt = a["x_fm"][:, dt_ * TOK + th * 512: dt_ * TOK + (th + 1) * 512]
                        ctx.cinc(v.tensor_tensor(xt, xt, psd, ALU.add), "s_dve")


def _unit_plan(layer, qb):
    if layer == 0:
        kts = []
        if qb > 0:
            kts += [(4 * (qb - 1) + j, MO + j * 512) for j in range(4)]
        kts += [(4 * qb + j, MO + 2048 + j * 512) for j in range(4)]
    else:
        kts = [(j, None) for j in range(4 * qb)]
        kts += [(4 * qb + j, MO + 2048 + j * 512) for j in range(4)]
    return [(kts[2 * i], kts[2 * i + 1]) for i in range(len(kts) // 2)]


def p_attn_units(ctx, layer, b):
    nc, a = ctx.nc, ctx.a
    ar = a["arena"]
    uoff = [(QO, KO, VO), (Q1, K1, V1)]

    pe0 = ctx.cnt["s_pe"]
    act0 = ctx.cnt["s_act"]
    dve0 = ctx.cnt["s_dve"]
    stv = {}
    st3_0 = ctx.cnt["q3"]
    st7_0 = ctx.cnt["q7"]

    nqb = ctx.cfg.get("nqb", 8)
    pairs = []
    qb_of = []
    qb_last, qb_first = {}, {}
    for qb in range(nqb):
        pl = _unit_plan(layer, qb)
        for pi, (p1, p2) in enumerate(pl):
            pairs.append(dict(p1=p1, p2=p2, qb=qb, masked=p1[1] is not None))
            qb_of.append(qb)
        qb_last[qb] = len(pairs) - 1
    n = len(pairs)
    for j in range(n - 1, -1, -1):
        qb_first[qb_of[j]] = j

    events = []
    for j in range(n):
        for u in range(2):
            events.append(("S", j, u))
        if j >= 1:
            for u in range(2):
                events.append(("PV", j - 1, u))
            if qb_last[qb_of[j - 1]] == j - 1:
                for u in range(2):
                    events.append(("EPI", qb_of[j - 1], u))
    for u in range(2):
        events.append(("PV", n - 1, u))
    for u in range(2):
        events.append(("EPI", qb_of[n - 1], u))

    s_after, pv_after, bc_after = {}, {}, {}
    cnt = 0
    for kind, idx, u in events:
        cnt += 1
        (s_after if kind == "S" else pv_after if kind == "PV" else bc_after)[(idx, u)] = cnt

    e_after = {}
    ac = 0
    for j in range(n):
        for u in range(2):
            ac += 1
            e_after[(j, u)] = ac
    m_after, recip_after, bcev_after, mult_after = {}, {}, {}, {}
    dv = 0
    for j in range(n):
        for u in range(2):
            if pairs[j]["masked"]:
                dv += 1
            m_after[(j, u)] = dv
        if qb_last[qb_of[j]] == j:
            qb = qb_of[j]
            for u in range(2):
                recip_after[(qb, u)] = dv + 1
                bcev_after[(qb, u)] = dv + 2
                mult_after[(qb, u)] = dv + 3
                dv += 3

    bcs = [a["bc_sb"].ap()[:, 0:512], a["hidout"].ap()[:, 0:512]]
    ebuf = lambda u, j: (a["praw"] if j % 2 == 0 else a["pmask"]).ap()[:, u * 1024:(u + 1) * 1024]
    mbuf = lambda u: a["pextra"].ap()[:, u * 1024:(u + 1) * 1024]
    prevm = {}
    pm = None
    for j in range(n):
        prevm[j] = pm
        if pairs[j]["masked"]:
            pm = j

    with nc.Block() as blk:
        @blk.tensor
        def _(t):
            for kind, idx, u in events:
                qoff, koff, voff = uoff[u]
                if kind == "S":
                    j = idx
                    pr = pairs[j]
                    if j >= 1:
                        t.wait_ge(ctx.sems["s_act"], act0 + e_after[(j - 1, u)])
                    qcols = slice(qoff + pr["qb"] * 512, qoff + (pr["qb"] + 1) * 512)
                    for jj, (kt, _m) in enumerate((pr["p1"], pr["p2"])):
                        ii = t.matmul(a["ps_a"][:, u * 1024 + jj * 512: u * 1024 + (jj + 1) * 512],
                                      ar[:, koff + kt * 128:koff + (kt + 1) * 128],
                                      ar[:, qcols], start=True, stop=True)
                    ctx.cinc(ii, "s_pe")
                elif kind == "PV":
                    j = idx
                    pr = pairs[j]
                    qb = pr["qb"]
                    psO = a["ps_b"][:, u * 512:(u + 1) * 512]
                    psD = a["ps_c"][0:1, u * 512:(u + 1) * 512]
                    if pr["masked"]:
                        t.wait_ge(ctx.sems["s_dve"], dve0 + m_after[(j, u)])
                    else:
                        t.wait_ge(ctx.sems["s_act"], act0 + e_after[(j, u)])
                    if qb_first[qb] == j and qb >= 1:
                        t.wait_ge(ctx.sems["s_dve"], dve0 + mult_after[(qb - 1, u)])
                    srcb = mbuf(u) if pr["masked"] else ebuf(u, j)
                    for jj, (kt, _m) in enumerate((pr["p1"], pr["p2"])):
                        pp = srcb[:, jj * 512:(jj + 1) * 512]
                        t.matmul(psO, ar[:, voff + kt * 128:voff + (kt + 1) * 128], pp,
                                 start=(qb_first[qb] == j and jj == 0), stop=False)
                        ii = t.matmul(psD, a["consts"][:, 320:321], pp,
                                      start=(qb_first[qb] == j and jj == 0), stop=False)
                    ctx.cinc(ii, "s_pe")
                else:
                    qb = idx
                    t.wait_ge(ctx.sems["s_dve"], dve0 + recip_after[(qb, u)])
                    psDfull = a["ps_c"][:, u * 512:(u + 1) * 512]
                    ctx.cinc(t.matmul(psDfull, a["consts"][0:1, 320:448],
                                      a["vtmp"][0:1, u * 512:(u + 1) * 512],
                                      start=True, stop=True), "s_pe")

        @blk.scalar
        def _(sc):
            for j in range(n):
                pr = pairs[j]
                qb = pr["qb"]
                for u in range(2):
                    sc.wait_ge(ctx.sems["s_pe"], pe0 + s_after[(j, u)])
                    if j >= 2 and pairs[j - 2]["masked"]:
                        sc.wait_ge(ctx.sems["s_dve"], dve0 + m_after[(j - 2, u)])
                    ctx.cinc(sc.activation(ebuf(u, j),
                                           a["ps_a"][:, u * 1024:(u + 1) * 1024],
                                           AF.Exp, scale=SCALE), "s_act")
                if qb_last[qb] == j:
                    r, sl = _owner(qb)
                    dst_c = 1024 * (4 * b + r) + 512 * sl
                    for u in range(2):
                        sc.wait_ge(ctx.sems["s_dve"], dve0 + mult_after[(qb, u)])
                        stv[(qb, u)] = ctx.slot_dma(sc.dma_start(
                            out=ctx.attn_ag_in[u * 128:(u + 1) * 128, dst_c:dst_c + 512],
                            in_=a["scr"][:, u * 512:(u + 1) * 512]), 7 if u else 3)

        @blk.vector
        def _(v):
            v.wait_ge(ctx.sems["s_ld"], ctx.qkv_ld_batch)
            for j in range(n):
                pr = pairs[j]
                qb = pr["qb"]
                if pr["masked"]:
                    moff = pr["p1"][1]
                    for u in range(2):
                        v.wait_ge(ctx.sems["s_act"], act0 + e_after[(j, u)])
                        if prevm[j] is not None:
                            v.wait_ge(ctx.sems["s_pe"], pe0 + pv_after[(prevm[j], u)])
                        ctx.cinc(v.tensor_tensor(mbuf(u), ebuf(u, j),
                                                 ar[:, moff:moff + 1024], ALU.mult), "s_dve")
                if qb_last[qb] != j:
                    continue
                for u in range(2):
                    psD = a["ps_c"][0:1, u * 512:(u + 1) * 512]
                    psDfull = a["ps_c"][:, u * 512:(u + 1) * 512]
                    psO = a["ps_b"][:, u * 512:(u + 1) * 512]
                    rrow32 = a["xpool"].ap().bitcast(F32)[0:1, u * 512:(u + 1) * 512]
                    rrow16 = a["vtmp"][0:1, u * 512:(u + 1) * 512]
                    v.wait_ge(ctx.sems["s_pe"], pe0 + pv_after[(j, u)])
                    v.reciprocal(rrow32, psD)
                    ctx.cinc(v.tensor_copy(rrow16, rrow32), "s_dve")
                    v.wait_ge(ctx.sems["s_pe"], pe0 + bc_after[(qb, u)])
                    ctx.cinc(v.tensor_copy(bcs[u], psDfull), "s_dve")
                    if qb >= 1:
                        ctx.slot_wait(v, 7 if u else 3, stv[(qb - 1, u)])
                    elif qb == 0:
                        v.wait_ge(ctx.sems["q7" if u else "q3"], st7_0 if u else st3_0)
                    ctx.cinc(v.tensor_tensor(a["scr"][:, u * 512:(u + 1) * 512],
                                             psO, bcs[u], ALU.mult), "s_dve")


def p_attn_ag(ctx):
    nc = ctx.nc
    w3, w7 = ctx.cnt["q3"], ctx.cnt["q7"]
    with nc.Block() as blk:
        @blk.gpsimd
        def _(g):
            g.wait_ge(ctx.sems["q3"], w3)
            g.wait_ge(ctx.sems["q7"], w7)
            ctx.cinc(g.collective_compute(
                "AllGather", ALU.bypass,
                replica_groups=[list(range(8))],
                ins=[ctx.attn_ag_in.ap().opt()], outs=[ctx.attn_ag_out.ap().opt()]), "s_cc_at")
    ctx.attn_cc_done = ctx.cnt["s_cc_at"]


def p_proj(ctx, layer):
    nc, a = ctx.nc, ctx.a
    ar = a["arena"]
    wo = ctx.wo_bf[layer]
    av = ctx.attn_ag_out.ap().rearrange("(t p) n -> p t n", p=128)

    for tb in range(2):
        sc0 = ctx.cnt["s_sc"]
        pe0 = ctx.cnt["s_pe"]
        dve0 = ctx.cnt["s_dve"]
        wv = {}
        with nc.Block() as blk:
            @blk.scalar
            def _(sc):
                sc.wait_ge(ctx.sems["s_cc_at"], ctx.attn_cc_done)
                pid = sc.partition_id()
                dst = ar[:, RHS:RHS + 8192].rearrange("p (t n) -> p t n", n=512)
                ctx.dinc(sc.dma_start(
                    out=dst, in_=av[:, :, bass.ds(pid * 1024 + tb * 512, 512)]), "s_sc")

            @blk.sync
            def _(s):
                s.wait_ge(ctx.sems[ctx.wready["wo%d" % layer][0]], ctx.wready["wo%d" % layer][1])
                for dgrp in range(4):
                    for din in range(16):
                        i = dgrp * 16 + din
                        slot = i % 8
                        if i >= 8:
                            s.wait_ge(ctx.sems["s_pe"], pe0 + (i - 8) + 1)
                        wv[i] = ctx.slot_dma(s.dma_start(
                            out=a["wpool"][:, slot * 1024:slot * 1024 + 512],
                            in_=wo[din * 128:(din + 1) * 128, dgrp * 512:(dgrp + 1) * 512]), slot)

            @blk.tensor
            def _(t):
                t.wait_ge(ctx.sems["s_sc"], sc0 + 16)
                for dgrp in range(4):
                    if dgrp > 0:
                        t.wait_ge(ctx.sems["s_dve"], dve0 + dgrp * 4)
                    for din in range(16):
                        i = dgrp * 16 + din
                        slot = i % 8
                        ctx.slot_wait(t, slot, wv[i])
                        rhs = ar[:, RHS + din * 512: RHS + (din + 1) * 512]
                        for do in range(4):
                            ii = t.matmul(a["ps_a"][:, do * 512:(do + 1) * 512],
                                          a["wpool"][:, slot * 1024 + do * 128: slot * 1024 + (do + 1) * 128],
                                          rhs, start=(din == 0), stop=(din == 15))
                        ctx.cinc(ii, "s_pe")

            @blk.vector
            def _(v):
                for dgrp in range(4):
                    v.wait_ge(ctx.sems["s_pe"], pe0 + dgrp * 16 + 16)
                    for do in range(4):
                        dt_ = dgrp * 4 + do
                        xt = a["x_fm"][:, dt_ * TOK + tb * 512: dt_ * TOK + (tb + 1) * 512]
                        ctx.cinc(v.tensor_tensor(xt, xt, a["ps_a"][:, do * 512:(do + 1) * 512],
                                                 ALU.add), "s_dve")


def p_ffn1a(ctx):
    nc, a = ctx.nc, ctx.a
    w1 = ctx.w1t_bf.ap().rearrange("(dt p) h -> p dt h", p=128)
    w3 = ctx.w3t_bf.ap().rearrange("(dt p) h -> p dt h", p=128)
    hv = ctx.hid_dram.ap().rearrange("(t p) n -> t p n", p=128)

    pe0 = ctx.cnt["s_pe"]
    act0 = ctx.cnt["s_act"]
    dve0 = ctx.cnt["s_dve"]
    b4, b5 = ctx.cnt["q4"], ctx.cnt["q5"]
    sv = {i: (b4 if i % 2 == 0 else b5) + (i // 2 + 1) * 16 for i in range(64)}
    wv = {}
    with nc.Block() as blk:
        @blk.sync
        def _(s):
            s.wait_ge(ctx.sems[ctx.wready["w1t"][0]], ctx.wready["w1t"][1])
            s.wait_ge(ctx.sems[ctx.wready["w3t"][0]], ctx.wready["w3t"][1])
            for i in range(64):
                for dh in range(2):
                    step = i * 2 + dh
                    slot = step % 4
                    if step >= 4:
                        s.wait_ge(ctx.sems["s_pe"], pe0 + step - 3)
                    for mi, qm in enumerate((w1, w3)):
                        dst = a["wpool"][:, slot * 2048 + mi * 1024:
                                         slot * 2048 + (mi + 1) * 1024]
                        wv[step] = ctx.slot_dma(s.dma_start(
                            out=dst.rearrange("p (x b) -> p x b", b=128),
                            in_=qm[:, dh * 8:(dh + 1) * 8, i * 128:(i + 1) * 128]), slot)

        @blk.scalar
        def _(sc):
            for i in range(64):
                psx1 = a["ps_a"].ap()[:, 0:1024] if i % 2 == 0 else a["ps_b"].ap()[:, :]
                sc.wait_ge(ctx.sems["s_pe"], pe0 + i * 2 + 2)
                if i >= 2:
                    sc.wait_ge(ctx.sems["s_dve"], dve0 + i - 1)
                ctx.cinc(sc.activation(a["scr"][:, (i % 2) * 512:(i % 2) * 512 + 512],
                                       psx1[:, 0:512], AF.Silu), "s_act")
                ctx.cinc(sc.activation(a["vtmp"][:, (i % 2) * 512:(i % 2) * 512 + 512],
                                       psx1[:, 512:1024], AF.Silu), "s_act")

        @blk.tensor
        def _(t):
            for i in range(64):
                psx1 = a["ps_a"].ap()[:, 0:1024] if i % 2 == 0 else a["ps_b"].ap()[:, :]
                psx2 = a["ps_a"].ap()[:, 1024:2048] if i % 2 == 0 else a["ps_c"].ap()[:, :]
                if i >= 2:
                    t.wait_ge(ctx.sems["s_dve"], dve0 + i - 1)
                for dh in range(2):
                    step = i * 2 + dh
                    slot = step % 4
                    ctx.slot_wait(t, slot, wv[step])
                    for mi in range(2):
                        psd = psx1 if mi == 0 else psx2
                        for d8 in range(8):
                            d = dh * 8 + d8
                            w = a["wpool"][:, slot * 2048 + mi * 1024 + d8 * 128:
                                           slot * 2048 + mi * 1024 + (d8 + 1) * 128]
                            for th in range(2):
                                ii = t.matmul(psd[:, th * 512:(th + 1) * 512], w,
                                              a["xn"][:, d * TOK + th * 512: d * TOK + (th + 1) * 512],
                                              start=(d == 0), stop=(d == 15))
                    ctx.cinc(ii, "s_pe")

        @blk.vector
        def _(v):
            for i in range(64):
                psx2 = a["ps_a"].ap()[:, 1024:2048] if i % 2 == 0 else a["ps_c"].ap()[:, :]
                v.wait_ge(ctx.sems["s_act"], act0 + i * 2 + 2)
                if i >= 2:
                    ctx.slot_wait(v, 4 + i % 2, sv[i - 2])
                v.tensor_tensor(a["hidout"][:, (i % 2) * 1024:(i % 2) * 1024 + 512],
                                a["scr"][:, (i % 2) * 512:(i % 2) * 512 + 512],
                                psx2[:, 0:512], ALU.mult)
                ctx.cinc(v.tensor_tensor(
                    a["hidout"][:, (i % 2) * 1024 + 512:(i % 2) * 1024 + 1024],
                    a["vtmp"][:, (i % 2) * 512:(i % 2) * 512 + 512],
                    psx2[:, 512:1024], ALU.mult), "s_dve")

        @blk.gpsimd
        def _(g):
            for i in range(64):
                g.wait_ge(ctx.sems["s_dve"], dve0 + i + 1)
                w = ctx.slot_dma(g.dma_start(
                    out=hv[i, :, :],
                    in_=a["hidout"][:, (i % 2) * 1024:(i % 2) * 1024 + 1024]), 4 + i % 2)
                assert w == sv[i], (w, sv[i])


def p_fwht_b(ctx):
    nc, a = ctx.nc, ctx.a
    ar = a["arena"]
    hv = ctx.hid_dram.ap().rearrange("(t p) n -> t p n", p=128)
    bv = ctx.hidB_dram.ap().rearrange("(t p) n -> t p n", p=128)
    h64 = a["consts"][0:64, 256:320]
    pe0 = ctx.cnt["s_pe"]
    dve0 = ctx.cnt["s_dve"]
    hw4, hw5 = ctx.cnt["q4"], ctx.cnt["q5"]
    b2, b3 = ctx.cnt["q2"], ctx.cnt["q3"]
    sv = {c: (b2 if c % 2 == 0 else b3) + (c // 2 + 1) * 16 for c in range(16)}
    wv = {}
    with nc.Block() as blk:
        @blk.sync
        def _(s):
            s.wait_ge(ctx.sems["q4"], hw4)
            s.wait_ge(ctx.sems["q5"], hw5)
            for c in range(16):
                slot = c % 2
                if c >= 2:
                    s.wait_ge(ctx.sems["s_pe"], pe0 + (c - 1) * 16)
                wv[c] = ctx.slot_dma(s.dma_start(
                    out=ar[0:64, slot * 8192:(slot + 1) * 8192].rearrange(
                        "t (p n) -> t p n", n=1024),
                    in_=hv[:, c * 8:(c + 1) * 8, :]), slot)

        @blk.tensor
        def _(t):
            for c in range(16):
                slot = c % 2
                ctx.slot_wait(t, slot, wv[c])
                for j in range(16):
                    psd = a["ps_b"].ap() if j % 2 == 0 else a["ps_c"].ap()
                    if c * 16 + j >= 2:
                        t.wait_ge(ctx.sems["s_dve"], dve0 + c * 16 + j - 1)
                    ctx.cinc(t.matmul(psd[0:64, 0:512], h64,
                                      ar[0:64, slot * 8192 + j * 512: slot * 8192 + (j + 1) * 512],
                                      start=True, stop=True), "s_pe")

        @blk.vector
        def _(v):
            for c in range(16):
                oslot = c % 2
                if c >= 2:
                    ctx.slot_wait(v, 2 + oslot, sv[c - 2])
                for j in range(16):
                    psd = a["ps_b"].ap() if j % 2 == 0 else a["ps_c"].ap()
                    v.wait_ge(ctx.sems["s_pe"], pe0 + c * 16 + j + 1)
                    ctx.cinc(v.tensor_copy(
                        ar[0:64, 16384 + oslot * 8192 + j * 512: 16384 + oslot * 8192 + (j + 1) * 512],
                        psd[0:64, 0:512]), "s_dve")

        @blk.scalar
        def _(sc):
            for c in range(16):
                oslot = c % 2
                sc.wait_ge(ctx.sems["s_dve"], dve0 + c * 16 + 16)
                w = ctx.slot_dma(sc.dma_start(
                    out=bv[:, c * 8:(c + 1) * 8, :],
                    in_=ar[0:64, 16384 + oslot * 8192: 16384 + (oslot + 1) * 8192].rearrange(
                        "t (p n) -> t p n", n=1024)), 2 + oslot)
                assert w == sv[c], (w, sv[c])


def p_w2a(ctx):
    nc, a = ctx.nc, ctx.a
    ar = a["arena"]
    bv = ctx.hidB_dram.ap().rearrange("(t p) n -> t p n", p=128)
    w2 = ctx.w2t_bf
    h128 = a["consts"][:, 128:256]
    for tb in range(2):
        pe0 = ctx.cnt["s_pe"]
        dve0 = ctx.cnt["s_dve"]
        wv, wv2 = {}, {}
        with nc.Block() as blk:
            @blk.scalar
            def _(sc):
                if tb == 0:
                    sc.wait_ge(ctx.sems[ctx.wready["w2t"][0]], ctx.wready["w2t"][1])
                for T in range(64):
                    slot = T % 2
                    if T >= 2:
                        sc.wait_ge(ctx.sems["s_pe"], pe0 + T - 1)
                    wv[T] = ctx.slot_dma(sc.dma_start(
                        out=a["xpool"][:, slot * 1024: slot * 1024 + 512],
                        in_=bv[T, :, :][:, tb * 512:(tb + 1) * 512]), slot)

            @blk.tensor
            def _(t):
                for T in range(64):
                    slot = T % 2
                    psd = a["ps_b"].ap() if T % 2 == 0 else a["ps_c"].ap()
                    ctx.slot_wait(t, slot, wv[T])
                    if T >= 2:
                        t.wait_ge(ctx.sems["s_dve"], dve0 + T - 1)
                    ctx.cinc(t.matmul(psd[:, 0:512], h128,
                                      a["xpool"][:, slot * 1024: slot * 1024 + 512],
                                      start=True, stop=True), "s_pe")

            @blk.vector
            def _(v):
                v.wait_ge(ctx.sems["q2"], ctx.cnt["q2"])
                v.wait_ge(ctx.sems["q3"], ctx.cnt["q3"])
                for T in range(64):
                    psd = a["ps_b"].ap() if T % 2 == 0 else a["ps_c"].ap()
                    v.wait_ge(ctx.sems["s_pe"], pe0 + T + 1)
                    ctx.cinc(v.tensor_scalar_mul(ar[:, T * 512:(T + 1) * 512],
                                                 psd[:, 0:512], FW_SC), "s_dve")

        pe1 = ctx.cnt["s_pe"]
        dve1 = ctx.cnt["s_dve"]
        with nc.Block() as blk:
            @blk.sync
            def _(s):
                for dgrp in range(4):
                    for T in range(64):
                        step = dgrp * 64 + T
                        slot = step % 8
                        if step >= 8:
                            s.wait_ge(ctx.sems["s_pe"], pe1 + step - 7)
                        wv2[step] = ctx.slot_dma(s.dma_start(
                            out=a["wpool"][:, slot * 1024:slot * 1024 + 512],
                            in_=w2[T * 128:(T + 1) * 128, dgrp * 512:(dgrp + 1) * 512]), slot)

            @blk.tensor
            def _(t):
                for dgrp in range(4):
                    if dgrp > 0:
                        t.wait_ge(ctx.sems["s_dve"], dve1 + dgrp * 4)
                    for T in range(64):
                        step = dgrp * 64 + T
                        slot = step % 8
                        ctx.slot_wait(t, slot, wv2[step])
                        for do in range(4):
                            ii = t.matmul(a["ps_a"][:, do * 512:(do + 1) * 512],
                                          a["wpool"][:, slot * 1024 + do * 128:
                                                      slot * 1024 + (do + 1) * 128],
                                          ar[:, T * 512:(T + 1) * 512],
                                          start=(T == 0), stop=(T == 63))
                        ctx.cinc(ii, "s_pe")

            @blk.vector
            def _(v):
                for dgrp in range(4):
                    v.wait_ge(ctx.sems["s_pe"], pe1 + dgrp * 64 + 64)
                    for do in range(4):
                        dt_ = dgrp * 4 + do
                        xt = a["x_fm"][:, dt_ * TOK + tb * 512: dt_ * TOK + (tb + 1) * 512]
                        ctx.cinc(v.tensor_tensor(xt, xt, a["ps_a"][:, do * 512:(do + 1) * 512],
                                                 ALU.add), "s_dve")


def p_store(ctx):
    nc, a = ctx.nc, ctx.a
    arF = a["arena"].ap().bitcast(F32)
    pe0 = ctx.cnt["s_pe"]
    dve0 = ctx.cnt["s_dve"]
    b0, b1 = ctx.cnt["q0"], ctx.cnt["q1"]
    sv = {tt: (b0 if tt % 2 == 0 else b1) + (tt // 2 + 1) * 16 for tt in range(8)}
    with nc.Block() as blk:
        @blk.tensor
        def _(t):
            for tt in range(8):
                for d in range(16):
                    j = tt * 16 + d
                    dst = a["ps_b"][:, (d % 2) * 512:(d % 2) * 512 + 128]
                    if j >= 2:
                        t.wait_ge(ctx.sems["s_dve"], dve0 + j - 1)
                    ctx.cinc(t.transpose(
                        dst, a["x_fm"][:, d * TOK + tt * 128: d * TOK + (tt + 1) * 128],
                        a["identf"][:, :]), "s_pe")

        @blk.vector
        def _(v):
            for tt in range(8):
                oslot = tt % 2
                if tt >= 2:
                    ctx.slot_wait(v, oslot, sv[tt - 2])
                for d in range(16):
                    j = tt * 16 + d
                    src = a["ps_b"][:, (d % 2) * 512:(d % 2) * 512 + 128]
                    v.wait_ge(ctx.sems["s_pe"], pe0 + j + 1)
                    ctx.cinc(v.tensor_copy(
                        arF[:, oslot * 2048 + d * 128: oslot * 2048 + (d + 1) * 128], src),
                        "s_dve")

        @blk.sync
        def _(s):
            for tt in range(8):
                oslot = tt % 2
                s.wait_ge(ctx.sems["s_dve"], dve0 + tt * 16 + 16)
                w = ctx.slot_dma(s.dma_start(
                    out=ctx.out_own[tt * 128:(tt + 1) * 128, :],
                    in_=arF[:, oslot * 2048:(oslot + 1) * 2048]), oslot)
                assert w == sv[tt], (w, sv[tt])
            s.wait_ge(ctx.sems["q0"], sv[6])
            s.wait_ge(ctx.sems["q1"], sv[7])


def build(cfg):
    nc = bass.Bass("TRN2", target_bir_lowering=False, debug=False)
    ctx = Ctx(nc, cfg)

    ctx.x_in = nc.dram_tensor("x_own", [TOK, D], F32, kind="ExternalInput")

    ctx.identf_dram = nc.inline_tensor(np.eye(128, dtype=np.float32), "identf_c")
    consts = np.zeros((128, 768), dtype=ml_dtypes.bfloat16)
    consts[:, 0:128] = np.eye(128, dtype=ml_dtypes.bfloat16)
    consts[:, 128:256] = _hadamard(128).astype(ml_dtypes.bfloat16)
    consts[0:64, 256:320] = _hadamard(64).astype(ml_dtypes.bfloat16)
    consts[:, 320:448] = 1.0
    ctx.consts_dram = nc.inline_tensor(consts, "consts_c")
    ctx.masks_dram = nc.inline_tensor(_masks_np(), "masks_c")
    fconst = np.zeros((128, 136), dtype=np.float32)
    fconst[:, 0:128] = 1.0
    fconst[:, 128] = LN_EPS
    fconst[:, 129] = Q_EPS / 2.0
    ctx.fconst_dram = nc.inline_tensor(fconst, "fconst_c")

    ctx.xn_ag_in = nc.dram_tensor("xn_ag_in", [D, TOK], F16, kind="Internal")
    ctx.xn_ag_out = nc.dram_tensor("xn_ag_out", [8 * D, TOK], F16, kind="Internal")
    ctx.attn_ag_in = nc.dram_tensor("attn_ag_in", [256, 8192], F16, kind="Internal")
    ctx.attn_ag_out = nc.dram_tensor("attn_ag_out", [D, 8192], F16, kind="Internal")
    ctx.kvspill = nc.dram_tensor("kvspill", [768, 4096], F16, kind="Internal")

    ctx.wq_in = [nc.dram_tensor("l%d_qkv_sl" % l, [D, 768], F32, kind="ExternalInput")
                 for l in range(2)]
    ctx.wo_in = [nc.dram_tensor("l%d_wo" % l, [D, D], F32, kind="ExternalInput")
                 for l in range(2)]
    ctx.wq_bf = [nc.dram_tensor("l%d_qkv_bf" % l, [D, 768], F16, kind="Internal")
                 for l in range(2)]
    ctx.wo_bf = [nc.dram_tensor("l%d_wo_bf" % l, [D, D], F16, kind="Internal")
                 for l in range(2)]
    ctx.w1t_in = nc.dram_tensor("l1_w1t", [D, H1], F32, kind="ExternalInput")
    ctx.w3t_in = nc.dram_tensor("l1_w3t", [D, H1], F32, kind="ExternalInput")
    ctx.w2t_in = nc.dram_tensor("l1_w2t", [H1, D], F32, kind="ExternalInput")
    ctx.w1t_bf = nc.dram_tensor("w1t_bf", [D, H1], F16, kind="Internal")
    ctx.w3t_bf = nc.dram_tensor("w3t_bf", [D, H1], F16, kind="Internal")
    ctx.w2t_bf = nc.dram_tensor("w2t_bf", [H1, D], F16, kind="Internal")
    ctx.hid_dram = nc.dram_tensor("hid_dram", [H1, TOK], F16, kind="Internal")
    ctx.hidB_dram = nc.dram_tensor("hidB_dram", [H1, TOK], F16, kind="Internal")
    ctx.out_own = nc.dram_tensor("out_own", [TOK, D], F32, kind="ExternalOutput")
    ctx.cast_list = [
        ("wq0", ctx.wq_in[0], ctx.wq_bf[0]),
        ("wo0", ctx.wo_in[0], ctx.wo_bf[0]),
        ("w1t", ctx.w1t_in, ctx.w1t_bf),
        ("w3t", ctx.w3t_in, ctx.w3t_bf),
        ("wq1", ctx.wq_in[1], ctx.wq_bf[1]),
        ("wo1", ctx.wo_in[1], ctx.wo_bf[1]),
        ("w2t", ctx.w2t_in, ctx.w2t_bf),
    ]
    ctx.ffn0_sl = nc.dram_tensor("l0_ffn_sl", [128, 3 * QSLF], F32, kind="ExternalInput")
    ctx.q_in = [nc.dram_tensor("q_in%d" % m, [128, QSLF], F16, kind="Internal")
                for m in range(3)]
    ctx.q_out = [nc.dram_tensor("q_out%d" % m, [1024, QSLF], F16, kind="Internal")
                 for m in range(3)]
    ctx.gm_in = nc.dram_tensor("gm_in", [1, 3], F32, kind="Internal")
    ctx.gm_out = nc.dram_tensor("gm_out", [8, 3], F32, kind="Internal")

    if cfg.get("dump_xn"):
        ctx.out_xn = nc.dram_tensor("out_xn", [D, TOK], F32, kind="ExternalOutput")
        ctx.out_xng = nc.dram_tensor("out_xng", [8 * D, TOK], F32, kind="ExternalOutput")
    if cfg.get("dump_attn"):
        ctx.out_attng = nc.dram_tensor("out_attng", [D, 8192], F32, kind="ExternalOutput")
    if cfg.get("dump_x"):
        ctx.out_x = nc.dram_tensor("out_x", [D, TOK], F32, kind="ExternalOutput")
    if cfg.get("dump_qkv"):
        ctx.out_qkv = nc.dram_tensor("out_qkv", [384, 4096], F16, kind="ExternalOutput")
    if cfg.get("dump_p"):
        ctx.out_p = nc.dram_tensor("out_p", [512, 2048], F16, kind="ExternalOutput")
    if cfg.get("dump_qw"):
        ctx.out_qw = nc.dram_tensor("out_qw", [1024, QSLF], F16, kind="ExternalOutput")
        ctx.out_hid = nc.dram_tensor("out_hid", [128, 22528], F16, kind="ExternalOutput")

    ctx.a = _alloc(ctx)

    upto = cfg.get("upto", 99)
    import contextlib
    def NS(name):
        return nc.named_scope(name)
    with NS("consts"):
        p_consts(ctx)
    with NS("front"):
        with nc.Block() as front_blk:
            p_load_x(ctx, ctx.x_in, front_blk)
            p_quant1(ctx, front_blk)
    with NS("ln0"):
        p_layernorm(ctx)
    with NS("xnag0"):
        p_xn_ag(ctx)
    if upto == 66:
        with nc.Block() as blk:
            @blk.gpsimd
            def _(g):
                ctx.cinc(g.collective_compute(
                    "AllGather", ALU.bypass, replica_groups=[list(range(8))],
                    ins=[ctx.attn_ag_in.ap().opt()], outs=[ctx.attn_ag_out.ap().opt()]),
                    "s_cc_at")
                g.wait_ge(ctx.sems["s_cc_at"], 1)
        ctx.stack.close()
        return nc, ctx
    if upto >= 2:
        with NS("quant2"):
            p_quant2(ctx)
    if upto == 67:
        with nc.Block() as blk:
            @blk.gpsimd
            def _(g):
                g.wait_ge(ctx.sems["s_cc_q"], ctx.q_cc_done)
                ctx.cinc(g.collective_compute(
                    "AllGather", ALU.bypass, replica_groups=[list(range(8))],
                    ins=[ctx.attn_ag_in.ap().opt()], outs=[ctx.attn_ag_out.ap().opt()]),
                    "s_cc_at")
                g.wait_ge(ctx.sems["s_cc_at"], 1)
        ctx.stack.close()
        return nc, ctx
    if cfg.get("upto") == "ln":
        pass
    else:
        if upto < 3:
            pass
        def _ccprobe():
            with nc.Block() as blk:
                @blk.gpsimd
                def _(g):
                    ctx.cinc(g.collective_compute(
                        "AllGather", ALU.bypass, replica_groups=[list(range(8))],
                        ins=[ctx.attn_ag_in.ap().opt()], outs=[ctx.attn_ag_out.ap().opt()]),
                        "s_cc_at")
                    g.wait_ge(ctx.sems["s_cc_at"], ctx.cnt["s_cc_at"])
        blist = [] if upto < 3 else ([0] if upto in (3, 68) else [0, 1])
        for b in blist:
            with NS("qkv0_b%d" % b):
                p_qkv(ctx, 0, b)
            if upto == 68:
                _ccprobe()
                ctx.stack.close()
                return nc, ctx
            if upto == 3:
                continue

            if b == 0 and cfg.get("dump_qkv"):
                with nc.Block() as blk:
                    @blk.sync
                    def _(s):
                        ctx.dinc(s.dma_start(out=ctx.out_qkv[0:128, :], in_=ctx.a["arena"][:, QO:QO + 4096]), "s_st")
                        ctx.dinc(s.dma_start(out=ctx.out_qkv[128:256, :], in_=ctx.a["arena"][:, KO:KO + 4096]), "s_st")
                        ctx.dinc(s.dma_start(out=ctx.out_qkv[256:384, :], in_=ctx.a["arena"][:, VO:VO + 4096]), "s_st")
                        s.wait_ge(ctx.sems["s_st"], ctx.cnt["s_st"])
            if b == 0:
                with nc.Block() as blk:
                    @blk.gpsimd
                    def _(g):
                        p_wcast_issue(ctx, ("wo0", "w1t", "w3t", "wq1", "wo1", "w2t"), g)
            with NS("attn0_b%d" % b):
                p_attn_units(ctx, 0, b)
        if upto == 70:
            _ccprobe()
            ctx.stack.close()
            return nc, ctx
        if upto < 5:
            ctx.stack.close()
            return nc, ctx
        p_attn_ag(ctx)
        if upto == 71:
            with nc.Block() as blk:
                @blk.gpsimd
                def _(g):
                    g.wait_ge(ctx.sems["s_cc_at"], ctx.attn_cc_done)
            ctx.stack.close()
            return nc, ctx
        if upto < 6:
            ctx.stack.close()
            return nc, ctx
        with NS("ffn0_h0"):
            p_ffn0(ctx, 0)
        if cfg.get("dump_qw"):
            with nc.Block() as blk:
                @blk.sync
                def _(s):
                    ctx.dinc(s.dma_start(out=ctx.out_qw[:, :], in_=ctx.q_out[0][:, :]), "s_st")
                    ctx.dinc(s.dma_start(out=ctx.out_hid[:, :],
                                         in_=ctx.a["arena"][:, 0:22528]), "s_st")
                    s.wait_ge(ctx.sems["s_st"], ctx.cnt["s_st"])
        if upto == 65:
            with nc.Block() as blk:
                @blk.gpsimd
                def _(g):
                    g.wait_ge(ctx.sems["s_cc_at"], ctx.attn_cc_done)
        if upto >= 7:
            with NS("proj0"):
                p_proj(ctx, 0)
        if upto >= 8:
            with NS("ffn0_h1"):
                p_ffn0(ctx, 1)
        if upto >= 9:
            with NS("ln1"):
                p_layernorm(ctx)
            with NS("xnag1"):
                p_xn_ag(ctx)
            with NS("ffn1a"):
                p_ffn1a(ctx)
            for b in range(2):
                with NS("qkv1_b%d" % b):
                    p_qkv(ctx, 1, b)
                with NS("attn1_b%d" % b):
                    p_attn_units(ctx, 1, b)
            with NS("attnag1"):
                p_attn_ag(ctx)
            with NS("fwhtb"):
                p_fwht_b(ctx)
            with NS("proj1"):
                p_proj(ctx, 1)
            with NS("w2a"):
                p_w2a(ctx)
            with NS("store"):
                p_store(ctx)

    if cfg.get("dump_xn"):
        with nc.Block() as blk:
            @blk.gpsimd
            def _(g):
                g.wait_ge(ctx.sems["s_cc"], ctx.xn_cc_done)
                ctx.dinc(g.dma_start(out=ctx.out_xng[:, :], in_=ctx.xn_ag_out[:, :]), "s_st")
                for d in range(16):
                    ctx.dinc(g.dma_start(out=ctx.out_xn[d * 128:(d + 1) * 128, :],
                                         in_=ctx.a["xn"][:, d * TOK:(d + 1) * TOK]), "s_st")
                g.wait_ge(ctx.sems["s_st"], ctx.cnt["s_st"])
    if cfg.get("dump_attn"):
        with nc.Block() as blk:
            @blk.gpsimd
            def _(g):
                g.wait_ge(ctx.sems["s_cc"], ctx.attn_cc_done)
                ctx.dinc(g.dma_start(out=ctx.out_attng[:, :], in_=ctx.attn_ag_out[:, :]), "s_st")
                g.wait_ge(ctx.sems["s_st"], ctx.cnt["s_st"])
    if cfg.get("dump_x"):
        with nc.Block() as blk:
            @blk.sync
            def _(s):
                for d in range(16):
                    ctx.dinc(s.dma_start(out=ctx.out_x[d * 128:(d + 1) * 128, :],
                                         in_=ctx.a["x_fm"][:, d * TOK:(d + 1) * TOK]), "s_st")
                s.wait_ge(ctx.sems["s_st"], ctx.cnt["s_st"])

    ctx.stack.close()
    return nc, ctx


def make_in_maps(inputs):
    x = np.asarray(inputs["L0_attn_x"])
    maps = []
    for c in range(NCORE):
        b, k = c // 4, c % 4
        blocks = [k, 7 - k]
        wqkv = np.asarray(inputs["L0_attn_w_qkv"])
        l0sl = np.concatenate([wqkv[256 * c:256 * (c + 1)],
                               wqkv[2048 + 256 * c:2048 + 256 * (c + 1)],
                               wqkv[4096 + 256 * c:4096 + 256 * (c + 1)]], axis=0)
        l1sl = np.concatenate([np.asarray(inputs["L1_attn_Wq"])[256 * c:256 * (c + 1)],
                               np.asarray(inputs["L1_attn_Wk"])[256 * c:256 * (c + 1)],
                               np.asarray(inputs["L1_attn_Wv"])[256 * c:256 * (c + 1)]], axis=0)
        QSL = H0 * D // NCORE
        fsl = [np.asarray(inputs["L0_ffn_w%d" % (m + 1)]).T.reshape(-1)[c * QSL:(c + 1) * QSL]
               .reshape(128, QSLF) for m in range(3)]
        m = {
            "x_own": np.ascontiguousarray(
                np.concatenate([x[b, 512 * g:512 * (g + 1)] for g in blocks], axis=0)),
            "l0_qkv_sl": np.ascontiguousarray(l0sl.T),
            "l1_qkv_sl": np.ascontiguousarray(l1sl.T),
            "l0_wo": np.ascontiguousarray(np.asarray(inputs["L0_attn_w_out"]).T),
            "l1_wo": np.ascontiguousarray(np.asarray(inputs["L1_attn_Wo"]).T),
            "l0_ffn_sl": np.ascontiguousarray(np.concatenate(fsl, axis=1)),
            "l1_w1t": np.ascontiguousarray(np.asarray(inputs["L1_ffn_W1"]).T),
            "l1_w3t": np.ascontiguousarray(np.asarray(inputs["L1_ffn_W3"]).T),
            "l1_w2t": np.ascontiguousarray(np.asarray(inputs["L1_ffn_W2"]).T),
        }
        maps.append(m)
    return maps


def kernel(**inputs):
    nc, ctx = build({})
    maps = make_in_maps(inputs)
    res = run_bass_kernel_spmd(nc, maps, core_ids=list(range(NCORE)))
    out = np.empty((B, S, D), dtype=np.float32)
    for c in range(NCORE):
        b, k = c // 4, c % 4
        oc = res.results[c]["out_own"]
        for s_i, g in enumerate([k, 7 - k]):
            out[b, 512 * g:512 * (g + 1)] = oc[512 * s_i:512 * (s_i + 1)]
    return out
